# version 16
# baseline (speedup 1.0000x reference)
import contextlib
import numpy as np
import concourse.bass as bass
import concourse.mybir as mybir
from concourse.bass_utils import run_bass_kernel_spmd
from concourse.alu_op_type import AluOpType as Op

F32 = mybir.dt.float32
BF16 = mybir.dt.bfloat16
AF = mybir.ActivationFunctionType
AX = mybir.AxisListType

MAGIC = float(np.float32(3 * 2**22))
EPS_LN = 1e-5
EPS_Q = 1e-5

D = 1024
NI = 4096
NST = 16
TT = 512
NDG = D // 128
NTJ = TT // 128
NKI = NI // 128
ICW = 512
NIC = NI // ICW
NHALF = 2
STP = NTJ // NHALF
NGI = NHALF * NIC
NPAIR = NGI * STP
PPN = NPAIR // NST
NQ = NI // 512


class Ctx:

    def __init__(self):
        self.lab = {}

    def start(self, pass2, sems=None):
        self.pass2 = pass2
        self.cnt = {}
        self.sems = sems
        self.last = {}

    def inc(self, instr, sem, by=1, labels=(), tag=None):
        c = self.cnt.get(sem, 0) + by
        self.cnt[sem] = c
        if not self.pass2:
            for lb in labels:
                self.lab[tuple(lb)] = (sem, c)
        if self.pass2 and instr is not None:
            instr.then_inc(self.sems[sem], by)
        if tag is not None:
            self.last[tag] = (sem, c)
        return c

    def mark(self, _sem_unused, *label):
        raise RuntimeError("mark replaced by inc labels")

    def wait(self, eng, _sem_unused, *label):
        if not self.pass2:
            return
        ent = self.lab.get(tuple(label))
        assert ent is not None, f"missing label {tuple(label)}"
        sem, v = ent
        if v > 0:
            eng.wait_ge(self.sems[sem], v)

    def wait_if(self, eng, _sem_unused, *label):
        if tuple(label) in self.lab:
            self.wait(eng, None, *label)


def build_program(T):
    NCH = T // TT
    SEG = NCH + 2
    nc = bass.Bass("TRN2", target_bir_lowering=False)

    def din(name, shape, dt=F32):
        return nc.dram_tensor(name, shape, dt, kind="ExternalInput").ap()

    x_h = din("x", [T, D])
    wgT_h = din("wgT", [NDG, 128, NI])
    wuT_h = din("wuT", [NDG, 128, NI])
    wdT_h = din("wdT", [NKI, 128, D])
    woT_h = din("woT", [NDG, 128, D])
    wxpT_h = din("wxpT", [128, NDG * 32])
    alog_h = din("alog", [128, NDG * NST])
    wdt_h = din("wdtr", [1, D])
    bdt_h = din("bdtr", [1, D])
    bxp_h = din("bxpr", [1, 32])
    bout_h = din("boutr", [1, D])
    oh_h = din("oh32", [32, 32 * 128], BF16)
    ones_h = din("ones1", [1, TT], BF16)
    o128_h = din("o128", [128, 128])
    out_h = nc.dram_tensor("out", [T, D], F32, kind="ExternalOutput").ap()

    wgq_s = nc.dram_tensor("wgq_s", [NIC, NDG, 128, ICW], BF16, kind="Internal").ap()
    wuq_s = nc.dram_tensor("wuq_s", [NIC, NDG, 128, ICW], BF16, kind="Internal").ap()
    wdq_s = nc.dram_tensor("wdq_s", [NKI, 128, D], BF16, kind="Internal").ap()
    woq_s = nc.dram_tensor("woq_s", [NDG, 128, D], BF16, kind="Internal").ap()

    ctx = Ctx()
    es = contextlib.ExitStack()
    with es:
        def SB(name, p, f, dt):
            return es.enter_context(nc.sbuf_tensor(name, [p, f], dt))

        def PS(name, p, f):
            return es.enter_context(nc.psum_tensor(name, [p, f], F32))

        woT_r = SB("woT_r", 128, 4 * D, BF16)
        wxpT_b = SB("wxpT_b", 128, NDG * 32, BF16)
        wdt_b = SB("wdt_b", 1, D, BF16)
        bdt_b = SB("bdt_b", 1, D, BF16)
        bxp_b = SB("bxp_b", 1, 32, BF16)
        bout_b = SB("bout_b", 1, D, BF16)
        oh32 = SB("oh32_t", 32, 32 * 128, BF16)
        ones1 = SB("ones1_t", 1, TT, BF16)
        o128 = SB("o128_t", 128, 128, F32)
        negA = SB("negA", 128, NDG * NST, F32)
        ccol = SB("ccol", 128, 4, F32)
        mwc = SB("mwc", 128, 8, F32)
        wsum = SB("wsum", 128, 4, F32)

        xl_r = SB("xl_r", 128, 2 * 512, F32)
        xr_r = SB("xr_r", 128, 2 * 512, F32)
        r2 = SB("r2", 128, NTJ * D, BF16)
        tok = SB("tok", 128, NTJ * D, BF16)
        xlnT = SB("xlnT", 128, NDG * TT, BF16)
        xqT = SB("xqT", 128, NDG * TT, BF16)
        dt_all = SB("dt_all", 128, NDG * TT, BF16)
        u_all = SB("u_all", 128, NDG * TT, BF16)
        bct = SB("bct", 32, TT, BF16)
        bcB = SB("bcB", 128, 2 * TT, BF16)
        bcC = SB("bcC", 128, 2 * TT, BF16)
        dA_r = SB("dA_r", 128, 2 * TT, F32)
        dB_all = SB("dB_all", 128, NDG * TT, BF16)
        yp_all = SB("yp_all", 128, NDG * TT, BF16)
        y_acc = SB("y_acc", 128, NDG * TT, BF16)
        yT_b = SB("yT_b", 128, NDG * TT, BF16)
        carr = SB("carr", 128, 2 * 128, BF16)
        sp1 = SB("sp1", 128, TT, F32)
        sp2 = SB("sp2", 128, TT, BF16)
        lncol = SB("lncol", 128, 64, F32)
        bns = SB("bns", 128, 12, F32)
        t1q = SB("t1q", 128, 512, F32)
        r_r = SB("r_r", 128, 2 * ICW, F32)
        h_half = SB("h_half", 128, STP * NI, BF16)
        maxp = SB("maxp", 128, STP * NIC, F32)
        qcol = SB("qcol", 128, 16, F32)
        hq_b = SB("hq_b", 128, NI, BF16)
        hqT = SB("hqT", 128, NKI * STP * 128, BF16)
        wg_r = SB("wg_r", 128, 2 * NDG * ICW, BF16)
        wu_r = SB("wu_r", 128, 2 * NDG * ICW, BF16)
        wd_r = SB("wd_r", 128, 2 * D, BF16)

        gbufA = h_half[:].bitcast(F32)
        gbufB = hqT[:].bitcast(F32)
        qout = y_acc[:]

        pg = PS("pg", 128, ICW)
        pu = PS("pu", 128, ICW)
        pd = [PS(f"pd{i}", 128, 512) for i in range(4)]
        pr = [PS(f"pr{i}", 128, 512) for i in range(2)]

        sem_names = [
            "s_c0", "s_pa0", "s_pa1", "s_wo0", "s_wo1", "s_pb0", "s_pb1",
            "s_pb2", "s_pb3", "s_st", "s_st1", "s_st2", "s_st3",
            "s_xl0", "s_xl1", "s_xr0", "s_xr1", "s_tl", "s_tq", "s_th0", "s_th1",
            "s_wg0", "s_wg1", "s_wu0", "s_wu1", "s_wd0", "s_wd1", "s_wd2",
            "s_tq1", "s_wl0", "s_wl1", "s_wl2", "s_wl3", "s_ws0", "s_ws1", "s_ws2", "s_ws3",
            "s_out",
            "t_p0", "t_wout", "t_bc", "t_dtz", "t_bbc", "t_g", "t_u",
            "t_d",
            "a_s", "a_rstd", "a_sp", "a_spp", "a_bce", "s_dA", "a_rel", "a_hq2",
            "v_s", "v_ln", "v_xln", "v_r2", "v_xq", "v_bce", "v_dt", "v_u",
            "v_dB", "v_scan", "v_ym", "v_h", "v_hq1", "v_dev",
            "p_s", "p_ya", "p_hh",
        ]
        sems = {n: es.enter_context(nc.semaphore(n)) for n in sem_names}
        block = es.enter_context(nc.Block())

        xlnT_v = xlnT[:].rearrange("p (k j t) -> p k j t", k=NDG, j=NTJ)
        xqT_v = xqT[:].rearrange("p (k j t) -> p k j t", k=NDG, j=NTJ)
        dt_v = dt_all[:].rearrange("p (k t) -> p k t", k=NDG)
        dB_v = dB_all[:].rearrange("p (k t) -> p k t", k=NDG)
        u_v = u_all[:].rearrange("p (k t) -> p k t", k=NDG)
        yp_v = yp_all[:].rearrange("p (k t) -> p k t", k=NDG)
        ya_v = y_acc[:].rearrange("p (k j t) -> p k j t", k=NDG, j=NTJ)
        yT_v = yT_b[:].rearrange("p (k j t) -> p k j t", k=NDG, j=NTJ)
        r2_v = r2[:].rearrange("p (j d) -> p j d", j=NTJ)
        tok_v = tok[:].rearrange("p (j d) -> p j d", j=NTJ)
        woT_rv = woT_r[:].rearrange("p (r o) -> p r o", r=4)
        wxpT_v = wxpT_b[:].rearrange("p (k n) -> p k n", k=NDG)
        oh_v = oh32[:].rearrange("q (n p) -> q n p", n=32)
        dA_v = dA_r[:].rearrange("p (r t) -> p r t", r=2)
        bcB_v = bcB[:].rearrange("p (r t) -> p r t", r=2)
        bcC_v = bcC[:].rearrange("p (r t) -> p r t", r=2)
        carr_v = carr[:].rearrange("p (r i) -> p r i", r=2)
        xl_v = xl_r[:].rearrange("p (r d) -> p r d", r=2)
        xr_v = xr_r[:].rearrange("p (r d) -> p r d", r=2)
        r_rv = r_r[:].rearrange("p (r t) -> p r t", r=2)
        hf_v = h_half[:].rearrange("p (s q) -> p s q", s=STP)
        maxp_v = maxp[:].rearrange("p (s i) -> p s i", s=STP)
        hqT_v = hqT[:].rearrange("p (k s t) -> p k s t", k=NKI, s=STP)
        def hqslot(st):
            return hq_b[:] if st % 2 == 0 else xlnT[:]
        wg_rv = wg_r[:].rearrange("p (r k t) -> p r k t", r=2, k=NDG)
        wu_rv = wu_r[:].rearrange("p (r k t) -> p r k t", r=2, k=NDG)
        wd_rv = wd_r[:].rearrange("p (r d) -> p r d", r=2)

        def qcol_s(c, st):
            o = (c % 2) * 8 + st * 2
            return qcol[:, o:o + 1]

        def qcol_f(c, st):
            o = (c % 2) * 8 + st * 2 + 1
            return qcol[:, o:o + 1]

        def l1(j, i):
            return lncol[:, j * 8 + i:j * 8 + i + 1]

        def l2(j, i):
            return lncol[:, 32 + j * 8 + i:32 + j * 8 + i + 1]

        NW_T = {"g": NDG, "u": NDG, "d": NKI}
        NW_FD = {"g": NI, "u": NI, "d": D}
        WSRC = {"g": wgT_h, "u": wuT_h, "d": wdT_h}
        WSCR = {"g": wgq_s, "u": wuq_s, "d": wdq_s}
        WCOL = {"g": 0, "u": 1, "d": 2}

        _PBPREV = []
        for _w in ("g", "u", "d"):
            for _i in range(NW_T[_w]):
                _PBPREV.append((_w, _i))

        def gb(i):
            return gbufA if i % 2 == 0 else gbufB

        xt_h = x_h.rearrange("(c j p) d -> c j p d", c=NCH, j=NTJ)

        def emit_sync(sync):
            p2 = ctx.pass2

            def dma(out, in_, sem, labels=(), transpose=False):
                i = sync.dma_start(out=out, in_=in_, transpose=transpose) if p2 else None
                ctx.inc(i, sem, 16, labels)

            dma(o128[:], o128_h, "s_c0")
            dma(oh32[:], oh_h, "s_c0")
            dma(ones1[:], ones_h, "s_c0")
            dma(gbufA[:, 0:NDG * 32], wxpT_h, "s_c0")
            dma(gbufA[:, 512:512 + NDG * NST], alog_h, "s_c0")
            dma(gbufB[0:1, 0:D], wdt_h, "s_c0")
            dma(gbufB[0:1, D:2 * D], bdt_h, "s_c0")
            dma(gbufB[0:1, 2 * D:2 * D + 32], bxp_h, "s_c0")
            dma(gbufB[0:1, 3 * D:4 * D], bout_h, "s_c0", labels=[("c0",)])


            for c in range(SEG):
                cf = c - 1
                if c < NCH:
                    for j in range(NTJ):
                        ctx.wait(sync, "v_xln", "xln", c, j)
                        if c > 0 and j == 0:
                            ctx.wait(sync, "v_u", "u", c - 1)
                            ctx.wait(sync, "t_dtz", "dtz", c - 1, NDG - 1)
                            if c >= 2:
                                ctx.wait_if(sync, None, "th", c - 2, 0, 1)
                                ctx.wait_if(sync, None, "th", c - 2, 1, 1)
                        src = tok_v[:, j].rearrange("p (k t) -> p k t", k=NDG)
                        dma(xlnT_v[:, :, j], src, "s_tl", labels=[("tl", c, j)],
                            transpose=True)
                if 1 <= c <= NCH:
                    for j in range(NTJ):
                        ctx.wait(sync, "v_xq", "xq", cf, j)
                        if cf > 0 and j == 0:
                            ctx.wait(sync, "t_u", "gu", cf - 1, NGI - 1)
                        src = tok_v[:, j].rearrange("p (k t) -> p k t", k=NDG)
                        dma(xqT_v[:, :, j], src,
                            "s_tq" if j < 2 else "s_tq1",
                            labels=[("tq", cf, j)], transpose=True)
                    for gi in range(NGI):
                        ic = gi % NIC
                        gg = cf * NGI + gi
                        ctx.wait_if(sync, "t_u", "guG", gg - 2)
                        if cf == 0 and gi <= 1:
                            for uu in range(4 * NW_T["g"] - 4, 4 * NW_T["g"]):
                                ctx.wait(sync, None, "stu", uu)
                        dma(wg_rv[:, gi % 2], wgq_s[ic].rearrange("k p t -> p k t"),
                            f"s_wg{gi % 2}", labels=[("wg", cf, gi)])
                    for h in range(NHALF):
                        for sj in range(STP):
                            st = h * STP + sj
                            sg = cf * NTJ + st
                            ctx.wait(sync, "a_hq2", "hq2", cf, st)
                            if cf == 0:
                                for uu in range(4 * 16 + NW_T["d"] - 4,
                                                4 * 16 + NW_T["d"]):
                                    ctx.wait(sync, None, "stu", uu)
                            if not (cf == 0 and h == 0):
                                ph = (cf, h - 1) if h > 0 else (cf - 1, NHALF - 1)
                                ctx.wait(sync, "t_d", "dmm", ph[0], ph[1])
                            src = hqslot(st).rearrange(
                                "p (k t) -> p k t", k=NKI)
                            dma(hqT_v[:, :, sj], src, f"s_th{st % 2}",
                                labels=[("th", cf, h, sj), ("thg", sg)],
                                transpose=True)
                        for k in range(NKI):
                            kg = (cf * NHALF + h) * NKI + k
                            ctx.wait_if(sync, "t_d", "wdG", kg - 2)
                            if cf == 0:
                                ctx.wait(sync, None, "st", "d", k)
                            dma(wd_rv[:, k % 2], wdq_s[k], f"s_wd{k % 2}",
                                labels=[("wd", cf, h, k)])
            if p2:
                sync.wait_ge(sems["s_out"], ctx.cnt.get("s_out", 0))

        def emit_pool(g):
            p2 = ctx.pass2

            def pstep(f, sem=None, labels=()):
                use = sem or "p_s"
                i = f() if p2 else None
                c = ctx.inc(i, use, 1, labels)
                if p2:
                    g.wait_ge(sems[use], c)

            def dma(out, in_, sem, labels=()):
                i = g.dma_start(out=out, in_=in_) if p2 else None
                ctx.inc(i, sem, 16, labels)

            for hh in range(2):
                dma(xl_v[:, hh], xt_h[0, 0][:, hh * 512:(hh + 1) * 512],
                    f"s_xl{hh}", labels=[("xl", 0, 0, hh)])

            p0ld = []
            p0i2 = 0
            for i in range(NDG):
                def tld(i=i, p0i2=p0i2):
                    if p0i2 == 0:
                        ctx.wait(g, None, "alog_ld")
                        ctx.wait(g, None, "negAexp")
                    if p0i2 == 1:
                        ctx.wait(g, None, "alog_ld")
                    if i >= 2:
                        ctx.wait(g, None, "wos", i - 2)
                    dma(gb(p0i2)[:, 0:D], woT_h[i], f"s_wo{p0i2 % 2}",
                        labels=[("ldwo", i)])
                p0ld.append(tld)
                if i >= 1:
                    def tws(i=i, p0i2=p0i2):
                        ctx.wait(g, None, "ldwo", i - 1)
                        dma(woq_s[i - 1], gb(p0i2 - 1)[:, 0:D],
                            f"s_ws{(i - 1) % 4}", labels=[("wos", i - 1)])
                    p0ld.append(tws)
                p0i2 += 1
            def twsl(p0i2=p0i2):
                ctx.wait(g, None, "ldwo", NDG - 1)
                dma(woq_s[NDG - 1], gb(p0i2 - 1)[:, 0:D],
                    f"s_ws{(NDG - 1) % 4}", labels=[("wos", NDG - 1)])
            p0ld.append(twsl)
            for w in ("g", "u", "d"):
                for i in range(NW_T[w]):
                    def ta2(w=w, i=i, p0i2=p0i2):
                        if p0i2 in (NDG, NDG + 1):
                            ctx.wait(g, None, "wos", p0i2 - 2)
                        ctx.wait_if(g, None, "p0c", p0i2 - 2)
                        dma(gb(p0i2)[:, 0:NW_FD[w]], WSRC[w][i],
                            f"s_pa{p0i2 % 2}", labels=[("ldA", w, i)])
                    p0ld.append(ta2)
                    p0i2 += 1
            PB_UNITS2 = []
            for w in ("g", "u"):
                for i in range(NW_T[w]):
                    for q in range(4):
                        PB_UNITS2.append((w, i, q))
            for i in range(NW_T["d"]):
                PB_UNITS2.append(("d", i, 0))
            for ug, (w, i, q) in enumerate(PB_UNITS2):
                def tbu2(w=w, i=i, q=q, ug=ug):
                    s = ug % 4
                    if ug >= 4:
                        ctx.wait_if(g, None, "pBu", ug - 4)
                    else:
                        ctx.wait(g, "v_s", "pA", "d", NW_T["d"] - 1)
                    srcu = (WSRC[w][i][:, q * D:(q + 1) * D] if w != "d"
                            else WSRC["d"][i])
                    dma(gbufB[:, s * D:(s + 1) * D], srcu,
                        f"s_pb{ug % 4}", labels=[("ldU", ug)])
                p0ld.append(tbu2)

            n_p0ld = len(p0ld)

            def p0ld_drain(kmax, cap=None):
                k = 0
                while p0ld and k < kmax:
                    if cap is not None and n_p0ld - len(p0ld) >= cap:
                        return
                    p0ld.pop(0)()
                    k += 1

            for c in range(SEG):
                cf = c - 1
                if c < NCH:
                    for j in range(1, NTJ):
                        for hh in range(2):
                            hg = (c * NTJ + j) * 2 + hh
                            ctx.wait_if(g, "v_ln", "xlh", hg - 2)
                            dma(xl_v[:, hh],
                                xt_h[c, j][:, hh * 512:(hh + 1) * 512],
                                f"s_xl{hh}", labels=[("xl", c, j, hh)])
                if c + 1 < NCH:
                    for hh in range(2):
                        hg = ((c + 1) * NTJ) * 2 + hh
                        ctx.wait_if(g, "v_ln", "xlh", hg - 2)
                        dma(xl_v[:, hh],
                            xt_h[c + 1, 0][:, hh * 512:(hh + 1) * 512],
                            f"s_xl{hh}", labels=[("xl", c + 1, 0, hh)])
                if 2 <= c <= NCH + 1:
                    ctx.wait(g, "v_dev", "dev", c - 2)
                    ov = out_h.rearrange("(c j p) d -> c j p d", c=NCH, j=NTJ)[c - 2]
                    for j in range(NTJ):
                        dma(ov[j], r2_v[:, j], "s_out",
                            labels=[("out", c - 2)] if j == NTJ - 1 else ())
                if 1 <= c <= NCH:
                    for j in range(NTJ):
                        for oh in range(2):
                            k = cf * NTJ * 2 + j * 2 + oh
                            ctx.wait_if(g, "v_r2", "r2G", k - 2)
                            dma(xr_v[:, (j * 2 + oh) % 2],
                                xt_h[cf, j][:, oh * 512:(oh + 1) * 512],
                                f"s_xr{(j * 2 + oh) % 2}",
                                labels=[("xr", cf, j, oh)])
                if c == 2:
                    p0ld_drain(10 ** 9)
                if c < NCH:
                    for n in range(NST):
                        ctx.wait(g, "v_ym", "ym", c, n)
                        if n == 0:
                            if c > 0:
                                ctx.wait(g, None, "yT", c - 1)
                            pstep(lambda: g.tensor_copy(out=y_acc[:], in_=yp_all[:]),
                                  "p_ya", [("ya", c, n)])
                        else:
                            pstep(lambda: g.tensor_tensor(
                                out=y_acc[:], in0=y_acc[:], in1=yp_all[:], op=Op.add),
                                "p_ya", [("ya", c, n)])
                        p0ld_drain(8, cap=(2 * NDG + sum(NW_T.values()) + 4)
                                   if c == 0 else None)
                if c < NCH:
                    if c >= 1:
                        ctx.wait(g, "t_wout", "wout", c - 1, NTJ - 1, 1)
                    pstep(lambda: g.tensor_copy(out=yT_b[:], in_=y_acc[:]),
                          "p_ya", [("yT", c)])


        def emit_pe(t):
            p2 = ctx.pass2

            def mm(out, lhsT, rhs, start, stop, sem=None, labels=()):
                i = t.matmul(out=out, lhsT=lhsT, rhs=rhs, start=start, stop=stop) if p2 else None
                if sem:
                    ctx.inc(i, sem, 1, labels)

            for c in range(SEG):
                cf = c - 1
                if c < NCH:
                    ctx.wait(t, "s_tl", "tl", c, NTJ - 1)
                    if c >= 1:
                        ctx.wait(t, "a_bce", "bceB", c - 1, NST - 1)
                    for dg in range(NDG):
                        mm(pr[0][0:32, 0:TT], wxpT_v[:, dg],
                           xlnT_v[:, dg].rearrange("p j t -> p (j t)"), dg == 0, False)
                    mm(pr[0][0:32, 0:TT], bxp_b[:], ones1[:], False, True,
                       "t_bc", [("bc", c)])
                    for dg in range(NDG):
                        pp = pr[dg % 2]
                        if dg == 0:
                            ctx.wait(t, "v_bce", "bce", c)
                        elif dg == 1:
                            if c >= 1:
                                ctx.wait(t, "a_bce", "bceC", c - 1, NST - 1)
                        else:
                            ctx.wait(t, "a_spp", "spp", c, dg - 2)
                        mm(pp[:, 0:TT], wdt_b[0:1, dg * 128:(dg + 1) * 128],
                           xlnT[0:1, 0:TT], True, False)
                        mm(pp[:, 0:TT], bdt_b[0:1, dg * 128:(dg + 1) * 128],
                           ones1[:], False, True, "t_dtz", [("dtz", c, dg)])

                def bbc_pair(n, c=c):
                    if n == 0:
                        ctx.wait(t, "a_spp", "spp", c, NDG - 2)
                        ctx.wait(t, "v_bce", "bce", c)
                    else:
                        ctx.wait(t, "a_bce", "bceB", c, n - 1)
                    mm(pr[0][:, 0:TT], oh_v[:, n], bct[:], True, True,
                       "t_bbc", [("bbcB", c, n)])
                    if n == 0:
                        ctx.wait(t, "a_spp", "spp", c, NDG - 1)
                    else:
                        ctx.wait(t, "a_bce", "bceC", c, n - 1)
                    mm(pr[1][:, 0:TT], oh_v[:, NST + n], bct[:], True, True,
                       "t_bbc", [("bbcC", c, n)])

                if c < NCH:
                    bbc_pair(0)
                    if c == 0:
                        for n in range(1, NST):
                            bbc_pair(n)
                        ctx.wait(t, "v_s", "partials")
                        mm(pd[0][:, 0:4], o128[:], wsum[:, 0:4], True, True,
                           "t_p0", [("tot",)])
                if 1 <= c <= NCH:
                    ctx.wait(t, None, "yT", cf)
                    for rnd in range(2):
                        for dg in range(NDG):
                            wog = (cf * 2 + rnd) * NDG + dg
                            ctx.wait(t, None, "wol", cf, rnd, dg)
                            for b in range(4):
                                k = rnd * 4 + b
                                j, oh = k // 2, k % 2
                                if dg == 0:
                                    if rnd == 0:
                                        if c >= 2:
                                            ctx.wait(t, "v_dev", "devst",
                                                     c - 2, 1, b // 2, b % 2)
                                        elif b == 0:
                                            ctx.wait(t, "v_s", "cols")
                                    else:
                                        ctx.wait(t, "v_r2", "r2",
                                                 cf, b // 2, b % 2)
                                mm(pd[b][:], yT_v[:, dg, j],
                                   woT_rv[:, wog % 4, oh * 512:(oh + 1) * 512],
                                   dg == 0, False,
                                   "t_wout" if b == 3 else None,
                                   [("wocG", wog)])
                        for b in range(4):
                            k = rnd * 4 + b
                            j, oh = k // 2, k % 2
                            mm(pd[b][:], ones1[0:1, 0:128],
                               bout_b[0:1, oh * 512:(oh + 1) * 512], False, True,
                               "t_wout", [("wout", cf, j, oh)])
                if 1 <= c < NCH:
                    bbc_pair(1)
                    bbc_pair(2)
                    if c == 1:
                        for m in range(3, NST):
                            bbc_pair(m)
                for gi in range(NGI):
                    h, ic = divmod(gi, NIC)
                    if 1 <= c <= NCH and gi == 0:
                        ctx.wait(t, "s_tq", "tq", cf, 1)
                    if 1 <= c <= NCH and gi == NIC:
                        ctx.wait(t, "s_tq1", "tq", cf, 3)
                    if 1 <= c <= NCH:
                        gg = cf * NGI + gi
                        ctx.wait(t, None, "wg", cf, gi)
                        ctx.wait(t, None, "wu", cf, gi)
                        for sj in range(STP):
                            st = h * STP + sj
                            idx = gi * STP + sj
                            pidx = cf * NPAIR + idx
                            ctx.wait_if(t, "a_rel", "relG", pidx - 1)
                            for dg in range(NDG):
                                mm(pg[:], xqT_v[:, dg, st], wg_rv[:, gi % 2, dg],
                                   dg == 0, dg == NDG - 1,
                                   "t_g" if dg == NDG - 1 else None,
                                   [("g", cf, idx)])
                            ctx.wait_if(t, "v_h", "hG", pidx - 1)
                            for dg in range(NDG):
                                labs = [("u", cf, idx)]
                                if sj == STP - 1:
                                    labs += [("gu", cf, gi), ("guG", gg)]
                                mm(pu[:], xqT_v[:, dg, st], wu_rv[:, gi % 2, dg],
                                   dg == 0, dg == NDG - 1,
                                   "t_u" if dg == NDG - 1 else None, labs)
                    if 2 <= c < NCH and gi % 2 == 1:
                        bbc_pair((gi + 5) // 2)
                if 1 <= c <= NCH:
                    m0 = (NGI - 1 + 5) // 2 + 1
                    for h in range(NHALF):
                        ctx.wait(t, None, "th", cf, h, 0)
                        ctx.wait(t, None, "th", cf, h, STP - 1)
                        for k in range(NKI):
                            if (h == 0 and 2 <= c < NCH and k % 6 == 4
                                    and m0 + (k - 4) // 6 < NST):
                                bbc_pair(m0 + (k - 4) // 6)
                            ctx.wait(t, "s_wd", "wd", cf, h, k)
                            kg = (cf * NHALF + h) * NKI + k
                            for sj in range(STP):
                                for dh in range(2):
                                    pp = pd[sj * 2 + dh]
                                    if k == 0:
                                        if h == 0:
                                            ctx.wait(t, "v_r2", "r2",
                                                     cf, 2 + sj, dh)
                                        else:
                                            ctx.wait(t, "v_dev", "devst",
                                                     cf, 0, sj, dh)
                                    labs = [("wdG", kg)]
                                    if k == NKI - 1:
                                        labs.append(("dmm", cf, h))
                                    mm(pp[:], hqT_v[:, k, sj],
                                       wd_rv[:, k % 2, dh * 512:(dh + 1) * 512],
                                       k == 0, k == NKI - 1,
                                       "t_d" if (sj == STP - 1 and dh == 1) else None,
                                       labs)
                        if h == 0 and 2 <= c < NCH:
                            for m in range(m0 + 5, NST):
                                bbc_pair(m)

        def emit_act(a):
            p2 = ctx.pass2

            def adma(out, in_, sem, labels=()):
                i = a.dma_start(out=out, in_=in_) if p2 else None
                ctx.inc(i, sem, 16, labels)

            pstq = []
            PB_UNITS3 = []
            for w in ("g", "u"):
                for i in range(NW_T[w]):
                    for q in range(4):
                        PB_UNITS3.append((w, i, q))
            for i in range(NW_T["d"]):
                PB_UNITS3.append(("d", i, 0))
            for ug, (w, i, q) in enumerate(PB_UNITS3):
                def tst(w=w, i=i, q=q, ug=ug):
                    s = ug % 4
                    ctx.wait(a, "v_s", "pBu", ug)
                    srcu = wd_rv[:, ug % 2]
                    sem = "s_st" if ug % 4 == 0 else f"s_st{ug % 4}"
                    labs = [("stu", ug)]
                    if w == "d":
                        labs.append(("st", "d", i))
                        adma(WSCR["d"][i], srcu, sem, labels=labs)
                    else:
                        if q == 3:
                            labs.append(("st", w, i))
                        adma(WSCR[w][2 * q:2 * q + 2, i]
                             .rearrange("ic p t -> p ic t"),
                             srcu.rearrange("p (ic t) -> p ic t", ic=2),
                             sem, labels=labs)
                pstq.append(tst)

            def pst_drain(kmax):
                k = 0
                while pstq and k < kmax:
                    pstq.pop(0)()
                    k += 1

            def wu_load(cf, gi):
                ic = gi % NIC
                gg = cf * NGI + gi
                ctx.wait_if(a, "t_u", "guG", gg - 2)
                if cf == 0 and gi <= 1:
                    for uu in range(8 * NW_T["g"] - 4, 8 * NW_T["g"]):
                        ctx.wait(a, None, "stu", uu)
                adma(wu_rv[:, gi % 2], wuq_s[ic].rearrange("k p t -> p k t"),
                     f"s_wu{gi % 2}", labels=[("wu", cf, gi)])

            def wol_round(cf, rnd):
                for dg in range(NDG):
                    wog = (cf * 2 + rnd) * NDG + dg
                    ctx.wait_if(a, None, "wocG", wog - 4)
                    if cf == 0 and rnd == 0:
                        ctx.wait(a, f"s_ws{dg % 4}", "wos", 4 + dg % 4)
                    adma(woT_rv[:, wog % 4], woq_s[dg], f"s_wl{wog % 4}",
                         labels=[("wol", cf, rnd, dg)])

            def astep(f, sem=None, labels=()):
                use = sem or "a_s"
                i = f() if p2 else None
                c = ctx.inc(i, use, 1, labels)
                if p2:
                    a.wait_ge(sems[use], c)

            def act(out, in_, func, bias=None, scale=None):
                kw = {}
                if bias is not None:
                    kw["bias"] = bias
                if scale is not None:
                    kw["scale"] = scale
                return a.activation(out=out, in_=in_, func=func, **kw)

            def relu_step(cf, idx):
                pidx = cf * NPAIR + idx
                ctx.wait(a, "t_g", "g", cf, idx)
                ctx.wait_if(a, "v_h", "hG", pidx - 2)
                astep(lambda idx=idx: act(r_rv[:, idx % 2], pg[:], AF.Relu),
                      None, [("relG0", pidx)])
                astep(lambda idx=idx: act(r_rv[:, idx % 2], r_rv[:, idx % 2],
                                          AF.Square),
                      "a_rel", [("rel", cf, idx), ("relG", pidx)])

            act_queue = []
            act_slot = [0]

            def hq2_drain(kmax):
                k = 0
                while act_queue and k < kmax:
                    cfq, st, sj, q = act_queue.pop(0)
                    qg = act_slot[0]
                    act_slot[0] += 1
                    h = st // STP
                    lastpiece = 2 * (h * NIC + (q * 512 + 511) // ICW) + sj
                    ctx.wait(a, "v_h", "h", cfq, lastpiece)
                    ctx.wait(a, None, "qcol", cfq, st)
                    if q == 0:
                        ctx.wait(a, None, "xq", cfq, NTJ - 1)
                    astep(lambda st=st, q=q, qg=qg, cfq=cfq, sj=sj: act(
                        t1q[:],
                        hf_v[:, sj][:, q * 512:(q + 1) * 512],
                        AF.Identity, scale=qcol_s(cfq, st), bias=ccol[:, 3:4]),
                          None, [("hq1", cfq, st, q)])
                    if q == 0:
                        sg = cfq * NTJ + st
                        ctx.wait_if(a, "s_th", "thg", sg - 2)
                    if st % 2 == 1 and q == 0:
                        ctx.wait_if(a, None, "u", cfq + 1)
                        ctx.wait_if(a, None, "dtz", cfq + 1, NDG - 1)
                    labels = []
                    if q == NQ - 1:
                        labels.append(("hq2", cfq, st))
                    astep(lambda st=st, q=q, qg=qg: act(
                        hqslot(st)[:, q * 512:(q + 1) * 512],
                        t1q[:], AF.Identity, bias=ccol[:, 2:3]),
                          "a_hq2", labels)
                    k += 1

            ctx.wait(a, "v_s", "alog_ld")
            astep(lambda: act(negA[:], gbufA[:, 512:512 + NDG * NST], AF.Exp),
                  "a_rstd", [("negAexp",)])

            for c in range(SEG):
                cf = c - 1
                if c < NCH:
                    for j in range(NTJ):
                        ctx.wait(a, "v_ln", "ln1", c, j)
                        astep(lambda j=j: act(l1(j, 1), l1(j, 1), AF.Ln,
                                              bias=ccol[:, 0:1]))
                        astep(lambda j=j: act(l1(j, 1), l1(j, 1), AF.Exp,
                                              scale=ccol[:, 1:2]),
                              "a_rstd", [("rstd1", c, j)])
                if c < NCH:
                    for dg in range(NDG):
                        pp = pr[dg % 2]
                        ctx.wait(a, "t_dtz", "dtz", c, dg)
                        if dg >= 1:
                            ctx.wait(a, None, "dt", c, dg - 1)
                        elif c >= 1:
                            ctx.wait(a, None, "dt", c - 1, NDG - 1)
                        astep(lambda pp=pp: act(sp1[:], pp[:, 0:TT], AF.Abs))
                        astep(lambda pp=pp: act(sp2[:], pp[:, 0:TT], AF.Relu),
                              "a_spp", [("spp", c, dg)])
                        astep(lambda: act(sp1[:], sp1[:], AF.Exp, scale=-1.0))
                        astep(lambda: act(sp1[:], sp1[:], AF.Ln, bias=1.0),
                              "a_sp", [("sp", c, dg)])
                if c == 2:
                    pst_drain(10 ** 9)
                if 1 <= c <= NCH:
                    wol_round(cf, 0)
                    wol_round(cf, 1)
                if 1 <= c <= NCH:
                    for j in range(NTJ):
                        ctx.wait(a, "v_ln", "ln2", cf, j)
                        astep(lambda j=j: act(l2(j, 1), l2(j, 1), AF.Ln,
                                              bias=ccol[:, 0:1]))
                        astep(lambda j=j: act(l2(j, 1), l2(j, 1), AF.Exp,
                                              scale=ccol[:, 1:2]),
                              "a_rstd", [("rstd2", cf, j)])
                if c < NCH:
                    for n in range(NST):
                        m = c * NST + n
                        ctx.wait(a, "t_bbc", "bbcB", c, n)
                        ctx.wait_if(a, "v_dB", "dBG", m - 2)
                        astep(lambda n=n: act(bcB_v[:, n % 2], pr[0][:, 0:TT],
                                              AF.Copy),
                              "a_bce", [("bceB", c, n)])
                        ctx.wait(a, "t_bbc", "bbcC", c, n)
                        ctx.wait_if(a, "v_ym", "ymG", m - 2)
                        astep(lambda n=n: act(bcC_v[:, n % 2], pr[1][:, 0:TT],
                                              AF.Copy),
                              "a_bce", [("bceC", c, n)])
                        for dg in range(NDG):
                            gidx = m * NDG + dg
                            ctx.wait(a, "v_dt", "dt", c, dg)
                            ctx.wait_if(a, "v_scan", "scani", gidx - 2)
                            astep(lambda n=n, dg=dg, gidx=gidx: act(
                                dA_v[:, gidx % 2], dt_v[:, dg], AF.Exp,
                                scale=negA[:, dg * NST + n:dg * NST + n + 1]),
                                "s_dA", [("dA", c, n, dg)])
                        if c == 1:
                            pst_drain(8)
                        if 2 <= c <= NCH and n >= 2:
                            for gi2 in (2 * (n - 2), 2 * n - 3):
                                if gi2 < NGI:
                                    wu_load(cf, gi2)
                        if 1 <= c <= NCH:
                            ps = 99 if c == 1 else 2
                            rs = 99 if c == 1 else 2
                            if n == ps + 4:
                                act_queue.extend(hq_entries(cf, 0))
                            if c >= 2 and n == 10:
                                act_queue.extend(hq_entries(cf, 1))
                            if n >= ps + 4:
                                hq2_drain(4)
                            if n >= rs and 4 * (n - rs) < NPAIR:
                                for q in range(4):
                                    relu_step(cf, 4 * (n - rs) + q)
                elif 1 <= c <= NCH:
                    for idx in range(NPAIR):
                        if idx % 2 == 0:
                            wu_load(cf, idx // 2)
                        relu_step(cf, idx)
                        if idx == NPAIR // 2:
                            act_queue.extend(hq_entries(cf, 0))
                        if idx >= NPAIR // 2:
                            hq2_drain(1)
                if c == 1:
                    pst_drain(10 ** 9)
                    for idx in range(NPAIR):
                        if idx % 2 == 0:
                            wu_load(cf, idx // 2)
                        relu_step(cf, idx)
                        if idx == NPAIR // 2:
                            act_queue.extend(hq_entries(cf, 0))
                        if idx >= NPAIR // 2:
                            hq2_drain(1)
                if 1 <= c <= NCH:
                    hq2_drain(10 ** 9)
                    if c == 1 or c == NCH:
                        act_queue.extend(hq_entries(cf, 1))
                        hq2_drain(10 ** 9)

        def hq_entries(cf, h):
            out = []
            for q in range(NQ):
                for sj in range(STP):
                    out.append((cf, h * STP + sj, sj, q))
            return out


        def vstep_factory(v):
            def vstep(f, sem=None, labels=()):
                use = sem or "v_s"
                i = f() if ctx.pass2 else None
                c = ctx.inc(i, use, 1, labels)
                if ctx.pass2:
                    v.wait_ge(sems[use], c)
            return vstep

        def emit_dve(v):
            p2 = ctx.pass2
            vstep = vstep_factory(v)

            def ts(out, in0, s1, s2, op0, op1):
                return v.tensor_scalar(out=out, in0=in0, scalar1=s1, scalar2=s2,
                                       op0=op0, op1=op1)

            ctx.wait(v, "s_scr", "c0")
            vstep(lambda: v.memset(ccol[:, 0:1], EPS_LN))
            vstep(lambda: v.memset(ccol[:, 1:2], -0.5))
            vstep(lambda: v.memset(ccol[:, 2:3], -MAGIC))
            vstep(lambda: v.memset(ccol[:, 3:4], MAGIC))
            vstep(lambda: v.memset(carr_v[:, 1], 0.0))
            vstep(lambda: v.memset(wsum[:], 0.0))
            vstep(lambda: v.tensor_copy(out=wxpT_b[:], in_=gbufA[:, 0:NDG * 32]))
            vstep(lambda: v.tensor_copy(out=wdt_b[:], in_=gbufB[0:1, 0:D]))
            vstep(lambda: v.tensor_copy(out=bdt_b[:], in_=gbufB[0:1, D:2 * D]))
            vstep(lambda: v.tensor_copy(out=bxp_b[:], in_=gbufB[0:1, 2 * D:2 * D + 32]))
            vstep(lambda: v.tensor_copy(out=bout_b[:], in_=gbufB[0:1, 3 * D:4 * D]),
                  labels=[("alog_ld",)])
            ctx.wait(v, "a_rstd", "negAexp")
            vstep(lambda: ts(negA[:], negA[:], -1.0, 0.0, Op.mult, Op.add))
            p0q = []
            p0i = NDG
            for w in ("g", "u", "d"):
                col = WCOL[w]
                for i in range(NW_T[w]):
                    def ta(w=w, i=i, p0i=p0i, col=col):
                        ctx.wait(v, None, "ldA", w, i)
                        vstep(lambda: v.tensor_reduce(
                            out=wsum[:, 3:4], in_=gb(p0i)[:, 0:NW_FD[w]],
                            axis=AX.X, op=Op.add, apply_absolute_value=True),
                            labels=[("p0c", p0i)])
                        vstep(lambda: v.tensor_tensor(
                            out=wsum[:, col:col + 1], in0=wsum[:, col:col + 1],
                            in1=wsum[:, 3:4], op=Op.add), labels=[("pA", w, i)])
                    p0q.append(ta)
                    p0i += 1

            def tpart():
                vstep(lambda: v.memset(wsum[:, 3:4], 0.0),
                      labels=[("partials",)])
            p0q.append(tpart)
            NG = float(NDG * 128 * NI)
            NDW = float(NKI * 128 * D)

            def tmeans():
                ctx.wait(v, "t_p0", "tot")
                vstep(lambda: ts(mwc[:, 5:6], pd[0][:, 0:1], 1.0 / NG, EPS_Q,
                                 Op.mult, Op.max))
                vstep(lambda: ts(mwc[:, 6:7], pd[0][:, 1:2], 1.0 / NG, EPS_Q,
                                 Op.mult, Op.max))
                vstep(lambda: ts(mwc[:, 3:4], pd[0][:, 2:3], 1.0 / NDW, EPS_Q,
                                 Op.mult, Op.max))
                vstep(lambda: v.reciprocal(out=mwc[:, 0:1], in_=mwc[:, 5:6]))
                vstep(lambda: v.reciprocal(out=mwc[:, 1:2], in_=mwc[:, 6:7]))
                vstep(lambda: v.reciprocal(out=mwc[:, 2:3], in_=mwc[:, 3:4]))
                vstep(lambda: v.tensor_tensor(out=mwc[:, 4:5], in0=mwc[:, 5:6],
                                              in1=mwc[:, 5:6], op=Op.mult))
                vstep(lambda: v.tensor_tensor(out=mwc[:, 4:5], in0=mwc[:, 4:5],
                                              in1=mwc[:, 6:7], op=Op.mult),
                      labels=[("cols",)])
            p0q.append(tmeans)
            PB_UNITS = []
            for w in ("g", "u"):
                for i in range(NW_T[w]):
                    for q in range(4):
                        PB_UNITS.append((w, i, q))
            for i in range(NW_T["d"]):
                PB_UNITS.append(("d", i, 0))
            for ug, (w, i, q) in enumerate(PB_UNITS):
                def tbu(w=w, i=i, q=q, ug=ug):
                    col = WCOL[w]
                    s = ug % 4
                    gbd = gbufB[:, s * D:(s + 1) * D]
                    g16 = wd_rv[:, ug % 2]
                    ctx.wait(v, None, "ldU", ug)
                    if ug >= 2:
                        ctx.wait(v, None, "stu", ug - 2)
                    vstep(lambda: ts(gbd, gbd, mwc[:, col:col + 1], MAGIC,
                                     Op.mult, Op.add))
                    vstep(lambda: ts(gbd, gbd, MAGIC, -1.0,
                                     Op.subtract, Op.max))
                    labs = [("pBu", ug)]
                    if w == "d" or q == 3:
                        labs.append(("pBq", w, i))
                    vstep(lambda: ts(g16, gbd, 1.0, 0.0, Op.min, Op.add),
                          labels=labs)
                p0q.append(tbu)

            n_p0q = len(p0q)

            def p0_drain(kmax, cap=None):
                k = 0
                while p0q and k < kmax:
                    if cap is not None and n_p0q - len(p0q) >= cap:
                        return
                    p0q.pop(0)()
                    k += 1

            for c in range(SEG):
                cf = c - 1
                if c == 1:
                    p0_drain(2)
                if c == 2:
                    p0_drain(10 ** 9)
                if c < NCH:
                    for j in range(NTJ):
                        for hh in range(2):
                            ctx.wait(v, None, "xl", c, j, hh)
                            vstep(lambda j=j, hh=hh: v.bn_stats(
                                out=bns[:, hh * 6:hh * 6 + 6], in_=xl_v[:, hh]))
                        vstep(lambda j=j: v.bn_aggr(
                            out=lncol[:, j * 8:j * 8 + 2], in_=bns[:]))
                        vstep(lambda j=j: ts(l1(j, 2), l1(j, 0), -1.0, 0.0,
                                             Op.mult, Op.add),
                              "v_ln", [("ln1", c, j)])
                        ctx.wait(v, "a_rstd", "rstd1", c, j)
                        if c >= 2:
                            ctx.wait(v, "s_tq", "tq", c - 2, NTJ - 1)
                        jg = c * NTJ + j
                        for hh in range(2):
                            hg = (c * NTJ + j) * 2 + hh
                            labs = [("xlh", hg)]
                            if hh == 1:
                                labs += [("xln", c, j), ("xlnG", jg)]
                            vstep(lambda j=j, c=c, hh=hh: ts(
                                tok_v[:, j][:, hh * 512:(hh + 1) * 512],
                                xl_v[:, hh], l1(j, 2), l1(j, 1),
                                Op.add, Op.mult),
                                "v_ln" if hh == 0 else "v_xln", labs)
                if 2 <= c <= NCH + 1:
                    cd = c - 2
                    ctx.wait(v, "t_d", "dmm", cd, 1)
                    for sj in range(STP):
                        st = 2 + sj
                        for dh in range(2):
                            def f(sj=sj, st=st, dh=dh, cd=cd):
                                sl = r2_v[:, st][:, dh * 512:(dh + 1) * 512]
                                return v.scalar_tensor_tensor(
                                    out=sl, in0=pd[sj * 2 + dh][:],
                                    scalar=qcol_f(cd, st), in1=sl,
                                    op0=Op.mult, op1=Op.add)
                            labs = [("devst", cd, 1, sj, dh)]
                            if sj == STP - 1 and dh == 1:
                                labs.append(("dev", cd))
                            vstep(f, "v_dev", labs)
                if c < NCH:
                    ctx.wait(v, "t_bc", "bc", c)
                    vstep(lambda: v.tensor_copy(out=bct[:], in_=pr[0][0:32, 0:TT]),
                          "v_bce", [("bce", c)])
                    for dg in range(NDG):
                        ctx.wait(v, "a_sp", "sp", c, dg)
                        vstep(lambda dg=dg: v.tensor_tensor(
                            out=dt_v[:, dg], in0=sp1[:], in1=sp2[:], op=Op.add),
                            "v_dt", [("dt", c, dg)])
                    ctx.wait(v, "s_tl", "tl", c, NTJ - 1)
                    vstep(lambda: v.tensor_tensor(
                        out=u_all[:], in0=dt_all[:], in1=xlnT[:], op=Op.mult),
                        "v_u", [("u", c)])
                if 1 <= c <= NCH:
                    for j in range(NTJ):
                        for oh in range(2):
                            k = cf * NTJ * 2 + j * 2 + oh
                            ctx.wait(v, "t_wout", "wout", cf, j, oh)
                            ctx.wait(v, "s_x", "xr", cf, j, oh)
                            if j == 0 and oh == 0 and cf >= 1:
                                ctx.wait(v, "s_out", "out", cf - 1)
                            def f(j=j, oh=oh):
                                sl = r2_v[:, j][:, oh * 512:(oh + 1) * 512]
                                return v.scalar_tensor_tensor(
                                    out=sl, in0=pd[(j * 2 + oh) % 4][:], scalar=1.0,
                                    in1=xr_v[:, (j * 2 + oh) % 2],
                                    op0=Op.mult, op1=Op.add)
                            labs = [("r2", cf, j, oh), ("r2G", k)]
                            if j == NTJ - 1 and oh == 1:
                                labs.append(("r2done", cf))
                            vstep(f, "v_r2", labs)
                    for j in range(NTJ):
                        vstep(lambda j=j: v.bn_stats(out=bns[:, 0:6],
                                                     in_=r2_v[:, j][:, 0:512]))
                        vstep(lambda j=j: v.bn_stats(out=bns[:, 6:12],
                                                     in_=r2_v[:, j][:, 512:1024]))
                        vstep(lambda j=j: v.bn_aggr(
                            out=lncol[:, 32 + j * 8:32 + j * 8 + 2], in_=bns[:]))
                        vstep(lambda j=j: ts(l2(j, 2), l2(j, 0), -1.0, 0.0,
                                             Op.mult, Op.add),
                              "v_ln", [("ln2", cf, j)])
                        vstep(lambda j=j: v.tensor_reduce(
                            out=l2(j, 3), in_=r2_v[:, j], axis=AX.X, op=Op.max))
                        vstep(lambda j=j: v.tensor_reduce(
                            out=l2(j, 4), in_=r2_v[:, j], axis=AX.X, op=Op.min))
                        vstep(lambda j=j: v.scalar_tensor_tensor(
                            out=l2(j, 3), in0=l2(j, 3), scalar=1.0, in1=l2(j, 2),
                            op0=Op.mult, op1=Op.add))
                        vstep(lambda j=j: v.scalar_tensor_tensor(
                            out=l2(j, 4), in0=l2(j, 4), scalar=-1.0, in1=l2(j, 2),
                            op0=Op.mult, op1=Op.subtract))
                        vstep(lambda j=j: v.tensor_tensor(
                            out=l2(j, 3), in0=l2(j, 3), in1=l2(j, 4), op=Op.max))
                    for j in range(NTJ):
                        ctx.wait(v, "a_rstd", "rstd2", cf, j)
                        vstep(lambda j=j: v.tensor_tensor(
                            out=l2(j, 3), in0=l2(j, 3), in1=l2(j, 1), op=Op.mult))
                        vstep(lambda j=j: ts(l2(j, 3), l2(j, 3), EPS_Q, 0.0,
                                             Op.max, Op.add))
                        vstep(lambda j=j: v.reciprocal(out=l2(j, 5), in_=l2(j, 3)))
                        vstep(lambda j=j: ts(l2(j, 5), l2(j, 5), 127.0, 0.0,
                                             Op.mult, Op.add))
                        vstep(lambda j=j: v.tensor_tensor(
                            out=l2(j, 6), in0=l2(j, 1), in1=l2(j, 5), op=Op.mult))
                        vstep(lambda j=j: ts(l2(j, 3), l2(j, 3), 1.0 / 127.0, 0.0,
                                             Op.mult, Op.add))
                        vstep(lambda j=j: v.tensor_tensor(
                            out=l2(j, 4), in0=l2(j, 3), in1=l2(j, 3), op=Op.mult))
                        vstep(lambda j=j: v.tensor_tensor(
                            out=l2(j, 4), in0=l2(j, 4), in1=l2(j, 3), op=Op.mult))
                        vstep(lambda j=j: v.tensor_tensor(
                            out=l2(j, 7), in0=l2(j, 4), in1=mwc[:, 4:5], op=Op.mult))
                        if c < NCH:
                            ctx.wait(v, "s_tl", "tl", c, NTJ - 1)
                        else:
                            ctx.wait(v, "s_tq", "tq", cf - 1, NTJ - 1)
                        if cf >= 1 and j == 0:
                            ctx.wait(v, "a_hq2", "hq2", cf - 1, NTJ - 1)
                        for ph in range(2):
                            vstep(lambda j=j, ph=ph: ts(
                                t1q[:], r2_v[:, j][:, ph * 512:(ph + 1) * 512],
                                l2(j, 2), l2(j, 6), Op.add, Op.mult))
                            vstep(lambda j=j, ph=ph: ts(
                                tok_v[:, j][:, ph * 512:(ph + 1) * 512],
                                t1q[:], MAGIC, MAGIC, Op.add, Op.subtract),
                                "v_xq" if ph == 1 else None,
                                [("xq", cf, j)] if ph == 1 else ())
                if c < NCH:
                    for n in range(NST):
                        m = c * NST + n
                        ctx.wait(v, "a_bce", "bceB", c, n)
                        def fdB(n=n):
                            bc = bcB_v[:, n % 2].unsqueeze(1).broadcast_to(
                                [128, NDG, TT])
                            return v.tensor_tensor(out=dB_v, in0=u_v, in1=bc,
                                                   op=Op.mult)
                        vstep(fdB, "v_dB", [("dB", c, n), ("dBG", m)])
                        for dg in range(NDG):
                            gidx = m * NDG + dg
                            ctx.wait(v, "s_dA", "dA", c, n, dg)
                            def fsc(n=n, dg=dg, gidx=gidx, c=c):
                                col = n * NDG + dg
                                return v.tensor_tensor_scan(
                                    out=dB_v[:, dg], data0=dA_v[:, gidx % 2],
                                    data1=dB_v[:, dg],
                                    initial=carr_v[:, (c + 1) % 2][:, col:col + 1],
                                    op0=Op.mult, op1=Op.add)
                            vstep(fsc, "v_scan", [("scani", gidx)])
                        def fc(n=n, c=c):
                            src = dB_all[:].rearrange(
                                "p (k t) -> p k t", k=NDG)[:, :, TT - 1]
                            return v.tensor_copy(
                                out=carr_v[:, c % 2][:, n * NDG:(n + 1) * NDG],
                                in_=src)
                        vstep(fc)
                        ctx.wait(v, "a_bce", "bceC", c, n)
                        if m >= 1:
                            ctx.wait(v, "p_ya", "ya", (m - 1) // NST, (m - 1) % NST)
                        def fy(n=n):
                            bc = bcC_v[:, n % 2].unsqueeze(1).broadcast_to(
                                [128, NDG, TT])
                            return v.tensor_tensor(out=yp_v, in0=dB_v, in1=bc,
                                                   op=Op.mult)
                        vstep(fy, "v_ym", [("ym", c, n), ("ymG", m)])
                        if c * NST + n >= 3:
                            p0_drain(8, cap=(sum(NW_T.values()) + 1)
                                     if c == 0 else None)
                        if 1 <= c <= NCH:
                            ps = 99 if c == 1 else 2
                            if n >= ps and 4 * (n - ps) < NPAIR:
                                for q in range(4):
                                    hstep(v, vstep, cf, 4 * (n - ps) + q, c)
                    if c == 1:
                        for idx in range(NPAIR):
                            hstep(v, vstep, cf, idx, c)
                elif 1 <= c <= NCH:
                    for idx in range(NPAIR):
                        hstep(v, vstep, cf, idx, c)
                if 1 <= c <= NCH:
                    ctx.wait(v, "t_d", "dmm", cf, 0)
                    for sj3 in range(STP):
                        st3 = sj3
                        for dh in range(2):
                            def fd(sj3=sj3, st3=st3, dh=dh, cf=cf):
                                sl = r2_v[:, st3][:, dh * 512:(dh + 1) * 512]
                                return v.scalar_tensor_tensor(
                                    out=sl, in0=pd[sj3 * 2 + dh][:],
                                    scalar=qcol_f(cf, st3), in1=sl,
                                    op0=Op.mult, op1=Op.add)
                            vstep(fd, "v_dev", [("devst", cf, 0, sj3, dh)])

        def hstep(v, vstep, cf, idx, c):
            gi, sj = divmod(idx, STP)
            h, ic = divmod(gi, NIC)
            pidx = cf * NPAIR + idx
            ctx.wait(v, "a_rel", "rel", cf, idx)
            ctx.wait(v, "t_u", "u", cf, idx)
            if cf == 0 and idx == 0:
                ctx.wait(v, None, "pA", "d", NW_T["d"] - 1)
            if h == 1:
                ctx.wait_if(v, None, "hq1", cf, sj, ic * ICW // 512)
            else:
                ctx.wait_if(v, None, "hq1", cf - 1, STP + sj, ic * ICW // 512)
            rv = r_rv[:, idx % 2]

            def fh():
                return v.tensor_tensor(
                    out=hf_v[:, sj][:, ic * ICW:(ic + 1) * ICW],
                    in0=rv, in1=pu[:], op=Op.mult)
            vstep(fh, "v_h", [("h", cf, idx), ("hG", pidx)])

            def fm():
                return v.tensor_reduce(
                    out=maxp_v[:, sj][:, ic:ic + 1],
                    in_=hf_v[:, sj][:, ic * ICW:(ic + 1) * ICW], axis=AX.X,
                    op=Op.max, apply_absolute_value=True)
            vstep(fm)
            if ic == NIC - 1 and sj == STP - 1:
                for sj2 in range(STP):
                    st2 = h * STP + sj2
                    j = st2
                    vstep(lambda sj2=sj2: v.tensor_reduce(
                        out=lncol[:, 6:7], in_=maxp_v[:, sj2], axis=AX.X,
                        op=Op.max))
                    vstep(lambda j=j: v.reciprocal(out=lncol[:, 7:8],
                                                   in_=l2(j, 7)))
                    vstep(lambda: v.tensor_scalar(
                        out=lncol[:, 7:8], in0=lncol[:, 7:8], scalar1=EPS_Q,
                        scalar2=0.0, op0=Op.mult, op1=Op.add))
                    vstep(lambda: v.tensor_tensor(
                        out=lncol[:, 6:7], in0=lncol[:, 6:7],
                        in1=lncol[:, 7:8], op=Op.max))
                    vstep(lambda cf=cf, st2=st2: v.reciprocal(
                        out=qcol_s(cf, st2), in_=lncol[:, 6:7]))
                    vstep(lambda cf=cf, st2=st2: v.tensor_scalar(
                        out=qcol_s(cf, st2), in0=qcol_s(cf, st2), scalar1=127.0,
                        scalar2=0.0, op0=Op.mult, op1=Op.add),
                        labels=[("qcol", cf, st2)])
                    vstep(lambda cf=cf, st2=st2, j=j: v.tensor_tensor(
                        out=qcol_f(cf, st2), in0=l2(j, 7), in1=lncol[:, 6:7],
                        op=Op.mult))
                    vstep(lambda cf=cf, st2=st2: v.tensor_tensor(
                        out=qcol_f(cf, st2), in0=qcol_f(cf, st2), in1=mwc[:, 3:4],
                        op=Op.mult))
                    vstep(lambda cf=cf, st2=st2: v.tensor_scalar(
                        out=qcol_f(cf, st2), in0=qcol_f(cf, st2),
                        scalar1=1.0 / 127.0, scalar2=0.0, op0=Op.mult, op1=Op.add))



        ctx.start(False, sems)
        emit_dve(None)
        emit_act(None)
        emit_pe(None)
        emit_pool(None)
        emit_sync(None)

        ctx.start(True, sems)

        @block.vector
        def _(vector):
            emit_dve(vector)

        @block.scalar
        def _(scalar):
            emit_act(scalar)

        @block.tensor
        def _(tensor):
            emit_pe(tensor)

        @block.gpsimd
        def _(gpsimd):
            emit_pool(gpsimd)

        @block.sync
        def _(sync):
            emit_sync(sync)

    return nc


def _prep_inputs(A_log, W_dt, b_dt, W_xp, b_xp, W_out, b_out, w_gate, w_up,
                 w_down):
    import ml_dtypes
    f32 = np.float32
    d = {}
    d["wgT"] = np.ascontiguousarray(np.asarray(w_gate, f32).T).reshape(NDG, 128, NI)
    d["wuT"] = np.ascontiguousarray(np.asarray(w_up, f32).T).reshape(NDG, 128, NI)
    d["wdT"] = np.ascontiguousarray(np.asarray(w_down, f32).T).reshape(NKI, 128, D)
    d["woT"] = np.ascontiguousarray(np.asarray(W_out, f32).T).reshape(NDG, 128, D)
    d["wxpT"] = np.ascontiguousarray(
        np.asarray(W_xp, f32).T.reshape(NDG, 128, 32).transpose(1, 0, 2)
        .reshape(128, NDG * 32))
    d["alog"] = np.ascontiguousarray(
        np.asarray(A_log, f32).reshape(NDG, 128, NST).transpose(1, 0, 2)
        .reshape(128, NDG * NST))
    d["wdtr"] = np.ascontiguousarray(np.asarray(W_dt, f32)[:, 0][None, :])
    d["bdtr"] = np.ascontiguousarray(np.asarray(b_dt, f32)[None, :])
    d["bxpr"] = np.ascontiguousarray(np.asarray(b_xp, f32)[None, :])
    d["boutr"] = np.ascontiguousarray(np.asarray(b_out, f32)[None, :])
    oh = np.zeros((32, 32, 128), ml_dtypes.bfloat16)
    for n in range(32):
        oh[n, n, :] = 1
    d["oh32"] = np.ascontiguousarray(oh.transpose(1, 0, 2).reshape(32, 32 * 128))
    d["ones1"] = np.ones((1, TT), ml_dtypes.bfloat16)
    d["o128"] = np.ones((128, 128), np.float32)
    return d


_NC_CACHE = {}


def kernel(L0_attn_x, L0_attn_A_log, L0_attn_W_dt, L0_attn_b_dt, L0_attn_W_xp,
           L0_attn_b_xp, L0_attn_W_out, L0_attn_b_out, L0_ffn_w_gate,
           L0_ffn_w_up, L0_ffn_w_down):
    x = np.asarray(L0_attn_x, np.float32)
    B, T = x.shape[0], x.shape[1]
    if T not in _NC_CACHE:
        _NC_CACHE[T] = build_program(T)
    nc = _NC_CACHE[T]
    shared = _prep_inputs(L0_attn_A_log, L0_attn_W_dt, L0_attn_b_dt, L0_attn_W_xp,
                          L0_attn_b_xp, L0_attn_W_out, L0_attn_b_out,
                          L0_ffn_w_gate, L0_ffn_w_up, L0_ffn_w_down)
    in_maps = []
    for b in range(B):
        m = dict(shared)
        m["x"] = np.ascontiguousarray(x[b])
        in_maps.append(m)
    res = run_bass_kernel_spmd(nc, in_maps, core_ids=list(range(B)))
    return np.stack([np.asarray(r["out"]) for r in res.results], axis=0)


# version 17
# speedup vs baseline: 1.0809x; 1.0809x over previous
import contextlib
import numpy as np
import concourse.bass as bass
import concourse.mybir as mybir
from concourse.bass_utils import run_bass_kernel_spmd
from concourse.alu_op_type import AluOpType as Op

F32 = mybir.dt.float32
BF16 = mybir.dt.bfloat16
AF = mybir.ActivationFunctionType
AX = mybir.AxisListType

MAGIC = float(np.float32(3 * 2**22))
EPS_LN = 1e-5
EPS_Q = 1e-5

D = 1024
NI = 4096
NST = 16
TT = 512
NDG = D // 128
NTJ = TT // 128
NKI = NI // 128
ICW = 512
NIC = NI // ICW
NHALF = 2
STP = NTJ // NHALF
NGI = NHALF * NIC
NPAIR = NGI * STP
PPN = NPAIR // NST
NQ = NI // 512


class Ctx:

    def __init__(self):
        self.lab = {}

    def start(self, pass2, sems=None):
        self.pass2 = pass2
        self.cnt = {}
        self.sems = sems
        self.last = {}

    def inc(self, instr, sem, by=1, labels=(), tag=None):
        c = self.cnt.get(sem, 0) + by
        self.cnt[sem] = c
        if not self.pass2:
            for lb in labels:
                self.lab[tuple(lb)] = (sem, c)
        if self.pass2 and instr is not None:
            instr.then_inc(self.sems[sem], by)
        if tag is not None:
            self.last[tag] = (sem, c)
        return c

    def mark(self, _sem_unused, *label):
        raise RuntimeError("mark replaced by inc labels")

    def wait(self, eng, _sem_unused, *label):
        if not self.pass2:
            return
        ent = self.lab.get(tuple(label))
        assert ent is not None, f"missing label {tuple(label)}"
        sem, v = ent
        if v > 0:
            eng.wait_ge(self.sems[sem], v)

    def wait_if(self, eng, _sem_unused, *label):
        if tuple(label) in self.lab:
            self.wait(eng, None, *label)


def build_program(T):
    NCH = T // TT
    SEG = NCH + 2
    nc = bass.Bass("TRN2", target_bir_lowering=False)

    def din(name, shape, dt=F32):
        return nc.dram_tensor(name, shape, dt, kind="ExternalInput").ap()

    x_h = din("x", [T, D])
    wgT_h = din("wgT", [NDG, 128, NI])
    wuT_h = din("wuT", [NDG, 128, NI])
    wdT_h = din("wdT", [NKI, 128, D])
    woT_h = din("woT", [NDG, 128, D])
    wxpT_h = din("wxpT", [128, NDG * 32])
    alog_h = din("alog", [128, NDG * NST])
    wdt_h = din("wdtr", [1, D])
    bdt_h = din("bdtr", [1, D])
    bxp_h = din("bxpr", [1, 32])
    bout_h = din("boutr", [1, D])
    oh_h = din("oh32", [32, 32 * 128], BF16)
    ones_h = din("ones1", [1, TT], BF16)
    o128_h = din("o128", [128, 128])
    out_h = nc.dram_tensor("out", [T, D], F32, kind="ExternalOutput").ap()

    wgq_s = nc.dram_tensor("wgq_s", [NIC, NDG, 128, ICW], BF16, kind="Internal").ap()
    wuq_s = nc.dram_tensor("wuq_s", [NIC, NDG, 128, ICW], BF16, kind="Internal").ap()
    wdq_s = nc.dram_tensor("wdq_s", [NKI, 128, D], BF16, kind="Internal").ap()
    woq_s = nc.dram_tensor("woq_s", [NDG, 128, D], BF16, kind="Internal").ap()

    ctx = Ctx()
    es = contextlib.ExitStack()
    with es:
        def SB(name, p, f, dt):
            return es.enter_context(nc.sbuf_tensor(name, [p, f], dt))

        def PS(name, p, f):
            return es.enter_context(nc.psum_tensor(name, [p, f], F32))

        woT_r = SB("woT_r", 128, 4 * D, BF16)
        wxpT_b = SB("wxpT_b", 128, NDG * 32, BF16)
        wdt_b = SB("wdt_b", 1, D, BF16)
        bdt_b = SB("bdt_b", 1, D, BF16)
        bxp_b = SB("bxp_b", 1, 32, BF16)
        bout_b = SB("bout_b", 1, D, BF16)
        oh32 = SB("oh32_t", 32, 32 * 128, BF16)
        ones1 = SB("ones1_t", 1, TT, BF16)
        o128 = SB("o128_t", 128, 128, F32)
        negA = SB("negA", 128, NDG * NST, F32)
        ccol = SB("ccol", 128, 4, F32)
        mwc = SB("mwc", 128, 8, F32)
        wsum = SB("wsum", 128, 4, F32)

        xl_r = SB("xl_r", 128, 2 * 512, F32)
        xr_r = SB("xr_r", 128, 2 * 512, F32)
        r2 = SB("r2", 128, NTJ * D, BF16)
        tok = SB("tok", 128, NTJ * D, BF16)
        xlnT = SB("xlnT", 128, NDG * TT, BF16)
        xqT = SB("xqT", 128, NDG * TT, BF16)
        dt_all = SB("dt_all", 128, NDG * TT, BF16)
        u_all = SB("u_all", 128, NDG * TT, BF16)
        bct = SB("bct", 32, TT, BF16)
        bcB = SB("bcB", 128, 2 * TT, BF16)
        bcC = SB("bcC", 128, 2 * TT, BF16)
        dA_r = SB("dA_r", 128, 2 * TT, F32)
        dB_all = SB("dB_all", 128, NDG * TT, BF16)
        yp_all = SB("yp_all", 128, NDG * TT, BF16)
        y_acc = SB("y_acc", 128, NDG * TT, BF16)
        yT_b = SB("yT_b", 128, NDG * TT, BF16)
        carr = SB("carr", 128, 2 * 128, BF16)
        sp1 = SB("sp1", 128, TT, F32)
        sp2 = SB("sp2", 128, TT, BF16)
        lncol = SB("lncol", 128, 64, F32)
        bns = SB("bns", 128, 12, F32)
        t1q = SB("t1q", 128, 512, F32)
        r_r = SB("r_r", 128, 2 * ICW, F32)
        h_half = SB("h_half", 128, STP * NI, BF16)
        maxp = SB("maxp", 128, STP * NIC, F32)
        qcol = SB("qcol", 128, 16, F32)
        hq_b = SB("hq_b", 128, NI, BF16)
        hqT = SB("hqT", 128, NKI * STP * 128, BF16)
        wg_r = SB("wg_r", 128, 2 * NDG * ICW, BF16)
        wu_r = SB("wu_r", 128, 2 * NDG * ICW, BF16)
        wd_r = SB("wd_r", 128, 2 * D, BF16)

        gbufA = h_half[:].bitcast(F32)
        gbufB = hqT[:].bitcast(F32)
        qout = y_acc[:]

        pg = PS("pg", 128, ICW)
        pu = PS("pu", 128, ICW)
        pd = [PS(f"pd{i}", 128, 512) for i in range(4)]
        pr = [PS(f"pr{i}", 128, 512) for i in range(2)]

        sem_names = [
            "s_c0", "s_pa0", "s_pa1", "s_wo0", "s_wo1", "s_pb0", "s_pb1",
            "s_pb2", "s_pb3", "s_st", "s_st1", "s_st2", "s_st3",
            "s_xl0", "s_xl1", "s_xr0", "s_xr1", "s_tl", "s_tq", "s_th0", "s_th1",
            "s_wg0", "s_wg1", "s_wu0", "s_wu1", "s_wd0", "s_wd1", "s_wd2",
            "s_tq1", "s_wl0", "s_wl1", "s_wl2", "s_wl3", "s_ws0", "s_ws1", "s_ws2", "s_ws3",
            "s_out",
            "t_p0", "t_wout", "t_bc", "t_dtz", "t_bbc", "t_g", "t_u",
            "t_d",
            "a_s", "a_rstd", "a_sp", "a_spp", "a_bce", "s_dA", "a_rel", "a_hq2",
            "v_s", "v_ln", "v_xln", "v_r2", "v_xq", "v_bce", "v_dt", "v_u",
            "v_dB", "v_scan", "v_ym", "v_h", "v_hq1", "v_dev",
            "p_s", "p_ya", "p_hh",
        ]
        sems = {n: es.enter_context(nc.semaphore(n)) for n in sem_names}
        block = es.enter_context(nc.Block())

        xlnT_v = xlnT[:].rearrange("p (k j t) -> p k j t", k=NDG, j=NTJ)
        xqT_v = xqT[:].rearrange("p (k j t) -> p k j t", k=NDG, j=NTJ)
        dt_v = dt_all[:].rearrange("p (k t) -> p k t", k=NDG)
        dB_v = dB_all[:].rearrange("p (k t) -> p k t", k=NDG)
        u_v = u_all[:].rearrange("p (k t) -> p k t", k=NDG)
        yp_v = yp_all[:].rearrange("p (k t) -> p k t", k=NDG)
        ya_v = y_acc[:].rearrange("p (k j t) -> p k j t", k=NDG, j=NTJ)
        yT_v = yT_b[:].rearrange("p (k j t) -> p k j t", k=NDG, j=NTJ)
        r2_v = r2[:].rearrange("p (j d) -> p j d", j=NTJ)
        tok_v = tok[:].rearrange("p (j d) -> p j d", j=NTJ)
        woT_rv = woT_r[:].rearrange("p (r o) -> p r o", r=4)
        wxpT_v = wxpT_b[:].rearrange("p (k n) -> p k n", k=NDG)
        oh_v = oh32[:].rearrange("q (n p) -> q n p", n=32)
        dA_v = dA_r[:].rearrange("p (r t) -> p r t", r=2)
        bcB_v = bcB[:].rearrange("p (r t) -> p r t", r=2)
        bcC_v = bcC[:].rearrange("p (r t) -> p r t", r=2)
        carr_v = carr[:].rearrange("p (r i) -> p r i", r=2)
        xl_v = xl_r[:].rearrange("p (r d) -> p r d", r=2)
        xr_v = xr_r[:].rearrange("p (r d) -> p r d", r=2)
        r_rv = r_r[:].rearrange("p (r t) -> p r t", r=2)
        hf_v = h_half[:].rearrange("p (s q) -> p s q", s=STP)
        maxp_v = maxp[:].rearrange("p (s i) -> p s i", s=STP)
        hqT_v = hqT[:].rearrange("p (k s t) -> p k s t", k=NKI, s=STP)
        def hqslot(st):
            return hq_b[:] if st % 2 == 0 else xlnT[:]
        wg_rv = wg_r[:].rearrange("p (r k t) -> p r k t", r=2, k=NDG)
        wu_rv = wu_r[:].rearrange("p (r k t) -> p r k t", r=2, k=NDG)
        wd_rv = wd_r[:].rearrange("p (r d) -> p r d", r=2)

        def qcol_s(c, st):
            o = (c % 2) * 8 + st * 2
            return qcol[:, o:o + 1]

        def qcol_f(c, st):
            o = (c % 2) * 8 + st * 2 + 1
            return qcol[:, o:o + 1]

        def l1(j, i):
            return lncol[:, j * 8 + i:j * 8 + i + 1]

        def l2(j, i):
            return lncol[:, 32 + j * 8 + i:32 + j * 8 + i + 1]

        NW_T = {"g": NDG, "u": NDG, "d": NKI}
        NW_FD = {"g": NI, "u": NI, "d": D}
        WSRC = {"g": wgT_h, "u": wuT_h, "d": wdT_h}
        WSCR = {"g": wgq_s, "u": wuq_s, "d": wdq_s}
        WCOL = {"g": 0, "u": 1, "d": 2}

        _PBPREV = []
        for _w in ("g", "u", "d"):
            for _i in range(NW_T[_w]):
                _PBPREV.append((_w, _i))

        def gb(i):
            return gbufA if i % 2 == 0 else gbufB

        xt_h = x_h.rearrange("(c j p) d -> c j p d", c=NCH, j=NTJ)

        def emit_sync(sync):
            p2 = ctx.pass2

            def dma(out, in_, sem, labels=(), transpose=False):
                i = sync.dma_start(out=out, in_=in_, transpose=transpose) if p2 else None
                ctx.inc(i, sem, 16, labels)

            dma(o128[:], o128_h, "s_c0")
            dma(oh32[:], oh_h, "s_c0")
            dma(ones1[:], ones_h, "s_c0")
            dma(gbufA[:, 0:NDG * 32], wxpT_h, "s_c0")
            dma(gbufA[:, 512:512 + NDG * NST], alog_h, "s_c0")
            dma(gbufB[0:1, 0:D], wdt_h, "s_c0")
            dma(gbufB[0:1, D:2 * D], bdt_h, "s_c0")
            dma(gbufB[0:1, 2 * D:2 * D + 32], bxp_h, "s_c0")
            dma(gbufB[0:1, 3 * D:4 * D], bout_h, "s_c0", labels=[("c0",)])


            for c in range(SEG):
                cf = c - 1
                if c < NCH:
                    for j in range(NTJ):
                        ctx.wait(sync, "v_xln", "xln", c, j)
                        if c > 0 and j == 0:
                            ctx.wait(sync, "v_u", "u", c - 1)
                            ctx.wait(sync, "t_dtz", "dtz", c - 1, NDG - 1)
                            if c >= 2:
                                ctx.wait_if(sync, None, "th", c - 2, 0, 1)
                                ctx.wait_if(sync, None, "th", c - 2, 1, 1)
                        src = tok_v[:, j].rearrange("p (k t) -> p k t", k=NDG)
                        dma(xlnT_v[:, :, j], src, "s_tl", labels=[("tl", c, j)],
                            transpose=True)
                if 1 <= c <= NCH:
                    for j in range(NTJ):
                        ctx.wait(sync, "v_xq", "xq", cf, j)
                        if cf > 0 and j == 0:
                            ctx.wait(sync, "t_u", "gu", cf - 1, NGI - 1)
                        src = tok_v[:, j].rearrange("p (k t) -> p k t", k=NDG)
                        dma(xqT_v[:, :, j], src,
                            "s_tq" if j < 2 else "s_tq1",
                            labels=[("tq", cf, j)], transpose=True)
                    for gi in range(NGI):
                        ic = gi % NIC
                        gg = cf * NGI + gi
                        ctx.wait_if(sync, "t_u", "guG", gg - 2)
                        if cf == 0 and gi <= 1:
                            for uu in range(4 * NW_T["g"] - 4, 4 * NW_T["g"]):
                                ctx.wait(sync, None, "stu", uu)
                        dma(wg_rv[:, gi % 2], wgq_s[ic].rearrange("k p t -> p k t"),
                            f"s_wg{gi % 2}", labels=[("wg", cf, gi)])
                        if cf == 0 and gi <= 1:
                            for uu in range(8 * NW_T["g"] - 4, 8 * NW_T["g"]):
                                ctx.wait(sync, None, "stu", uu)
                        dma(wu_rv[:, gi % 2], wuq_s[ic].rearrange("k p t -> p k t"),
                            f"s_wu{gi % 2}", labels=[("wu", cf, gi)])
                    for h in range(NHALF):
                        for sj in range(STP):
                            st = h * STP + sj
                            sg = cf * NTJ + st
                            ctx.wait(sync, "a_hq2", "hq2", cf, st)
                            if cf == 0:
                                for uu in range(4 * 16 + NW_T["d"] - 4,
                                                4 * 16 + NW_T["d"]):
                                    ctx.wait(sync, None, "stu", uu)
                            if not (cf == 0 and h == 0):
                                ph = (cf, h - 1) if h > 0 else (cf - 1, NHALF - 1)
                                ctx.wait(sync, "t_d", "dmm", ph[0], ph[1])
                            src = hqslot(st).rearrange(
                                "p (k t) -> p k t", k=NKI)
                            dma(hqT_v[:, :, sj], src, f"s_th{st % 2}",
                                labels=[("th", cf, h, sj), ("thg", sg)],
                                transpose=True)
                        for k in range(NKI):
                            kg = (cf * NHALF + h) * NKI + k
                            ctx.wait_if(sync, "t_d", "wdG", kg - 2)
                            if cf == 0:
                                ctx.wait(sync, None, "st", "d", k)
                            dma(wd_rv[:, k % 2], wdq_s[k], f"s_wd{k % 2}",
                                labels=[("wd", cf, h, k)])
            if p2:
                sync.wait_ge(sems["s_out"], ctx.cnt.get("s_out", 0))

        def emit_pool(g):
            p2 = ctx.pass2

            def pstep(f, sem=None, labels=()):
                use = sem or "p_s"
                i = f() if p2 else None
                c = ctx.inc(i, use, 1, labels)
                if p2:
                    g.wait_ge(sems[use], c)

            def dma(out, in_, sem, labels=()):
                i = g.dma_start(out=out, in_=in_) if p2 else None
                ctx.inc(i, sem, 16, labels)

            for hh in range(2):
                dma(xl_v[:, hh], xt_h[0, 0][:, hh * 512:(hh + 1) * 512],
                    f"s_xl{hh}", labels=[("xl", 0, 0, hh)])

            p0ld = []
            p0i2 = 0
            for i in range(NDG):
                def tld(i=i, p0i2=p0i2):
                    if p0i2 == 0:
                        ctx.wait(g, None, "alog_ld")
                        ctx.wait(g, None, "negAexp")
                    if p0i2 == 1:
                        ctx.wait(g, None, "alog_ld")
                    if i >= 2:
                        ctx.wait(g, None, "wos", i - 2)
                    dma(gb(p0i2)[:, 0:D], woT_h[i], f"s_wo{p0i2 % 2}",
                        labels=[("ldwo", i)])
                p0ld.append(tld)
                if i >= 1:
                    def tws(i=i, p0i2=p0i2):
                        ctx.wait(g, None, "ldwo", i - 1)
                        dma(woq_s[i - 1], gb(p0i2 - 1)[:, 0:D],
                            f"s_ws{(i - 1) % 4}", labels=[("wos", i - 1)])
                    p0ld.append(tws)
                p0i2 += 1
            def twsl(p0i2=p0i2):
                ctx.wait(g, None, "ldwo", NDG - 1)
                dma(woq_s[NDG - 1], gb(p0i2 - 1)[:, 0:D],
                    f"s_ws{(NDG - 1) % 4}", labels=[("wos", NDG - 1)])
            p0ld.append(twsl)
            for w in ("g", "u", "d"):
                for i in range(NW_T[w]):
                    def ta2(w=w, i=i, p0i2=p0i2):
                        if p0i2 in (NDG, NDG + 1):
                            ctx.wait(g, None, "wos", p0i2 - 2)
                        ctx.wait_if(g, None, "p0c", p0i2 - 2)
                        dma(gb(p0i2)[:, 0:NW_FD[w]], WSRC[w][i],
                            f"s_pa{p0i2 % 2}", labels=[("ldA", w, i)])
                    p0ld.append(ta2)
                    p0i2 += 1
            PB_UNITS2 = []
            for w in ("g", "u"):
                for i in range(NW_T[w]):
                    for q in range(4):
                        PB_UNITS2.append((w, i, q))
            for i in range(NW_T["d"]):
                PB_UNITS2.append(("d", i, 0))
            for ug, (w, i, q) in enumerate(PB_UNITS2):
                def tbu2(w=w, i=i, q=q, ug=ug):
                    s = ug % 4
                    if ug >= 4:
                        ctx.wait_if(g, None, "pBu", ug - 4)
                    else:
                        ctx.wait(g, "v_s", "pA", "d", NW_T["d"] - 1)
                    srcu = (WSRC[w][i][:, q * D:(q + 1) * D] if w != "d"
                            else WSRC["d"][i])
                    dma(gbufB[:, s * D:(s + 1) * D], srcu,
                        f"s_pb{ug % 4}", labels=[("ldU", ug)])
                p0ld.append(tbu2)

            n_p0ld = len(p0ld)

            def p0ld_drain(kmax, cap=None):
                k = 0
                while p0ld and k < kmax:
                    if cap is not None and n_p0ld - len(p0ld) >= cap:
                        return
                    p0ld.pop(0)()
                    k += 1

            for c in range(SEG):
                cf = c - 1
                if c < NCH:
                    for j in range(1, NTJ):
                        for hh in range(2):
                            hg = (c * NTJ + j) * 2 + hh
                            ctx.wait_if(g, "v_ln", "xlh", hg - 2)
                            dma(xl_v[:, hh],
                                xt_h[c, j][:, hh * 512:(hh + 1) * 512],
                                f"s_xl{hh}", labels=[("xl", c, j, hh)])
                if c + 1 < NCH:
                    for hh in range(2):
                        hg = ((c + 1) * NTJ) * 2 + hh
                        ctx.wait_if(g, "v_ln", "xlh", hg - 2)
                        dma(xl_v[:, hh],
                            xt_h[c + 1, 0][:, hh * 512:(hh + 1) * 512],
                            f"s_xl{hh}", labels=[("xl", c + 1, 0, hh)])
                if 2 <= c <= NCH + 1:
                    ctx.wait(g, "v_dev", "dev", c - 2)
                    ov = out_h.rearrange("(c j p) d -> c j p d", c=NCH, j=NTJ)[c - 2]
                    for j in range(NTJ):
                        dma(ov[j], r2_v[:, j], "s_out",
                            labels=[("out", c - 2)] if j == NTJ - 1 else ())
                if 1 <= c <= NCH:
                    for j in range(NTJ):
                        for oh in range(2):
                            k = cf * NTJ * 2 + j * 2 + oh
                            ctx.wait_if(g, "v_r2", "r2G", k - 2)
                            dma(xr_v[:, (j * 2 + oh) % 2],
                                xt_h[cf, j][:, oh * 512:(oh + 1) * 512],
                                f"s_xr{(j * 2 + oh) % 2}",
                                labels=[("xr", cf, j, oh)])
                if c == 2:
                    p0ld_drain(10 ** 9)
                if c < NCH:
                    for n in range(NST):
                        ctx.wait(g, "v_ym", "ym", c, n)
                        if n == 0:
                            if c > 0:
                                ctx.wait(g, None, "yT", c - 1)
                            pstep(lambda: g.tensor_copy(out=y_acc[:], in_=yp_all[:]),
                                  "p_ya", [("ya", c, n)])
                        else:
                            pstep(lambda: g.tensor_tensor(
                                out=y_acc[:], in0=y_acc[:], in1=yp_all[:], op=Op.add),
                                "p_ya", [("ya", c, n)])
                        p0ld_drain(8, cap=(2 * NDG + sum(NW_T.values()) + 4)
                                   if c == 0 else None)
                if c < NCH:
                    if c >= 1:
                        ctx.wait(g, "t_wout", "wout", c - 1, NTJ - 1, 1)
                    pstep(lambda: g.tensor_copy(out=yT_b[:], in_=y_acc[:]),
                          "p_ya", [("yT", c)])


        def emit_pe(t):
            p2 = ctx.pass2

            def mm(out, lhsT, rhs, start, stop, sem=None, labels=()):
                i = t.matmul(out=out, lhsT=lhsT, rhs=rhs, start=start, stop=stop) if p2 else None
                if sem:
                    ctx.inc(i, sem, 1, labels)

            for c in range(SEG):
                cf = c - 1
                if c < NCH:
                    ctx.wait(t, "s_tl", "tl", c, NTJ - 1)
                    if c >= 1:
                        ctx.wait(t, "a_bce", "bceB", c - 1, NST - 1)
                    for dg in range(NDG):
                        mm(pr[0][0:32, 0:TT], wxpT_v[:, dg],
                           xlnT_v[:, dg].rearrange("p j t -> p (j t)"), dg == 0, False)
                    mm(pr[0][0:32, 0:TT], bxp_b[:], ones1[:], False, True,
                       "t_bc", [("bc", c)])
                    for dg in range(NDG):
                        pp = pr[dg % 2]
                        if dg == 0:
                            ctx.wait(t, "v_bce", "bce", c)
                        elif dg == 1:
                            if c >= 1:
                                ctx.wait(t, "a_bce", "bceC", c - 1, NST - 1)
                        else:
                            ctx.wait(t, "a_spp", "spp", c, dg - 2)
                        mm(pp[:, 0:TT], wdt_b[0:1, dg * 128:(dg + 1) * 128],
                           xlnT[0:1, 0:TT], True, False)
                        mm(pp[:, 0:TT], bdt_b[0:1, dg * 128:(dg + 1) * 128],
                           ones1[:], False, True, "t_dtz", [("dtz", c, dg)])

                def bbc_pair(n, c=c):
                    if n == 0:
                        ctx.wait(t, "a_spp", "spp", c, NDG - 2)
                        ctx.wait(t, "v_bce", "bce", c)
                    else:
                        ctx.wait(t, "a_bce", "bceB", c, n - 1)
                    mm(pr[0][:, 0:TT], oh_v[:, n], bct[:], True, True,
                       "t_bbc", [("bbcB", c, n)])
                    if n == 0:
                        ctx.wait(t, "a_spp", "spp", c, NDG - 1)
                    else:
                        ctx.wait(t, "a_bce", "bceC", c, n - 1)
                    mm(pr[1][:, 0:TT], oh_v[:, NST + n], bct[:], True, True,
                       "t_bbc", [("bbcC", c, n)])

                if c < NCH:
                    bbc_pair(0)
                    if c == 0:
                        for n in range(1, NST):
                            bbc_pair(n)
                        ctx.wait(t, "v_s", "partials")
                        mm(pd[0][:, 0:4], o128[:], wsum[:, 0:4], True, True,
                           "t_p0", [("tot",)])
                if 1 <= c <= NCH:
                    ctx.wait(t, None, "yT", cf)
                    for rnd in range(2):
                        for dg in range(NDG):
                            wog = (cf * 2 + rnd) * NDG + dg
                            ctx.wait(t, None, "wol", cf, rnd, dg)
                            for b in range(4):
                                k = rnd * 4 + b
                                j, oh = k // 2, k % 2
                                if dg == 0:
                                    if rnd == 0:
                                        if c >= 2:
                                            ctx.wait(t, "v_dev", "devst",
                                                     c - 2, 1, b // 2, b % 2)
                                        elif b == 0:
                                            ctx.wait(t, "v_s", "cols")
                                    else:
                                        ctx.wait(t, "v_r2", "r2",
                                                 cf, b // 2, b % 2)
                                mm(pd[b][:], yT_v[:, dg, j],
                                   woT_rv[:, wog % 4, oh * 512:(oh + 1) * 512],
                                   dg == 0, False,
                                   "t_wout" if b == 3 else None,
                                   [("wocG", wog)])
                        for b in range(4):
                            k = rnd * 4 + b
                            j, oh = k // 2, k % 2
                            mm(pd[b][:], ones1[0:1, 0:128],
                               bout_b[0:1, oh * 512:(oh + 1) * 512], False, True,
                               "t_wout", [("wout", cf, j, oh)])
                if 1 <= c < NCH:
                    bbc_pair(1)
                    bbc_pair(2)
                    if c == 1:
                        for m in range(3, NST):
                            bbc_pair(m)
                for gi in range(NGI):
                    h, ic = divmod(gi, NIC)
                    if 1 <= c <= NCH and gi == 0:
                        ctx.wait(t, "s_tq", "tq", cf, 1)
                    if 1 <= c <= NCH and gi == NIC:
                        ctx.wait(t, "s_tq1", "tq", cf, 3)
                    if 1 <= c <= NCH:
                        gg = cf * NGI + gi
                        ctx.wait(t, None, "wg", cf, gi)
                        ctx.wait(t, None, "wu", cf, gi)
                        for sj in range(STP):
                            st = h * STP + sj
                            idx = gi * STP + sj
                            pidx = cf * NPAIR + idx
                            ctx.wait_if(t, "a_rel", "relG", pidx - 1)
                            for dg in range(NDG):
                                mm(pg[:], xqT_v[:, dg, st], wg_rv[:, gi % 2, dg],
                                   dg == 0, dg == NDG - 1,
                                   "t_g" if dg == NDG - 1 else None,
                                   [("g", cf, idx)])
                            ctx.wait_if(t, "v_h", "hG", pidx - 1)
                            for dg in range(NDG):
                                labs = [("u", cf, idx)]
                                if sj == STP - 1:
                                    labs += [("gu", cf, gi), ("guG", gg)]
                                mm(pu[:], xqT_v[:, dg, st], wu_rv[:, gi % 2, dg],
                                   dg == 0, dg == NDG - 1,
                                   "t_u" if dg == NDG - 1 else None, labs)
                    if 2 <= c < NCH and gi % 2 == 1:
                        bbc_pair((gi + 5) // 2)
                if 1 <= c <= NCH:
                    m0 = (NGI - 1 + 5) // 2 + 1
                    for h in range(NHALF):
                        ctx.wait(t, None, "th", cf, h, 0)
                        ctx.wait(t, None, "th", cf, h, STP - 1)
                        for k in range(NKI):
                            if (h == 0 and 2 <= c < NCH and k % 6 == 4
                                    and m0 + (k - 4) // 6 < NST):
                                bbc_pair(m0 + (k - 4) // 6)
                            ctx.wait(t, "s_wd", "wd", cf, h, k)
                            kg = (cf * NHALF + h) * NKI + k
                            for sj in range(STP):
                                for dh in range(2):
                                    pp = pd[sj * 2 + dh]
                                    if k == 0:
                                        if h == 0:
                                            ctx.wait(t, "v_r2", "r2",
                                                     cf, 2 + sj, dh)
                                        else:
                                            ctx.wait(t, "v_dev", "devst",
                                                     cf, 0, sj, dh)
                                    labs = [("wdG", kg)]
                                    if k == NKI - 1:
                                        labs.append(("dmm", cf, h))
                                    mm(pp[:], hqT_v[:, k, sj],
                                       wd_rv[:, k % 2, dh * 512:(dh + 1) * 512],
                                       k == 0, k == NKI - 1,
                                       "t_d" if (sj == STP - 1 and dh == 1) else None,
                                       labs)
                        if h == 0 and 2 <= c < NCH:
                            for m in range(m0 + 5, NST):
                                bbc_pair(m)

        def emit_act(a):
            p2 = ctx.pass2

            def adma(out, in_, sem, labels=()):
                i = a.dma_start(out=out, in_=in_) if p2 else None
                ctx.inc(i, sem, 16, labels)

            pstq = []
            PB_UNITS3 = []
            for w in ("g", "u"):
                for i in range(NW_T[w]):
                    for q in range(4):
                        PB_UNITS3.append((w, i, q))
            for i in range(NW_T["d"]):
                PB_UNITS3.append(("d", i, 0))
            for ug, (w, i, q) in enumerate(PB_UNITS3):
                def tst(w=w, i=i, q=q, ug=ug):
                    s = ug % 4
                    ctx.wait(a, "v_s", "pBu", ug)
                    srcu = wd_rv[:, ug % 2]
                    sem = "s_st" if ug % 4 == 0 else f"s_st{ug % 4}"
                    labs = [("stu", ug)]
                    if w == "d":
                        labs.append(("st", "d", i))
                        adma(WSCR["d"][i], srcu, sem, labels=labs)
                    else:
                        if q == 3:
                            labs.append(("st", w, i))
                        adma(WSCR[w][2 * q:2 * q + 2, i]
                             .rearrange("ic p t -> p ic t"),
                             srcu.rearrange("p (ic t) -> p ic t", ic=2),
                             sem, labels=labs)
                pstq.append(tst)

            def pst_drain(kmax):
                k = 0
                while pstq and k < kmax:
                    pstq.pop(0)()
                    k += 1

            def wol_round(cf, rnd):
                for dg in range(NDG):
                    wog = (cf * 2 + rnd) * NDG + dg
                    ctx.wait_if(a, None, "wocG", wog - 4)
                    if cf == 0 and rnd == 0:
                        ctx.wait(a, f"s_ws{dg % 4}", "wos", 4 + dg % 4)
                    adma(woT_rv[:, wog % 4], woq_s[dg], f"s_wl{wog % 4}",
                         labels=[("wol", cf, rnd, dg)])

            def astep(f, sem=None, labels=()):
                use = sem or "a_s"
                i = f() if p2 else None
                c = ctx.inc(i, use, 1, labels)
                if p2:
                    a.wait_ge(sems[use], c)

            def act(out, in_, func, bias=None, scale=None):
                kw = {}
                if bias is not None:
                    kw["bias"] = bias
                if scale is not None:
                    kw["scale"] = scale
                return a.activation(out=out, in_=in_, func=func, **kw)

            def relu_step(cf, idx):
                pidx = cf * NPAIR + idx
                ctx.wait(a, "t_g", "g", cf, idx)
                ctx.wait_if(a, "v_h", "hG", pidx - 2)
                astep(lambda idx=idx: act(r_rv[:, idx % 2], pg[:], AF.Relu),
                      None, [("relG0", pidx)])
                astep(lambda idx=idx: act(r_rv[:, idx % 2], r_rv[:, idx % 2],
                                          AF.Square),
                      "a_rel", [("rel", cf, idx), ("relG", pidx)])

            act_queue = []
            act_slot = [0]

            def hq2_drain(kmax):
                k = 0
                while act_queue and k < kmax:
                    cfq, st, sj, q = act_queue.pop(0)
                    qg = act_slot[0]
                    act_slot[0] += 1
                    h = st // STP
                    lastpiece = 2 * (h * NIC + (q * 512 + 511) // ICW) + sj
                    ctx.wait(a, "v_h", "h", cfq, lastpiece)
                    ctx.wait(a, None, "qcol", cfq, st)
                    if q == 0:
                        ctx.wait(a, None, "xq", cfq, NTJ - 1)
                    astep(lambda st=st, q=q, qg=qg, cfq=cfq, sj=sj: act(
                        t1q[:],
                        hf_v[:, sj][:, q * 512:(q + 1) * 512],
                        AF.Identity, scale=qcol_s(cfq, st), bias=ccol[:, 3:4]),
                          None, [("hq1", cfq, st, q)])
                    if q == 0:
                        sg = cfq * NTJ + st
                        ctx.wait_if(a, "s_th", "thg", sg - 2)
                    if st % 2 == 1 and q == 0:
                        ctx.wait_if(a, None, "u", cfq + 1)
                        ctx.wait_if(a, None, "dtz", cfq + 1, NDG - 1)
                    labels = []
                    if q == NQ - 1:
                        labels.append(("hq2", cfq, st))
                    astep(lambda st=st, q=q, qg=qg: act(
                        hqslot(st)[:, q * 512:(q + 1) * 512],
                        t1q[:], AF.Identity, bias=ccol[:, 2:3]),
                          "a_hq2", labels)
                    k += 1

            ctx.wait(a, "v_s", "alog_ld")
            astep(lambda: act(negA[:], gbufA[:, 512:512 + NDG * NST], AF.Exp),
                  "a_rstd", [("negAexp",)])

            for c in range(SEG):
                cf = c - 1
                if c < NCH:
                    for j in range(NTJ):
                        ctx.wait(a, "v_ln", "ln1", c, j)
                        astep(lambda j=j: act(l1(j, 1), l1(j, 1), AF.Ln,
                                              bias=ccol[:, 0:1]))
                        astep(lambda j=j: act(l1(j, 1), l1(j, 1), AF.Exp,
                                              scale=ccol[:, 1:2]),
                              "a_rstd", [("rstd1", c, j)])
                        if c >= 2:
                            ctx.wait(a, "s_tq", "tq", c - 2, NTJ - 1)
                        jg = c * NTJ + j
                        for hh in range(2):
                            hg = (c * NTJ + j) * 2 + hh
                            astep(lambda j=j, hh=hh: act(
                                t1q[:], xl_v[:, hh], AF.Identity,
                                bias=l1(j, 2)),
                                None, [("xlh", hg)])
                            labs2 = ([("xln", c, j), ("xlnG", jg)]
                                     if hh == 1 else [])
                            astep(lambda j=j, hh=hh: act(
                                tok_v[:, j][:, hh * 512:(hh + 1) * 512],
                                t1q[:], AF.Identity, scale=l1(j, 1)),
                                "a_rstd" if hh == 1 else None, labs2)
                if c < NCH:
                    for dg in range(NDG):
                        pp = pr[dg % 2]
                        ctx.wait(a, "t_dtz", "dtz", c, dg)
                        if dg >= 1:
                            ctx.wait(a, None, "dt", c, dg - 1)
                        elif c >= 1:
                            ctx.wait(a, None, "dt", c - 1, NDG - 1)
                        astep(lambda pp=pp: act(sp1[:], pp[:, 0:TT], AF.Abs))
                        astep(lambda pp=pp: act(sp2[:], pp[:, 0:TT], AF.Relu),
                              "a_spp", [("spp", c, dg)])
                        astep(lambda: act(sp1[:], sp1[:], AF.Exp, scale=-1.0))
                        astep(lambda: act(sp1[:], sp1[:], AF.Ln, bias=1.0),
                              "a_sp", [("sp", c, dg)])
                if c == 2:
                    pst_drain(10 ** 9)
                if 1 <= c <= NCH:
                    wol_round(cf, 0)
                    wol_round(cf, 1)
                if 1 <= c <= NCH:
                    for j in range(NTJ):
                        ctx.wait(a, "v_ln", "ln2", cf, j)
                        astep(lambda j=j: act(l2(j, 1), l2(j, 1), AF.Ln,
                                              bias=ccol[:, 0:1]))
                        astep(lambda j=j: act(l2(j, 1), l2(j, 1), AF.Exp,
                                              scale=ccol[:, 1:2]),
                              "a_rstd", [("rstd2", cf, j)])
                if c < NCH:
                    for n in range(NST):
                        m = c * NST + n
                        ctx.wait(a, "t_bbc", "bbcB", c, n)
                        ctx.wait_if(a, "v_dB", "dBG", m - 2)
                        astep(lambda n=n: act(bcB_v[:, n % 2], pr[0][:, 0:TT],
                                              AF.Copy),
                              "a_bce", [("bceB", c, n)])
                        ctx.wait(a, "t_bbc", "bbcC", c, n)
                        ctx.wait_if(a, "v_ym", "ymG", m - 2)
                        astep(lambda n=n: act(bcC_v[:, n % 2], pr[1][:, 0:TT],
                                              AF.Copy),
                              "a_bce", [("bceC", c, n)])
                        for dg in range(NDG):
                            gidx = m * NDG + dg
                            ctx.wait(a, "v_dt", "dt", c, dg)
                            ctx.wait_if(a, "v_scan", "scani", gidx - 2)
                            astep(lambda n=n, dg=dg, gidx=gidx: act(
                                dA_v[:, gidx % 2], dt_v[:, dg], AF.Exp,
                                scale=negA[:, dg * NST + n:dg * NST + n + 1]),
                                "s_dA", [("dA", c, n, dg)])
                        if c == 1:
                            pst_drain(8)
                        if 1 <= c <= NCH:
                            ps = 99 if c == 1 else 2
                            rs = 99 if c == 1 else 2
                            if n == ps + 4:
                                act_queue.extend(hq_entries(cf, 0))
                            if c >= 2 and n == 10:
                                act_queue.extend(hq_entries(cf, 1))
                            if n >= ps + 4:
                                hq2_drain(4)
                            if n >= rs and 4 * (n - rs) < NPAIR:
                                for q in range(4):
                                    relu_step(cf, 4 * (n - rs) + q)
                elif 1 <= c <= NCH:
                    for idx in range(NPAIR):
                        relu_step(cf, idx)
                        if idx == NPAIR // 2:
                            act_queue.extend(hq_entries(cf, 0))
                        if idx >= NPAIR // 2:
                            hq2_drain(1)
                if c == 1:
                    pst_drain(10 ** 9)
                    for idx in range(NPAIR):
                        relu_step(cf, idx)
                        if idx == NPAIR // 2:
                            act_queue.extend(hq_entries(cf, 0))
                        if idx >= NPAIR // 2:
                            hq2_drain(1)
                if 1 <= c <= NCH:
                    hq2_drain(10 ** 9)
                    if c == 1 or c == NCH:
                        act_queue.extend(hq_entries(cf, 1))
                        hq2_drain(10 ** 9)

        def hq_entries(cf, h):
            out = []
            for q in range(NQ):
                for sj in range(STP):
                    out.append((cf, h * STP + sj, sj, q))
            return out


        def vstep_factory(v):
            def vstep(f, sem=None, labels=()):
                use = sem or "v_s"
                i = f() if ctx.pass2 else None
                c = ctx.inc(i, use, 1, labels)
                if ctx.pass2:
                    v.wait_ge(sems[use], c)
            return vstep

        def emit_dve(v):
            p2 = ctx.pass2
            vstep = vstep_factory(v)

            def ts(out, in0, s1, s2, op0, op1):
                return v.tensor_scalar(out=out, in0=in0, scalar1=s1, scalar2=s2,
                                       op0=op0, op1=op1)

            ctx.wait(v, "s_scr", "c0")
            vstep(lambda: v.memset(ccol[:, 0:1], EPS_LN))
            vstep(lambda: v.memset(ccol[:, 1:2], -0.5))
            vstep(lambda: v.memset(ccol[:, 2:3], -MAGIC))
            vstep(lambda: v.memset(ccol[:, 3:4], MAGIC))
            vstep(lambda: v.memset(carr_v[:, 1], 0.0))
            vstep(lambda: v.memset(wsum[:], 0.0))
            vstep(lambda: v.tensor_copy(out=wxpT_b[:], in_=gbufA[:, 0:NDG * 32]))
            vstep(lambda: v.tensor_copy(out=wdt_b[:], in_=gbufB[0:1, 0:D]))
            vstep(lambda: v.tensor_copy(out=bdt_b[:], in_=gbufB[0:1, D:2 * D]))
            vstep(lambda: v.tensor_copy(out=bxp_b[:], in_=gbufB[0:1, 2 * D:2 * D + 32]))
            vstep(lambda: v.tensor_copy(out=bout_b[:], in_=gbufB[0:1, 3 * D:4 * D]),
                  labels=[("alog_ld",)])
            ctx.wait(v, "a_rstd", "negAexp")
            vstep(lambda: ts(negA[:], negA[:], -1.0, 0.0, Op.mult, Op.add))
            p0q = []
            p0i = NDG
            for w in ("g", "u", "d"):
                col = WCOL[w]
                for i in range(NW_T[w]):
                    def ta(w=w, i=i, p0i=p0i, col=col):
                        ctx.wait(v, None, "ldA", w, i)
                        vstep(lambda: v.tensor_reduce(
                            out=wsum[:, 3:4], in_=gb(p0i)[:, 0:NW_FD[w]],
                            axis=AX.X, op=Op.add, apply_absolute_value=True),
                            labels=[("p0c", p0i)])
                        vstep(lambda: v.tensor_tensor(
                            out=wsum[:, col:col + 1], in0=wsum[:, col:col + 1],
                            in1=wsum[:, 3:4], op=Op.add), labels=[("pA", w, i)])
                    p0q.append(ta)
                    p0i += 1

            def tpart():
                vstep(lambda: v.memset(wsum[:, 3:4], 0.0),
                      labels=[("partials",)])
            p0q.append(tpart)
            NG = float(NDG * 128 * NI)
            NDW = float(NKI * 128 * D)

            def tmeans():
                ctx.wait(v, "t_p0", "tot")
                vstep(lambda: ts(mwc[:, 5:6], pd[0][:, 0:1], 1.0 / NG, EPS_Q,
                                 Op.mult, Op.max))
                vstep(lambda: ts(mwc[:, 6:7], pd[0][:, 1:2], 1.0 / NG, EPS_Q,
                                 Op.mult, Op.max))
                vstep(lambda: ts(mwc[:, 3:4], pd[0][:, 2:3], 1.0 / NDW, EPS_Q,
                                 Op.mult, Op.max))
                vstep(lambda: v.reciprocal(out=mwc[:, 0:1], in_=mwc[:, 5:6]))
                vstep(lambda: v.reciprocal(out=mwc[:, 1:2], in_=mwc[:, 6:7]))
                vstep(lambda: v.reciprocal(out=mwc[:, 2:3], in_=mwc[:, 3:4]))
                vstep(lambda: v.tensor_tensor(out=mwc[:, 4:5], in0=mwc[:, 5:6],
                                              in1=mwc[:, 5:6], op=Op.mult))
                vstep(lambda: v.tensor_tensor(out=mwc[:, 4:5], in0=mwc[:, 4:5],
                                              in1=mwc[:, 6:7], op=Op.mult),
                      labels=[("cols",)])
            p0q.append(tmeans)
            PB_UNITS = []
            for w in ("g", "u"):
                for i in range(NW_T[w]):
                    for q in range(4):
                        PB_UNITS.append((w, i, q))
            for i in range(NW_T["d"]):
                PB_UNITS.append(("d", i, 0))
            for ug, (w, i, q) in enumerate(PB_UNITS):
                def tbu(w=w, i=i, q=q, ug=ug):
                    col = WCOL[w]
                    s = ug % 4
                    gbd = gbufB[:, s * D:(s + 1) * D]
                    g16 = wd_rv[:, ug % 2]
                    ctx.wait(v, None, "ldU", ug)
                    if ug >= 2:
                        ctx.wait(v, None, "stu", ug - 2)
                    vstep(lambda: ts(gbd, gbd, mwc[:, col:col + 1], MAGIC,
                                     Op.mult, Op.add))
                    vstep(lambda: ts(gbd, gbd, MAGIC, -1.0,
                                     Op.subtract, Op.max))
                    labs = [("pBu", ug)]
                    if w == "d" or q == 3:
                        labs.append(("pBq", w, i))
                    vstep(lambda: ts(g16, gbd, 1.0, 0.0, Op.min, Op.add),
                          labels=labs)
                p0q.append(tbu)

            n_p0q = len(p0q)

            def p0_drain(kmax, cap=None):
                k = 0
                while p0q and k < kmax:
                    if cap is not None and n_p0q - len(p0q) >= cap:
                        return
                    p0q.pop(0)()
                    k += 1

            for c in range(SEG):
                cf = c - 1
                if c == 1:
                    p0_drain(2)
                if c == 2:
                    p0_drain(10 ** 9)
                if c < NCH:
                    for j in range(NTJ):
                        for hh in range(2):
                            ctx.wait(v, None, "xl", c, j, hh)
                            vstep(lambda j=j, hh=hh: v.bn_stats(
                                out=bns[:, hh * 6:hh * 6 + 6], in_=xl_v[:, hh]))
                        vstep(lambda j=j: v.bn_aggr(
                            out=lncol[:, j * 8:j * 8 + 2], in_=bns[:]))
                        vstep(lambda j=j: ts(l1(j, 2), l1(j, 0), -1.0, 0.0,
                                             Op.mult, Op.add),
                              "v_ln", [("ln1", c, j)])
                if 2 <= c <= NCH + 1:
                    cd = c - 2
                    ctx.wait(v, "t_d", "dmm", cd, 1)
                    for sj in range(STP):
                        st = 2 + sj
                        for dh in range(2):
                            def f(sj=sj, st=st, dh=dh, cd=cd):
                                sl = r2_v[:, st][:, dh * 512:(dh + 1) * 512]
                                return v.scalar_tensor_tensor(
                                    out=sl, in0=pd[sj * 2 + dh][:],
                                    scalar=qcol_f(cd, st), in1=sl,
                                    op0=Op.mult, op1=Op.add)
                            labs = [("devst", cd, 1, sj, dh)]
                            if sj == STP - 1 and dh == 1:
                                labs.append(("dev", cd))
                            vstep(f, "v_dev", labs)
                if c < NCH:
                    ctx.wait(v, "t_bc", "bc", c)
                    vstep(lambda: v.tensor_copy(out=bct[:], in_=pr[0][0:32, 0:TT]),
                          "v_bce", [("bce", c)])
                    for dg in range(NDG):
                        ctx.wait(v, "a_sp", "sp", c, dg)
                        vstep(lambda dg=dg: v.tensor_tensor(
                            out=dt_v[:, dg], in0=sp1[:], in1=sp2[:], op=Op.add),
                            "v_dt", [("dt", c, dg)])
                    ctx.wait(v, "s_tl", "tl", c, NTJ - 1)
                    vstep(lambda: v.tensor_tensor(
                        out=u_all[:], in0=dt_all[:], in1=xlnT[:], op=Op.mult),
                        "v_u", [("u", c)])
                if 1 <= c <= NCH:
                    for j in range(NTJ):
                        for oh in range(2):
                            k = cf * NTJ * 2 + j * 2 + oh
                            ctx.wait(v, "t_wout", "wout", cf, j, oh)
                            ctx.wait(v, "s_x", "xr", cf, j, oh)
                            if j == 0 and oh == 0 and cf >= 1:
                                ctx.wait(v, "s_out", "out", cf - 1)
                            def f(j=j, oh=oh):
                                sl = r2_v[:, j][:, oh * 512:(oh + 1) * 512]
                                return v.scalar_tensor_tensor(
                                    out=sl, in0=pd[(j * 2 + oh) % 4][:], scalar=1.0,
                                    in1=xr_v[:, (j * 2 + oh) % 2],
                                    op0=Op.mult, op1=Op.add)
                            labs = [("r2", cf, j, oh), ("r2G", k)]
                            if j == NTJ - 1 and oh == 1:
                                labs.append(("r2done", cf))
                            vstep(f, "v_r2", labs)
                    for j in range(NTJ):
                        vstep(lambda j=j: v.bn_stats(out=bns[:, 0:6],
                                                     in_=r2_v[:, j][:, 0:512]))
                        vstep(lambda j=j: v.bn_stats(out=bns[:, 6:12],
                                                     in_=r2_v[:, j][:, 512:1024]))
                        vstep(lambda j=j: v.bn_aggr(
                            out=lncol[:, 32 + j * 8:32 + j * 8 + 2], in_=bns[:]))
                        vstep(lambda j=j: ts(l2(j, 2), l2(j, 0), -1.0, 0.0,
                                             Op.mult, Op.add),
                              "v_ln", [("ln2", cf, j)])
                        vstep(lambda j=j: v.tensor_reduce(
                            out=l2(j, 3), in_=r2_v[:, j], axis=AX.X, op=Op.max))
                        vstep(lambda j=j: v.tensor_reduce(
                            out=l2(j, 4), in_=r2_v[:, j], axis=AX.X, op=Op.min))
                        vstep(lambda j=j: v.scalar_tensor_tensor(
                            out=l2(j, 3), in0=l2(j, 3), scalar=1.0, in1=l2(j, 2),
                            op0=Op.mult, op1=Op.add))
                        vstep(lambda j=j: v.scalar_tensor_tensor(
                            out=l2(j, 4), in0=l2(j, 4), scalar=-1.0, in1=l2(j, 2),
                            op0=Op.mult, op1=Op.subtract))
                        vstep(lambda j=j: v.tensor_tensor(
                            out=l2(j, 3), in0=l2(j, 3), in1=l2(j, 4), op=Op.max))
                    for j in range(NTJ):
                        ctx.wait(v, "a_rstd", "rstd2", cf, j)
                        vstep(lambda j=j: v.tensor_tensor(
                            out=l2(j, 3), in0=l2(j, 3), in1=l2(j, 1), op=Op.mult))
                        vstep(lambda j=j: ts(l2(j, 3), l2(j, 3), EPS_Q, 0.0,
                                             Op.max, Op.add))
                        vstep(lambda j=j: v.reciprocal(out=l2(j, 5), in_=l2(j, 3)))
                        vstep(lambda j=j: ts(l2(j, 5), l2(j, 5), 127.0, 0.0,
                                             Op.mult, Op.add))
                        vstep(lambda j=j: v.tensor_tensor(
                            out=l2(j, 6), in0=l2(j, 1), in1=l2(j, 5), op=Op.mult))
                        vstep(lambda j=j: ts(l2(j, 3), l2(j, 3), 1.0 / 127.0, 0.0,
                                             Op.mult, Op.add))
                        vstep(lambda j=j: v.tensor_tensor(
                            out=l2(j, 4), in0=l2(j, 3), in1=l2(j, 3), op=Op.mult))
                        vstep(lambda j=j: v.tensor_tensor(
                            out=l2(j, 4), in0=l2(j, 4), in1=l2(j, 3), op=Op.mult))
                        vstep(lambda j=j: v.tensor_tensor(
                            out=l2(j, 7), in0=l2(j, 4), in1=mwc[:, 4:5], op=Op.mult))
                        if c < NCH:
                            ctx.wait(v, "s_tl", "tl", c, NTJ - 1)
                        else:
                            ctx.wait(v, "s_tq", "tq", cf - 1, NTJ - 1)
                        if cf >= 1 and j == 0:
                            ctx.wait(v, "a_hq2", "hq2", cf - 1, NTJ - 1)
                        for ph in range(2):
                            vstep(lambda j=j, ph=ph: ts(
                                t1q[:], r2_v[:, j][:, ph * 512:(ph + 1) * 512],
                                l2(j, 2), l2(j, 6), Op.add, Op.mult))
                            vstep(lambda j=j, ph=ph: ts(
                                tok_v[:, j][:, ph * 512:(ph + 1) * 512],
                                t1q[:], MAGIC, MAGIC, Op.add, Op.subtract),
                                "v_xq" if ph == 1 else None,
                                [("xq", cf, j)] if ph == 1 else ())
                if c < NCH:
                    for n in range(NST):
                        m = c * NST + n
                        ctx.wait(v, "a_bce", "bceB", c, n)
                        def fdB(n=n):
                            bc = bcB_v[:, n % 2].unsqueeze(1).broadcast_to(
                                [128, NDG, TT])
                            return v.tensor_tensor(out=dB_v, in0=u_v, in1=bc,
                                                   op=Op.mult)
                        vstep(fdB, "v_dB", [("dB", c, n), ("dBG", m)])
                        for dg in range(NDG):
                            gidx = m * NDG + dg
                            ctx.wait(v, "s_dA", "dA", c, n, dg)
                            def fsc(n=n, dg=dg, gidx=gidx, c=c):
                                col = n * NDG + dg
                                return v.tensor_tensor_scan(
                                    out=dB_v[:, dg], data0=dA_v[:, gidx % 2],
                                    data1=dB_v[:, dg],
                                    initial=carr_v[:, (c + 1) % 2][:, col:col + 1],
                                    op0=Op.mult, op1=Op.add)
                            vstep(fsc, "v_scan", [("scani", gidx)])
                        def fc(n=n, c=c):
                            src = dB_all[:].rearrange(
                                "p (k t) -> p k t", k=NDG)[:, :, TT - 1]
                            return v.tensor_copy(
                                out=carr_v[:, c % 2][:, n * NDG:(n + 1) * NDG],
                                in_=src)
                        vstep(fc)
                        ctx.wait(v, "a_bce", "bceC", c, n)
                        if m >= 1:
                            ctx.wait(v, "p_ya", "ya", (m - 1) // NST, (m - 1) % NST)
                        def fy(n=n):
                            bc = bcC_v[:, n % 2].unsqueeze(1).broadcast_to(
                                [128, NDG, TT])
                            return v.tensor_tensor(out=yp_v, in0=dB_v, in1=bc,
                                                   op=Op.mult)
                        vstep(fy, "v_ym", [("ym", c, n), ("ymG", m)])
                        if c * NST + n >= 3:
                            p0_drain(8, cap=(sum(NW_T.values()) + 1)
                                     if c == 0 else None)
                        if 1 <= c <= NCH:
                            ps = 99 if c == 1 else 2
                            if n >= ps and 4 * (n - ps) < NPAIR:
                                for q in range(4):
                                    hstep(v, vstep, cf, 4 * (n - ps) + q, c)
                    if c == 1:
                        for idx in range(NPAIR):
                            hstep(v, vstep, cf, idx, c)
                elif 1 <= c <= NCH:
                    for idx in range(NPAIR):
                        hstep(v, vstep, cf, idx, c)
                if 1 <= c <= NCH:
                    ctx.wait(v, "t_d", "dmm", cf, 0)
                    for sj3 in range(STP):
                        st3 = sj3
                        for dh in range(2):
                            def fd(sj3=sj3, st3=st3, dh=dh, cf=cf):
                                sl = r2_v[:, st3][:, dh * 512:(dh + 1) * 512]
                                return v.scalar_tensor_tensor(
                                    out=sl, in0=pd[sj3 * 2 + dh][:],
                                    scalar=qcol_f(cf, st3), in1=sl,
                                    op0=Op.mult, op1=Op.add)
                            vstep(fd, "v_dev", [("devst", cf, 0, sj3, dh)])

        def hstep(v, vstep, cf, idx, c):
            gi, sj = divmod(idx, STP)
            h, ic = divmod(gi, NIC)
            pidx = cf * NPAIR + idx
            ctx.wait(v, "a_rel", "rel", cf, idx)
            ctx.wait(v, "t_u", "u", cf, idx)
            if cf == 0 and idx == 0:
                ctx.wait(v, None, "pA", "d", NW_T["d"] - 1)
            if h == 1:
                ctx.wait_if(v, None, "hq1", cf, sj, ic * ICW // 512)
            else:
                ctx.wait_if(v, None, "hq1", cf - 1, STP + sj, ic * ICW // 512)
            rv = r_rv[:, idx % 2]

            def fh():
                return v.tensor_tensor(
                    out=hf_v[:, sj][:, ic * ICW:(ic + 1) * ICW],
                    in0=rv, in1=pu[:], op=Op.mult)
            vstep(fh, "v_h", [("h", cf, idx), ("hG", pidx)])

            def fm():
                return v.tensor_reduce(
                    out=maxp_v[:, sj][:, ic:ic + 1],
                    in_=hf_v[:, sj][:, ic * ICW:(ic + 1) * ICW], axis=AX.X,
                    op=Op.max, apply_absolute_value=True)
            vstep(fm)
            if ic == NIC - 1 and sj == STP - 1:
                for sj2 in range(STP):
                    st2 = h * STP + sj2
                    j = st2
                    vstep(lambda sj2=sj2: v.tensor_reduce(
                        out=lncol[:, 6:7], in_=maxp_v[:, sj2], axis=AX.X,
                        op=Op.max))
                    vstep(lambda j=j: v.reciprocal(out=lncol[:, 7:8],
                                                   in_=l2(j, 7)))
                    vstep(lambda: v.tensor_scalar(
                        out=lncol[:, 7:8], in0=lncol[:, 7:8], scalar1=EPS_Q,
                        scalar2=0.0, op0=Op.mult, op1=Op.add))
                    vstep(lambda: v.tensor_tensor(
                        out=lncol[:, 6:7], in0=lncol[:, 6:7],
                        in1=lncol[:, 7:8], op=Op.max))
                    vstep(lambda cf=cf, st2=st2: v.reciprocal(
                        out=qcol_s(cf, st2), in_=lncol[:, 6:7]))
                    vstep(lambda cf=cf, st2=st2: v.tensor_scalar(
                        out=qcol_s(cf, st2), in0=qcol_s(cf, st2), scalar1=127.0,
                        scalar2=0.0, op0=Op.mult, op1=Op.add),
                        labels=[("qcol", cf, st2)])
                    vstep(lambda cf=cf, st2=st2, j=j: v.tensor_tensor(
                        out=qcol_f(cf, st2), in0=l2(j, 7), in1=lncol[:, 6:7],
                        op=Op.mult))
                    vstep(lambda cf=cf, st2=st2: v.tensor_tensor(
                        out=qcol_f(cf, st2), in0=qcol_f(cf, st2), in1=mwc[:, 3:4],
                        op=Op.mult))
                    vstep(lambda cf=cf, st2=st2: v.tensor_scalar(
                        out=qcol_f(cf, st2), in0=qcol_f(cf, st2),
                        scalar1=1.0 / 127.0, scalar2=0.0, op0=Op.mult, op1=Op.add))



        ctx.start(False, sems)
        emit_dve(None)
        emit_act(None)
        emit_pe(None)
        emit_pool(None)
        emit_sync(None)

        ctx.start(True, sems)

        @block.vector
        def _(vector):
            emit_dve(vector)

        @block.scalar
        def _(scalar):
            emit_act(scalar)

        @block.tensor
        def _(tensor):
            emit_pe(tensor)

        @block.gpsimd
        def _(gpsimd):
            emit_pool(gpsimd)

        @block.sync
        def _(sync):
            emit_sync(sync)

    return nc


def _prep_inputs(A_log, W_dt, b_dt, W_xp, b_xp, W_out, b_out, w_gate, w_up,
                 w_down):
    import ml_dtypes
    f32 = np.float32
    d = {}
    d["wgT"] = np.ascontiguousarray(np.asarray(w_gate, f32).T).reshape(NDG, 128, NI)
    d["wuT"] = np.ascontiguousarray(np.asarray(w_up, f32).T).reshape(NDG, 128, NI)
    d["wdT"] = np.ascontiguousarray(np.asarray(w_down, f32).T).reshape(NKI, 128, D)
    d["woT"] = np.ascontiguousarray(np.asarray(W_out, f32).T).reshape(NDG, 128, D)
    d["wxpT"] = np.ascontiguousarray(
        np.asarray(W_xp, f32).T.reshape(NDG, 128, 32).transpose(1, 0, 2)
        .reshape(128, NDG * 32))
    d["alog"] = np.ascontiguousarray(
        np.asarray(A_log, f32).reshape(NDG, 128, NST).transpose(1, 0, 2)
        .reshape(128, NDG * NST))
    d["wdtr"] = np.ascontiguousarray(np.asarray(W_dt, f32)[:, 0][None, :])
    d["bdtr"] = np.ascontiguousarray(np.asarray(b_dt, f32)[None, :])
    d["bxpr"] = np.ascontiguousarray(np.asarray(b_xp, f32)[None, :])
    d["boutr"] = np.ascontiguousarray(np.asarray(b_out, f32)[None, :])
    oh = np.zeros((32, 32, 128), ml_dtypes.bfloat16)
    for n in range(32):
        oh[n, n, :] = 1
    d["oh32"] = np.ascontiguousarray(oh.transpose(1, 0, 2).reshape(32, 32 * 128))
    d["ones1"] = np.ones((1, TT), ml_dtypes.bfloat16)
    d["o128"] = np.ones((128, 128), np.float32)
    return d


_NC_CACHE = {}


def kernel(L0_attn_x, L0_attn_A_log, L0_attn_W_dt, L0_attn_b_dt, L0_attn_W_xp,
           L0_attn_b_xp, L0_attn_W_out, L0_attn_b_out, L0_ffn_w_gate,
           L0_ffn_w_up, L0_ffn_w_down):
    x = np.asarray(L0_attn_x, np.float32)
    B, T = x.shape[0], x.shape[1]
    if T not in _NC_CACHE:
        _NC_CACHE[T] = build_program(T)
    nc = _NC_CACHE[T]
    shared = _prep_inputs(L0_attn_A_log, L0_attn_W_dt, L0_attn_b_dt, L0_attn_W_xp,
                          L0_attn_b_xp, L0_attn_W_out, L0_attn_b_out,
                          L0_ffn_w_gate, L0_ffn_w_up, L0_ffn_w_down)
    in_maps = []
    for b in range(B):
        m = dict(shared)
        m["x"] = np.ascontiguousarray(x[b])
        in_maps.append(m)
    res = run_bass_kernel_spmd(nc, in_maps, core_ids=list(range(B)))
    return np.stack([np.asarray(r["out"]) for r in res.results], axis=0)


# version 18
# speedup vs baseline: 1.0910x; 1.0094x over previous
import contextlib
import numpy as np
import concourse.bass as bass
import concourse.mybir as mybir
from concourse.bass_utils import run_bass_kernel_spmd
from concourse.alu_op_type import AluOpType as Op

F32 = mybir.dt.float32
BF16 = mybir.dt.bfloat16
AF = mybir.ActivationFunctionType
AX = mybir.AxisListType

MAGIC = float(np.float32(3 * 2**22))
EPS_LN = 1e-5
EPS_Q = 1e-5

D = 1024
NI = 4096
NST = 16
TT = 512
NDG = D // 128
NTJ = TT // 128
NKI = NI // 128
ICW = 512
NIC = NI // ICW
NHALF = 2
STP = NTJ // NHALF
NGI = NHALF * NIC
NPAIR = NGI * STP
PPN = NPAIR // NST
NQ = NI // 512


class Ctx:

    def __init__(self):
        self.lab = {}

    def start(self, pass2, sems=None):
        self.pass2 = pass2
        self.cnt = {}
        self.sems = sems
        self.last = {}

    def inc(self, instr, sem, by=1, labels=(), tag=None):
        c = self.cnt.get(sem, 0) + by
        self.cnt[sem] = c
        if not self.pass2:
            for lb in labels:
                self.lab[tuple(lb)] = (sem, c)
        if self.pass2 and instr is not None:
            instr.then_inc(self.sems[sem], by)
        if tag is not None:
            self.last[tag] = (sem, c)
        return c

    def mark(self, _sem_unused, *label):
        raise RuntimeError("mark replaced by inc labels")

    def wait(self, eng, _sem_unused, *label):
        if not self.pass2:
            return
        ent = self.lab.get(tuple(label))
        assert ent is not None, f"missing label {tuple(label)}"
        sem, v = ent
        if v > 0:
            eng.wait_ge(self.sems[sem], v)

    def wait_if(self, eng, _sem_unused, *label):
        if tuple(label) in self.lab:
            self.wait(eng, None, *label)


def build_program(T):
    NCH = T // TT
    SEG = NCH + 2
    nc = bass.Bass("TRN2", target_bir_lowering=False)

    def din(name, shape, dt=F32):
        return nc.dram_tensor(name, shape, dt, kind="ExternalInput").ap()

    x_h = din("x", [T, D])
    wgT_h = din("wgT", [NDG, 128, NI])
    wuT_h = din("wuT", [NDG, 128, NI])
    wdT_h = din("wdT", [NKI, 128, D])
    woT_h = din("woT", [NDG, 128, D])
    wxpT_h = din("wxpT", [128, NDG * 32])
    alog_h = din("alog", [128, NDG * NST])
    wdt_h = din("wdtr", [1, D])
    bdt_h = din("bdtr", [1, D])
    bxp_h = din("bxpr", [1, 32])
    bout_h = din("boutr", [1, D])
    oh_h = din("oh32", [32, 32 * 128], BF16)
    ones_h = din("ones1", [1, TT], BF16)
    o128_h = din("o128", [128, 128])
    out_h = nc.dram_tensor("out", [T, D], F32, kind="ExternalOutput").ap()

    wgq_s = nc.dram_tensor("wgq_s", [NIC, NDG, 128, ICW], BF16, kind="Internal").ap()
    wuq_s = nc.dram_tensor("wuq_s", [NIC, NDG, 128, ICW], BF16, kind="Internal").ap()
    wdq_s = nc.dram_tensor("wdq_s", [NKI, 128, D], BF16, kind="Internal").ap()
    woq_s = nc.dram_tensor("woq_s", [NDG, 128, D], BF16, kind="Internal").ap()

    ctx = Ctx()
    es = contextlib.ExitStack()
    with es:
        def SB(name, p, f, dt):
            return es.enter_context(nc.sbuf_tensor(name, [p, f], dt))

        def PS(name, p, f):
            return es.enter_context(nc.psum_tensor(name, [p, f], F32))

        woT_r = SB("woT_r", 128, 4 * D, BF16)
        wxpT_b = SB("wxpT_b", 128, NDG * 32, BF16)
        wdt_b = SB("wdt_b", 1, D, BF16)
        bdt_b = SB("bdt_b", 1, D, BF16)
        bxp_b = SB("bxp_b", 1, 32, BF16)
        bout_b = SB("bout_b", 1, D, BF16)
        oh32 = SB("oh32_t", 32, 32 * 128, BF16)
        ones1 = SB("ones1_t", 1, TT, BF16)
        o128 = SB("o128_t", 128, 128, F32)
        negA = SB("negA", 128, NDG * NST, F32)
        ccol = SB("ccol", 128, 4, F32)
        mwc = SB("mwc", 128, 8, F32)
        wsum = SB("wsum", 128, 4, F32)

        xl_r = SB("xl_r", 128, 2 * 512, F32)
        xr_r = SB("xr_r", 128, 2 * 512, F32)
        r2 = SB("r2", 128, NTJ * D, BF16)
        tok = SB("tok", 128, NTJ * D, BF16)
        xlnT = SB("xlnT", 128, NDG * TT, BF16)
        xqT = SB("xqT", 128, NDG * TT, BF16)
        dt_all = SB("dt_all", 128, NDG * TT, BF16)
        u_all = SB("u_all", 128, NDG * TT, BF16)
        bct = SB("bct", 32, TT, BF16)
        bcB = SB("bcB", 128, 2 * TT, BF16)
        bcC = SB("bcC", 128, 2 * TT, BF16)
        dA_r = SB("dA_r", 128, 2 * TT, F32)
        dB_all = SB("dB_all", 128, NDG * TT, BF16)
        yp_all = SB("yp_all", 128, NDG * TT, BF16)
        y_acc = SB("y_acc", 128, NDG * TT, BF16)
        yT_b = SB("yT_b", 128, NDG * TT, BF16)
        carr = SB("carr", 128, 2 * 128, BF16)
        sp1 = SB("sp1", 128, TT, F32)
        sp2 = SB("sp2", 128, TT, BF16)
        lncol = SB("lncol", 128, 64, F32)
        bns = SB("bns", 128, 12, F32)
        t1q = SB("t1q", 128, 512, F32)
        r_r = SB("r_r", 128, 2 * ICW, F32)
        h_half = SB("h_half", 128, STP * NI, BF16)
        maxp = SB("maxp", 128, STP * NIC, F32)
        qcol = SB("qcol", 128, 16, F32)
        hq_b = SB("hq_b", 128, NI, BF16)
        hqT = SB("hqT", 128, NKI * STP * 128, BF16)
        wg_r = SB("wg_r", 128, 2 * NDG * ICW, BF16)
        wu_r = SB("wu_r", 128, 2 * NDG * ICW, BF16)
        wd_r = SB("wd_r", 128, 2 * D, BF16)

        gbufA = h_half[:].bitcast(F32)
        gbufB = hqT[:].bitcast(F32)
        qout = y_acc[:]

        pg = PS("pg", 128, ICW)
        pu = PS("pu", 128, ICW)
        pd = [PS(f"pd{i}", 128, 512) for i in range(4)]
        pr = [PS(f"pr{i}", 128, 512) for i in range(2)]

        sem_names = [
            "s_c0", "s_pa0", "s_pa1", "s_wo0", "s_wo1", "s_pb0", "s_pb1",
            "s_pb2", "s_pb3", "s_st", "s_st1", "s_st2", "s_st3",
            "s_xl0", "s_xl1", "s_xr0", "s_xr1", "s_tl", "s_tq", "s_th0", "s_th1",
            "s_wg0", "s_wg1", "s_wu0", "s_wu1", "s_wd0", "s_wd1", "s_wd2",
            "s_tq1", "s_wl0", "s_wl1", "s_wl2", "s_wl3", "s_ws0", "s_ws1", "s_ws2", "s_ws3",
            "s_out",
            "t_p0", "t_wout", "t_bc", "t_dtz", "t_bbc", "t_g", "t_u",
            "t_d",
            "a_s", "a_rstd", "a_sp", "a_spp", "a_bce", "s_dA", "a_rel", "a_hq2",
            "v_s", "v_ln", "v_xln", "v_r2", "v_xq", "v_bce", "v_dt", "v_u",
            "v_dB", "v_scan", "v_ym", "v_h", "v_hq1", "v_dev",
            "p_s", "p_ya", "p_hh",
        ]
        sems = {n: es.enter_context(nc.semaphore(n)) for n in sem_names}
        block = es.enter_context(nc.Block())

        xlnT_v = xlnT[:].rearrange("p (k j t) -> p k j t", k=NDG, j=NTJ)
        xqT_v = xqT[:].rearrange("p (k j t) -> p k j t", k=NDG, j=NTJ)
        dt_v = dt_all[:].rearrange("p (k t) -> p k t", k=NDG)
        dB_v = dB_all[:].rearrange("p (k t) -> p k t", k=NDG)
        u_v = u_all[:].rearrange("p (k t) -> p k t", k=NDG)
        yp_v = yp_all[:].rearrange("p (k t) -> p k t", k=NDG)
        ya_v = y_acc[:].rearrange("p (k j t) -> p k j t", k=NDG, j=NTJ)
        yT_v = yT_b[:].rearrange("p (k j t) -> p k j t", k=NDG, j=NTJ)
        r2_v = r2[:].rearrange("p (j d) -> p j d", j=NTJ)
        tok_v = tok[:].rearrange("p (j d) -> p j d", j=NTJ)
        woT_rv = woT_r[:].rearrange("p (r o) -> p r o", r=4)
        wxpT_v = wxpT_b[:].rearrange("p (k n) -> p k n", k=NDG)
        oh_v = oh32[:].rearrange("q (n p) -> q n p", n=32)
        dA_v = dA_r[:].rearrange("p (r t) -> p r t", r=2)
        bcB_v = bcB[:].rearrange("p (r t) -> p r t", r=2)
        bcC_v = bcC[:].rearrange("p (r t) -> p r t", r=2)
        carr_v = carr[:].rearrange("p (r i) -> p r i", r=2)
        xl_v = xl_r[:].rearrange("p (r d) -> p r d", r=2)
        xr_v = xr_r[:].rearrange("p (r d) -> p r d", r=2)
        r_rv = r_r[:].rearrange("p (r t) -> p r t", r=2)
        hf_v = h_half[:].rearrange("p (s q) -> p s q", s=STP)
        maxp_v = maxp[:].rearrange("p (s i) -> p s i", s=STP)
        hqT_v = hqT[:].rearrange("p (k s t) -> p k s t", k=NKI, s=STP)
        def hqslot(st):
            return hq_b[:] if st % 2 == 0 else xlnT[:]
        wg_rv = wg_r[:].rearrange("p (r k t) -> p r k t", r=2, k=NDG)
        wu_rv = wu_r[:].rearrange("p (r k t) -> p r k t", r=2, k=NDG)
        wd_rv = wd_r[:].rearrange("p (r d) -> p r d", r=2)

        def pg_of(idx):
            return pg if idx % 2 == 0 else pd[0]

        def pu_of(idx):
            return pu if idx % 2 == 0 else pd[1]

        def qcol_s(c, st):
            o = (c % 2) * 8 + st * 2
            return qcol[:, o:o + 1]

        def qcol_f(c, st):
            o = (c % 2) * 8 + st * 2 + 1
            return qcol[:, o:o + 1]

        def l1(j, i):
            return lncol[:, j * 8 + i:j * 8 + i + 1]

        def l2(j, i):
            return lncol[:, 32 + j * 8 + i:32 + j * 8 + i + 1]

        NW_T = {"g": NDG, "u": NDG, "d": NKI}
        NW_FD = {"g": NI, "u": NI, "d": D}
        WSRC = {"g": wgT_h, "u": wuT_h, "d": wdT_h}
        WSCR = {"g": wgq_s, "u": wuq_s, "d": wdq_s}
        WCOL = {"g": 0, "u": 1, "d": 2}

        _PBPREV = []
        for _w in ("g", "u", "d"):
            for _i in range(NW_T[_w]):
                _PBPREV.append((_w, _i))

        def gb(i):
            return gbufA if i % 2 == 0 else gbufB

        xt_h = x_h.rearrange("(c j p) d -> c j p d", c=NCH, j=NTJ)

        def emit_sync(sync):
            p2 = ctx.pass2

            def dma(out, in_, sem, labels=(), transpose=False):
                i = sync.dma_start(out=out, in_=in_, transpose=transpose) if p2 else None
                ctx.inc(i, sem, 16, labels)

            dma(o128[:], o128_h, "s_c0")
            dma(oh32[:], oh_h, "s_c0")
            dma(ones1[:], ones_h, "s_c0")
            dma(gbufA[:, 0:NDG * 32], wxpT_h, "s_c0")
            dma(gbufA[:, 512:512 + NDG * NST], alog_h, "s_c0")
            dma(gbufB[0:1, 0:D], wdt_h, "s_c0")
            dma(gbufB[0:1, D:2 * D], bdt_h, "s_c0")
            dma(gbufB[0:1, 2 * D:2 * D + 32], bxp_h, "s_c0")
            dma(gbufB[0:1, 3 * D:4 * D], bout_h, "s_c0", labels=[("c0",)])


            for c in range(SEG):
                cf = c - 1
                if c < NCH:
                    for j in range(NTJ):
                        ctx.wait(sync, "v_xln", "xln", c, j)
                        if c > 0 and j == 0:
                            ctx.wait(sync, "v_u", "u", c - 1)
                            ctx.wait(sync, "t_dtz", "dtz", c - 1, NDG - 1)
                            if c >= 2:
                                ctx.wait_if(sync, None, "th", c - 2, 0, 1)
                                ctx.wait_if(sync, None, "th", c - 2, 1, 1)
                        src = tok_v[:, j].rearrange("p (k t) -> p k t", k=NDG)
                        dma(xlnT_v[:, :, j], src, "s_tl", labels=[("tl", c, j)],
                            transpose=True)
                if 1 <= c <= NCH:
                    for j in range(NTJ):
                        ctx.wait(sync, "v_xq", "xq", cf, j)
                        if cf > 0 and j == 0:
                            ctx.wait(sync, "t_u", "gu", cf - 1, NGI - 1)
                        src = tok_v[:, j].rearrange("p (k t) -> p k t", k=NDG)
                        dma(xqT_v[:, :, j], src,
                            "s_tq" if j < 2 else "s_tq1",
                            labels=[("tq", cf, j)], transpose=True)
                    for gi in range(NGI):
                        ic = gi % NIC
                        gg = cf * NGI + gi
                        ctx.wait_if(sync, "t_u", "guG", gg - 2)
                        if cf == 0 and gi <= 1:
                            for uu in range(4 * NW_T["g"] - 4, 4 * NW_T["g"]):
                                ctx.wait(sync, None, "stu", uu)
                        dma(wg_rv[:, gi % 2], wgq_s[ic].rearrange("k p t -> p k t"),
                            f"s_wg{gi % 2}", labels=[("wg", cf, gi)])
                        if cf == 0 and gi <= 1:
                            for uu in range(8 * NW_T["g"] - 4, 8 * NW_T["g"]):
                                ctx.wait(sync, None, "stu", uu)
                        dma(wu_rv[:, gi % 2], wuq_s[ic].rearrange("k p t -> p k t"),
                            f"s_wu{gi % 2}", labels=[("wu", cf, gi)])
                    for h in range(NHALF):
                        for sj in range(STP):
                            st = h * STP + sj
                            sg = cf * NTJ + st
                            ctx.wait(sync, "a_hq2", "hq2", cf, st)
                            if cf == 0:
                                for uu in range(4 * 16 + NW_T["d"] - 4,
                                                4 * 16 + NW_T["d"]):
                                    ctx.wait(sync, None, "stu", uu)
                            if not (cf == 0 and h == 0):
                                ph = (cf, h - 1) if h > 0 else (cf - 1, NHALF - 1)
                                ctx.wait(sync, "t_d", "dmm", ph[0], ph[1])
                            src = hqslot(st).rearrange(
                                "p (k t) -> p k t", k=NKI)
                            dma(hqT_v[:, :, sj], src, f"s_th{st % 2}",
                                labels=[("th", cf, h, sj), ("thg", sg)],
                                transpose=True)
                        for k in range(NKI):
                            kg = (cf * NHALF + h) * NKI + k
                            ctx.wait_if(sync, "t_d", "wdG", kg - 2)
                            if cf == 0:
                                ctx.wait(sync, None, "st", "d", k)
                            dma(wd_rv[:, k % 2], wdq_s[k], f"s_wd{k % 2}",
                                labels=[("wd", cf, h, k)])
            if p2:
                sync.wait_ge(sems["s_out"], ctx.cnt.get("s_out", 0))

        def emit_pool(g):
            p2 = ctx.pass2

            def pstep(f, sem=None, labels=()):
                use = sem or "p_s"
                i = f() if p2 else None
                c = ctx.inc(i, use, 1, labels)
                if p2:
                    g.wait_ge(sems[use], c)

            def dma(out, in_, sem, labels=()):
                i = g.dma_start(out=out, in_=in_) if p2 else None
                ctx.inc(i, sem, 16, labels)

            for hh in range(2):
                dma(xl_v[:, hh], xt_h[0, 0][:, hh * 512:(hh + 1) * 512],
                    f"s_xl{hh}", labels=[("xl", 0, 0, hh)])

            p0ld = []
            p0i2 = 0
            for i in range(NDG):
                def tld(i=i, p0i2=p0i2):
                    if p0i2 == 0:
                        ctx.wait(g, None, "alog_ld")
                        ctx.wait(g, None, "negAexp")
                    if p0i2 == 1:
                        ctx.wait(g, None, "alog_ld")
                    if i >= 2:
                        ctx.wait(g, None, "wos", i - 2)
                    dma(gb(p0i2)[:, 0:D], woT_h[i], f"s_wo{p0i2 % 2}",
                        labels=[("ldwo", i)])
                p0ld.append(tld)
                if i >= 1:
                    def tws(i=i, p0i2=p0i2):
                        ctx.wait(g, None, "ldwo", i - 1)
                        dma(woq_s[i - 1], gb(p0i2 - 1)[:, 0:D],
                            f"s_ws{(i - 1) % 4}", labels=[("wos", i - 1)])
                    p0ld.append(tws)
                p0i2 += 1
            def twsl(p0i2=p0i2):
                ctx.wait(g, None, "ldwo", NDG - 1)
                dma(woq_s[NDG - 1], gb(p0i2 - 1)[:, 0:D],
                    f"s_ws{(NDG - 1) % 4}", labels=[("wos", NDG - 1)])
            p0ld.append(twsl)
            for w in ("g", "u", "d"):
                for i in range(NW_T[w]):
                    def ta2(w=w, i=i, p0i2=p0i2):
                        if p0i2 in (NDG, NDG + 1):
                            ctx.wait(g, None, "wos", p0i2 - 2)
                        ctx.wait_if(g, None, "p0c", p0i2 - 2)
                        dma(gb(p0i2)[:, 0:NW_FD[w]], WSRC[w][i],
                            f"s_pa{p0i2 % 2}", labels=[("ldA", w, i)])
                    p0ld.append(ta2)
                    p0i2 += 1
            PB_UNITS2 = []
            for w in ("g", "u"):
                for i in range(NW_T[w]):
                    for q in range(4):
                        PB_UNITS2.append((w, i, q))
            for i in range(NW_T["d"]):
                PB_UNITS2.append(("d", i, 0))
            for ug, (w, i, q) in enumerate(PB_UNITS2):
                def tbu2(w=w, i=i, q=q, ug=ug):
                    s = ug % 4
                    if ug >= 4:
                        ctx.wait_if(g, None, "pBu", ug - 4)
                    else:
                        ctx.wait(g, "v_s", "pA", "d", NW_T["d"] - 1)
                    srcu = (WSRC[w][i][:, q * D:(q + 1) * D] if w != "d"
                            else WSRC["d"][i])
                    dma(gbufB[:, s * D:(s + 1) * D], srcu,
                        f"s_pb{ug % 4}", labels=[("ldU", ug)])
                p0ld.append(tbu2)

            n_p0ld = len(p0ld)

            def p0ld_drain(kmax, cap=None):
                k = 0
                while p0ld and k < kmax:
                    if cap is not None and n_p0ld - len(p0ld) >= cap:
                        return
                    p0ld.pop(0)()
                    k += 1

            for c in range(SEG):
                cf = c - 1
                if c < NCH:
                    for j in range(1, NTJ):
                        for hh in range(2):
                            hg = (c * NTJ + j) * 2 + hh
                            ctx.wait_if(g, "v_ln", "xlh", hg - 2)
                            dma(xl_v[:, hh],
                                xt_h[c, j][:, hh * 512:(hh + 1) * 512],
                                f"s_xl{hh}", labels=[("xl", c, j, hh)])
                if c + 1 < NCH:
                    for hh in range(2):
                        hg = ((c + 1) * NTJ) * 2 + hh
                        ctx.wait_if(g, "v_ln", "xlh", hg - 2)
                        dma(xl_v[:, hh],
                            xt_h[c + 1, 0][:, hh * 512:(hh + 1) * 512],
                            f"s_xl{hh}", labels=[("xl", c + 1, 0, hh)])
                if 2 <= c <= NCH + 1:
                    ctx.wait(g, "v_dev", "dev", c - 2)
                    ov = out_h.rearrange("(c j p) d -> c j p d", c=NCH, j=NTJ)[c - 2]
                    for j in range(NTJ):
                        dma(ov[j], r2_v[:, j], "s_out",
                            labels=[("out", c - 2)] if j == NTJ - 1 else ())
                if 1 <= c <= NCH:
                    for j in range(NTJ):
                        for oh in range(2):
                            k = cf * NTJ * 2 + j * 2 + oh
                            ctx.wait_if(g, "v_r2", "r2G", k - 2)
                            dma(xr_v[:, (j * 2 + oh) % 2],
                                xt_h[cf, j][:, oh * 512:(oh + 1) * 512],
                                f"s_xr{(j * 2 + oh) % 2}",
                                labels=[("xr", cf, j, oh)])
                if c == 2:
                    p0ld_drain(10 ** 9)
                if c < NCH:
                    for n in range(NST):
                        ctx.wait(g, "v_ym", "ym", c, n)
                        if n == 0:
                            if c > 0:
                                ctx.wait(g, None, "yT", c - 1)
                            pstep(lambda: g.tensor_copy(out=y_acc[:], in_=yp_all[:]),
                                  "p_ya", [("ya", c, n)])
                        else:
                            pstep(lambda: g.tensor_tensor(
                                out=y_acc[:], in0=y_acc[:], in1=yp_all[:], op=Op.add),
                                "p_ya", [("ya", c, n)])
                        p0ld_drain(8, cap=(2 * NDG + sum(NW_T.values()) + 4)
                                   if c == 0 else None)
                if c < NCH:
                    if c >= 1:
                        ctx.wait(g, "t_wout", "wout", c - 1, NTJ - 1, 1)
                    pstep(lambda: g.tensor_copy(out=yT_b[:], in_=y_acc[:]),
                          "p_ya", [("yT", c)])


        def emit_pe(t):
            p2 = ctx.pass2

            def mm(out, lhsT, rhs, start, stop, sem=None, labels=()):
                i = t.matmul(out=out, lhsT=lhsT, rhs=rhs, start=start, stop=stop) if p2 else None
                if sem:
                    ctx.inc(i, sem, 1, labels)

            for c in range(SEG):
                cf = c - 1
                if c < NCH:
                    ctx.wait(t, "s_tl", "tl", c, NTJ - 1)
                    if c >= 1:
                        ctx.wait(t, "a_bce", "bceB", c - 1, NST - 1)
                    for dg in range(NDG):
                        mm(pr[0][0:32, 0:TT], wxpT_v[:, dg],
                           xlnT_v[:, dg].rearrange("p j t -> p (j t)"), dg == 0, False)
                    mm(pr[0][0:32, 0:TT], bxp_b[:], ones1[:], False, True,
                       "t_bc", [("bc", c)])
                    for dg in range(NDG):
                        pp = pr[dg % 2]
                        if dg == 0:
                            ctx.wait(t, "v_bce", "bce", c)
                        elif dg == 1:
                            if c >= 1:
                                ctx.wait(t, "a_bce", "bceC", c - 1, NST - 1)
                        else:
                            ctx.wait(t, "a_spp", "spp", c, dg - 2)
                        mm(pp[:, 0:TT], wdt_b[0:1, dg * 128:(dg + 1) * 128],
                           xlnT[0:1, 0:TT], True, False)
                        mm(pp[:, 0:TT], bdt_b[0:1, dg * 128:(dg + 1) * 128],
                           ones1[:], False, True, "t_dtz", [("dtz", c, dg)])

                def bbc_pair(n, c=c):
                    if n == 0:
                        ctx.wait(t, "a_spp", "spp", c, NDG - 2)
                        ctx.wait(t, "v_bce", "bce", c)
                    else:
                        ctx.wait(t, "a_bce", "bceB", c, n - 1)
                    mm(pr[0][:, 0:TT], oh_v[:, n], bct[:], True, True,
                       "t_bbc", [("bbcB", c, n)])
                    if n == 0:
                        ctx.wait(t, "a_spp", "spp", c, NDG - 1)
                    else:
                        ctx.wait(t, "a_bce", "bceC", c, n - 1)
                    mm(pr[1][:, 0:TT], oh_v[:, NST + n], bct[:], True, True,
                       "t_bbc", [("bbcC", c, n)])

                if c < NCH:
                    bbc_pair(0)
                    if c == 0:
                        for n in range(1, NST):
                            bbc_pair(n)
                        ctx.wait(t, "v_s", "partials")
                        mm(pd[0][:, 0:4], o128[:], wsum[:, 0:4], True, True,
                           "t_p0", [("tot",)])
                if 1 <= c <= NCH:
                    ctx.wait(t, None, "yT", cf)
                    for rnd in range(2):
                        for dg in range(NDG):
                            wog = (cf * 2 + rnd) * NDG + dg
                            ctx.wait(t, None, "wol", cf, rnd, dg)
                            for b in range(4):
                                k = rnd * 4 + b
                                j, oh = k // 2, k % 2
                                if dg == 0:
                                    if rnd == 0:
                                        if c >= 2:
                                            ctx.wait(t, "v_dev", "devst",
                                                     c - 2, 1, b // 2, b % 2)
                                        elif b == 0:
                                            ctx.wait(t, "v_s", "cols")
                                    else:
                                        ctx.wait(t, "v_r2", "r2",
                                                 cf, b // 2, b % 2)
                                mm(pd[b][:], yT_v[:, dg, j],
                                   woT_rv[:, wog % 4, oh * 512:(oh + 1) * 512],
                                   dg == 0, False,
                                   "t_wout" if b == 3 else None,
                                   [("wocG", wog)])
                        for b in range(4):
                            k = rnd * 4 + b
                            j, oh = k // 2, k % 2
                            mm(pd[b][:], ones1[0:1, 0:128],
                               bout_b[0:1, oh * 512:(oh + 1) * 512], False, True,
                               "t_wout", [("wout", cf, j, oh)])
                if 1 <= c < NCH:
                    bbc_pair(1)
                    bbc_pair(2)
                    if c == 1:
                        for m in range(3, NST):
                            bbc_pair(m)
                for gi in range(NGI):
                    h, ic = divmod(gi, NIC)
                    if 1 <= c <= NCH and gi == 0:
                        ctx.wait(t, "s_tq", "tq", cf, 1)
                    if 1 <= c <= NCH and gi == NIC:
                        ctx.wait(t, "s_tq1", "tq", cf, 3)
                    if 1 <= c <= NCH:
                        gg = cf * NGI + gi
                        ctx.wait(t, None, "wg", cf, gi)
                        ctx.wait(t, None, "wu", cf, gi)
                        for sj in range(STP):
                            st = h * STP + sj
                            idx = gi * STP + sj
                            pidx = cf * NPAIR + idx
                            if gi == 0 and sj == 1:
                                ctx.wait(t, "v_r2", "r2", cf, 2, 0)
                                ctx.wait(t, "v_r2", "r2", cf, 2, 1)
                            ctx.wait_if(t, "a_rel", "relG", pidx - 2)
                            for dg in range(NDG):
                                mm(pg_of(idx)[:, 0:ICW], xqT_v[:, dg, st],
                                   wg_rv[:, gi % 2, dg],
                                   dg == 0, dg == NDG - 1,
                                   "t_g" if dg == NDG - 1 else None,
                                   [("g", cf, idx)])
                            ctx.wait_if(t, "v_h", "hG", pidx - 2)
                            for dg in range(NDG):
                                labs = [("u", cf, idx)]
                                if sj == STP - 1:
                                    labs += [("gu", cf, gi), ("guG", gg)]
                                mm(pu_of(idx)[:, 0:ICW], xqT_v[:, dg, st],
                                   wu_rv[:, gi % 2, dg],
                                   dg == 0, dg == NDG - 1,
                                   "t_u" if dg == NDG - 1 else None, labs)
                    if 2 <= c < NCH and gi % 2 == 1:
                        bbc_pair((gi + 5) // 2)
                if 1 <= c <= NCH:
                    m0 = (NGI - 1 + 5) // 2 + 1
                    for h in range(NHALF):
                        ctx.wait(t, None, "th", cf, h, 0)
                        ctx.wait(t, None, "th", cf, h, STP - 1)
                        for k in range(NKI):
                            if (h == 0 and 2 <= c < NCH and k % 6 == 4
                                    and m0 + (k - 4) // 6 < NST):
                                bbc_pair(m0 + (k - 4) // 6)
                            ctx.wait(t, "s_wd", "wd", cf, h, k)
                            kg = (cf * NHALF + h) * NKI + k
                            for sj in range(STP):
                                for dh in range(2):
                                    pp = pd[sj * 2 + dh]
                                    if k == 0:
                                        if h == 0:
                                            ctx.wait(t, "v_r2", "r2",
                                                     cf, 2 + sj, dh)
                                            if sj == 0 and dh == 0:
                                                ctx.wait(t, "a_rel", "rel",
                                                         cf, NPAIR - 1)
                                            if sj == 0 and dh == 1:
                                                ctx.wait(t, "v_h", "h",
                                                         cf, NPAIR - 1)
                                        else:
                                            ctx.wait(t, "v_dev", "devst",
                                                     cf, 0, sj, dh)
                                    labs = [("wdG", kg)]
                                    if k == NKI - 1:
                                        labs.append(("dmm", cf, h))
                                    mm(pp[:], hqT_v[:, k, sj],
                                       wd_rv[:, k % 2, dh * 512:(dh + 1) * 512],
                                       k == 0, k == NKI - 1,
                                       "t_d" if (sj == STP - 1 and dh == 1) else None,
                                       labs)
                        if h == 0 and 2 <= c < NCH:
                            for m in range(m0 + 5, NST):
                                bbc_pair(m)

        def emit_act(a):
            p2 = ctx.pass2

            def adma(out, in_, sem, labels=()):
                i = a.dma_start(out=out, in_=in_) if p2 else None
                ctx.inc(i, sem, 16, labels)

            pstq = []
            PB_UNITS3 = []
            for w in ("g", "u"):
                for i in range(NW_T[w]):
                    for q in range(4):
                        PB_UNITS3.append((w, i, q))
            for i in range(NW_T["d"]):
                PB_UNITS3.append(("d", i, 0))
            for ug, (w, i, q) in enumerate(PB_UNITS3):
                def tst(w=w, i=i, q=q, ug=ug):
                    s = ug % 4
                    ctx.wait(a, "v_s", "pBu", ug)
                    srcu = wd_rv[:, ug % 2]
                    sem = "s_st" if ug % 4 == 0 else f"s_st{ug % 4}"
                    labs = [("stu", ug)]
                    if w == "d":
                        labs.append(("st", "d", i))
                        adma(WSCR["d"][i], srcu, sem, labels=labs)
                    else:
                        if q == 3:
                            labs.append(("st", w, i))
                        adma(WSCR[w][2 * q:2 * q + 2, i]
                             .rearrange("ic p t -> p ic t"),
                             srcu.rearrange("p (ic t) -> p ic t", ic=2),
                             sem, labels=labs)
                pstq.append(tst)

            def pst_drain(kmax):
                k = 0
                while pstq and k < kmax:
                    pstq.pop(0)()
                    k += 1

            def wol_round(cf, rnd):
                for dg in range(NDG):
                    wog = (cf * 2 + rnd) * NDG + dg
                    ctx.wait_if(a, None, "wocG", wog - 4)
                    if cf == 0 and rnd == 0:
                        ctx.wait(a, f"s_ws{dg % 4}", "wos", 4 + dg % 4)
                    adma(woT_rv[:, wog % 4], woq_s[dg], f"s_wl{wog % 4}",
                         labels=[("wol", cf, rnd, dg)])

            def astep(f, sem=None, labels=()):
                use = sem or "a_s"
                i = f() if p2 else None
                c = ctx.inc(i, use, 1, labels)
                if p2:
                    a.wait_ge(sems[use], c)

            def act(out, in_, func, bias=None, scale=None):
                kw = {}
                if bias is not None:
                    kw["bias"] = bias
                if scale is not None:
                    kw["scale"] = scale
                return a.activation(out=out, in_=in_, func=func, **kw)

            def relu_step(cf, idx):
                pidx = cf * NPAIR + idx
                ctx.wait(a, "t_g", "g", cf, idx)
                ctx.wait_if(a, "v_h", "hG", pidx - 2)
                astep(lambda idx=idx: act(r_rv[:, idx % 2],
                                          pg_of(idx)[:, 0:ICW], AF.Relu),
                      None, [("relG0", pidx)])
                astep(lambda idx=idx: act(r_rv[:, idx % 2], r_rv[:, idx % 2],
                                          AF.Square),
                      "a_rel", [("rel", cf, idx), ("relG", pidx)])

            act_queue = []
            act_slot = [0]

            def hq2_drain(kmax):
                k = 0
                while act_queue and k < kmax:
                    cfq, st, sj, q = act_queue.pop(0)
                    qg = act_slot[0]
                    act_slot[0] += 1
                    h = st // STP
                    lastpiece = 2 * (h * NIC + (q * 512 + 511) // ICW) + sj
                    ctx.wait(a, "v_h", "h", cfq, lastpiece)
                    ctx.wait(a, None, "qcol", cfq, st)
                    if q == 0:
                        ctx.wait(a, None, "xq", cfq, NTJ - 1)
                    astep(lambda st=st, q=q, qg=qg, cfq=cfq, sj=sj: act(
                        t1q[:],
                        hf_v[:, sj][:, q * 512:(q + 1) * 512],
                        AF.Identity, scale=qcol_s(cfq, st), bias=ccol[:, 3:4]),
                          None, [("hq1", cfq, st, q)])
                    if q == 0:
                        sg = cfq * NTJ + st
                        ctx.wait_if(a, "s_th", "thg", sg - 2)
                    if st % 2 == 1 and q == 0:
                        ctx.wait_if(a, None, "u", cfq + 1)
                        ctx.wait_if(a, None, "dtz", cfq + 1, NDG - 1)
                    labels = []
                    if q == NQ - 1:
                        labels.append(("hq2", cfq, st))
                    astep(lambda st=st, q=q, qg=qg: act(
                        hqslot(st)[:, q * 512:(q + 1) * 512],
                        t1q[:], AF.Identity, bias=ccol[:, 2:3]),
                          "a_hq2", labels)
                    k += 1

            ctx.wait(a, "v_s", "alog_ld")
            astep(lambda: act(negA[:], gbufA[:, 512:512 + NDG * NST], AF.Exp),
                  "a_rstd", [("negAexp",)])

            for c in range(SEG):
                cf = c - 1
                if c < NCH:
                    for j in range(NTJ):
                        ctx.wait(a, "v_ln", "ln1", c, j)
                        astep(lambda j=j: act(l1(j, 1), l1(j, 1), AF.Ln,
                                              bias=ccol[:, 0:1]))
                        astep(lambda j=j: act(l1(j, 1), l1(j, 1), AF.Exp,
                                              scale=ccol[:, 1:2]),
                              "a_rstd", [("rstd1", c, j)])
                if c < NCH:
                    for dg in range(NDG):
                        pp = pr[dg % 2]
                        ctx.wait(a, "t_dtz", "dtz", c, dg)
                        if dg >= 1:
                            ctx.wait(a, None, "dt", c, dg - 1)
                        elif c >= 1:
                            ctx.wait(a, None, "dt", c - 1, NDG - 1)
                        astep(lambda pp=pp: act(sp1[:], pp[:, 0:TT], AF.Abs))
                        astep(lambda pp=pp: act(sp2[:], pp[:, 0:TT], AF.Relu),
                              "a_spp", [("spp", c, dg)])
                        astep(lambda: act(sp1[:], sp1[:], AF.Exp, scale=-1.0))
                        astep(lambda: act(sp1[:], sp1[:], AF.Ln, bias=1.0),
                              "a_sp", [("sp", c, dg)])
                if c == 2:
                    pst_drain(10 ** 9)
                if 1 <= c <= NCH:
                    wol_round(cf, 0)
                    wol_round(cf, 1)
                if 1 <= c <= NCH:
                    for j in range(NTJ):
                        ctx.wait(a, "v_ln", "ln2", cf, j)
                        astep(lambda j=j: act(l2(j, 1), l2(j, 1), AF.Ln,
                                              bias=ccol[:, 0:1]))
                        astep(lambda j=j: act(l2(j, 1), l2(j, 1), AF.Exp,
                                              scale=ccol[:, 1:2]),
                              "a_rstd", [("rstd2", cf, j)])
                if c < NCH:
                    for n in range(NST):
                        m = c * NST + n
                        ctx.wait(a, "t_bbc", "bbcB", c, n)
                        ctx.wait_if(a, "v_dB", "dBG", m - 2)
                        astep(lambda n=n: act(bcB_v[:, n % 2], pr[0][:, 0:TT],
                                              AF.Copy),
                              "a_bce", [("bceB", c, n)])
                        ctx.wait(a, "t_bbc", "bbcC", c, n)
                        ctx.wait_if(a, "v_ym", "ymG", m - 2)
                        astep(lambda n=n: act(bcC_v[:, n % 2], pr[1][:, 0:TT],
                                              AF.Copy),
                              "a_bce", [("bceC", c, n)])
                        for dg in range(NDG):
                            gidx = m * NDG + dg
                            ctx.wait(a, "v_dt", "dt", c, dg)
                            ctx.wait_if(a, "v_scan", "scani", gidx - 2)
                            astep(lambda n=n, dg=dg, gidx=gidx: act(
                                dA_v[:, gidx % 2], dt_v[:, dg], AF.Exp,
                                scale=negA[:, dg * NST + n:dg * NST + n + 1]),
                                "s_dA", [("dA", c, n, dg)])
                        if c == 1:
                            pst_drain(8)
                        if 1 <= c <= NCH:
                            ps = 99 if c == 1 else 2
                            rs = 99 if c == 1 else 2
                            if n == ps + 4:
                                act_queue.extend(hq_entries(cf, 0))
                            if c >= 2 and n == 10:
                                act_queue.extend(hq_entries(cf, 1))
                            if n >= ps + 4:
                                hq2_drain(4)
                            if n >= rs and 4 * (n - rs) < NPAIR:
                                for q in range(4):
                                    relu_step(cf, 4 * (n - rs) + q)
                elif 1 <= c <= NCH:
                    for idx in range(NPAIR):
                        relu_step(cf, idx)
                        if idx == NPAIR // 2:
                            act_queue.extend(hq_entries(cf, 0))
                        if idx >= NPAIR // 2:
                            hq2_drain(1)
                if c == 1:
                    pst_drain(10 ** 9)
                    for idx in range(NPAIR):
                        relu_step(cf, idx)
                        if idx == NPAIR // 2:
                            act_queue.extend(hq_entries(cf, 0))
                        if idx >= NPAIR // 2:
                            hq2_drain(1)
                if 1 <= c <= NCH:
                    hq2_drain(10 ** 9)
                    if c == 1 or c == NCH:
                        act_queue.extend(hq_entries(cf, 1))
                        hq2_drain(10 ** 9)

        def hq_entries(cf, h):
            out = []
            for q in range(NQ):
                for sj in range(STP):
                    out.append((cf, h * STP + sj, sj, q))
            return out


        def vstep_factory(v):
            def vstep(f, sem=None, labels=()):
                use = sem or "v_s"
                i = f() if ctx.pass2 else None
                c = ctx.inc(i, use, 1, labels)
                if ctx.pass2:
                    v.wait_ge(sems[use], c)
            return vstep

        def emit_dve(v):
            p2 = ctx.pass2
            vstep = vstep_factory(v)

            def ts(out, in0, s1, s2, op0, op1):
                return v.tensor_scalar(out=out, in0=in0, scalar1=s1, scalar2=s2,
                                       op0=op0, op1=op1)

            ctx.wait(v, "s_scr", "c0")
            vstep(lambda: v.memset(ccol[:, 0:1], EPS_LN))
            vstep(lambda: v.memset(ccol[:, 1:2], -0.5))
            vstep(lambda: v.memset(ccol[:, 2:3], -MAGIC))
            vstep(lambda: v.memset(ccol[:, 3:4], MAGIC))
            vstep(lambda: v.memset(carr_v[:, 1], 0.0))
            vstep(lambda: v.memset(wsum[:], 0.0))
            vstep(lambda: v.tensor_copy(out=wxpT_b[:], in_=gbufA[:, 0:NDG * 32]))
            vstep(lambda: v.tensor_copy(out=wdt_b[:], in_=gbufB[0:1, 0:D]))
            vstep(lambda: v.tensor_copy(out=bdt_b[:], in_=gbufB[0:1, D:2 * D]))
            vstep(lambda: v.tensor_copy(out=bxp_b[:], in_=gbufB[0:1, 2 * D:2 * D + 32]))
            vstep(lambda: v.tensor_copy(out=bout_b[:], in_=gbufB[0:1, 3 * D:4 * D]),
                  labels=[("alog_ld",)])
            ctx.wait(v, "a_rstd", "negAexp")
            vstep(lambda: ts(negA[:], negA[:], -1.0, 0.0, Op.mult, Op.add))
            p0q = []
            p0i = NDG
            for w in ("g", "u", "d"):
                col = WCOL[w]
                for i in range(NW_T[w]):
                    def ta(w=w, i=i, p0i=p0i, col=col):
                        ctx.wait(v, None, "ldA", w, i)
                        vstep(lambda: v.tensor_reduce(
                            out=wsum[:, 3:4], in_=gb(p0i)[:, 0:NW_FD[w]],
                            axis=AX.X, op=Op.add, apply_absolute_value=True),
                            labels=[("p0c", p0i)])
                        vstep(lambda: v.tensor_tensor(
                            out=wsum[:, col:col + 1], in0=wsum[:, col:col + 1],
                            in1=wsum[:, 3:4], op=Op.add), labels=[("pA", w, i)])
                    p0q.append(ta)
                    p0i += 1

            def tpart():
                vstep(lambda: v.memset(wsum[:, 3:4], 0.0),
                      labels=[("partials",)])
            p0q.append(tpart)
            NG = float(NDG * 128 * NI)
            NDW = float(NKI * 128 * D)

            def tmeans():
                ctx.wait(v, "t_p0", "tot")
                vstep(lambda: ts(mwc[:, 5:6], pd[0][:, 0:1], 1.0 / NG, EPS_Q,
                                 Op.mult, Op.max))
                vstep(lambda: ts(mwc[:, 6:7], pd[0][:, 1:2], 1.0 / NG, EPS_Q,
                                 Op.mult, Op.max))
                vstep(lambda: ts(mwc[:, 3:4], pd[0][:, 2:3], 1.0 / NDW, EPS_Q,
                                 Op.mult, Op.max))
                vstep(lambda: v.reciprocal(out=mwc[:, 0:1], in_=mwc[:, 5:6]))
                vstep(lambda: v.reciprocal(out=mwc[:, 1:2], in_=mwc[:, 6:7]))
                vstep(lambda: v.reciprocal(out=mwc[:, 2:3], in_=mwc[:, 3:4]))
                vstep(lambda: v.tensor_tensor(out=mwc[:, 4:5], in0=mwc[:, 5:6],
                                              in1=mwc[:, 5:6], op=Op.mult))
                vstep(lambda: v.tensor_tensor(out=mwc[:, 4:5], in0=mwc[:, 4:5],
                                              in1=mwc[:, 6:7], op=Op.mult),
                      labels=[("cols",)])
            p0q.append(tmeans)
            PB_UNITS = []
            for w in ("g", "u"):
                for i in range(NW_T[w]):
                    for q in range(4):
                        PB_UNITS.append((w, i, q))
            for i in range(NW_T["d"]):
                PB_UNITS.append(("d", i, 0))
            for ug, (w, i, q) in enumerate(PB_UNITS):
                def tbu(w=w, i=i, q=q, ug=ug):
                    col = WCOL[w]
                    s = ug % 4
                    gbd = gbufB[:, s * D:(s + 1) * D]
                    g16 = wd_rv[:, ug % 2]
                    ctx.wait(v, None, "ldU", ug)
                    if ug >= 2:
                        ctx.wait(v, None, "stu", ug - 2)
                    vstep(lambda: ts(gbd, gbd, mwc[:, col:col + 1], MAGIC,
                                     Op.mult, Op.add))
                    vstep(lambda: ts(gbd, gbd, MAGIC, -1.0,
                                     Op.subtract, Op.max))
                    labs = [("pBu", ug)]
                    if w == "d" or q == 3:
                        labs.append(("pBq", w, i))
                    vstep(lambda: ts(g16, gbd, 1.0, 0.0, Op.min, Op.add),
                          labels=labs)
                p0q.append(tbu)

            n_p0q = len(p0q)

            def p0_drain(kmax, cap=None):
                k = 0
                while p0q and k < kmax:
                    if cap is not None and n_p0q - len(p0q) >= cap:
                        return
                    p0q.pop(0)()
                    k += 1

            for c in range(SEG):
                cf = c - 1
                if c == 1:
                    p0_drain(2)
                if c == 2:
                    p0_drain(10 ** 9)
                if c < NCH:
                    for j in range(NTJ):
                        for hh in range(2):
                            ctx.wait(v, None, "xl", c, j, hh)
                            vstep(lambda j=j, hh=hh: v.bn_stats(
                                out=bns[:, hh * 6:hh * 6 + 6], in_=xl_v[:, hh]))
                        vstep(lambda j=j: v.bn_aggr(
                            out=lncol[:, j * 8:j * 8 + 2], in_=bns[:]))
                        vstep(lambda j=j: ts(l1(j, 2), l1(j, 0), -1.0, 0.0,
                                             Op.mult, Op.add),
                              "v_ln", [("ln1", c, j)])
                        ctx.wait(v, "a_rstd", "rstd1", c, j)
                        if c >= 2:
                            ctx.wait(v, "s_tq", "tq", c - 2, NTJ - 1)
                        jg = c * NTJ + j
                        for hh in range(2):
                            hg = (c * NTJ + j) * 2 + hh
                            labs = [("xlh", hg)]
                            if hh == 1:
                                labs += [("xln", c, j), ("xlnG", jg)]
                            vstep(lambda j=j, c=c, hh=hh: ts(
                                tok_v[:, j][:, hh * 512:(hh + 1) * 512],
                                xl_v[:, hh], l1(j, 2), l1(j, 1),
                                Op.add, Op.mult),
                                "v_ln" if hh == 0 else "v_xln", labs)
                if 2 <= c <= NCH + 1:
                    cd = c - 2
                    ctx.wait(v, "t_d", "dmm", cd, 1)
                    for sj in range(STP):
                        st = 2 + sj
                        for dh in range(2):
                            def f(sj=sj, st=st, dh=dh, cd=cd):
                                sl = r2_v[:, st][:, dh * 512:(dh + 1) * 512]
                                return v.scalar_tensor_tensor(
                                    out=sl, in0=pd[sj * 2 + dh][:],
                                    scalar=qcol_f(cd, st), in1=sl,
                                    op0=Op.mult, op1=Op.add)
                            labs = [("devst", cd, 1, sj, dh)]
                            if sj == STP - 1 and dh == 1:
                                labs.append(("dev", cd))
                            vstep(f, "v_dev", labs)
                if c < NCH:
                    ctx.wait(v, "t_bc", "bc", c)
                    vstep(lambda: v.tensor_copy(out=bct[:], in_=pr[0][0:32, 0:TT]),
                          "v_bce", [("bce", c)])
                    for dg in range(NDG):
                        ctx.wait(v, "a_sp", "sp", c, dg)
                        vstep(lambda dg=dg: v.tensor_tensor(
                            out=dt_v[:, dg], in0=sp1[:], in1=sp2[:], op=Op.add),
                            "v_dt", [("dt", c, dg)])
                    ctx.wait(v, "s_tl", "tl", c, NTJ - 1)
                    vstep(lambda: v.tensor_tensor(
                        out=u_all[:], in0=dt_all[:], in1=xlnT[:], op=Op.mult),
                        "v_u", [("u", c)])
                if 1 <= c <= NCH:
                    for j in range(NTJ):
                        for oh in range(2):
                            k = cf * NTJ * 2 + j * 2 + oh
                            ctx.wait(v, "t_wout", "wout", cf, j, oh)
                            ctx.wait(v, "s_x", "xr", cf, j, oh)
                            if j == 0 and oh == 0 and cf >= 1:
                                ctx.wait(v, "s_out", "out", cf - 1)
                            def f(j=j, oh=oh):
                                sl = r2_v[:, j][:, oh * 512:(oh + 1) * 512]
                                return v.scalar_tensor_tensor(
                                    out=sl, in0=pd[(j * 2 + oh) % 4][:], scalar=1.0,
                                    in1=xr_v[:, (j * 2 + oh) % 2],
                                    op0=Op.mult, op1=Op.add)
                            labs = [("r2", cf, j, oh), ("r2G", k)]
                            if j == NTJ - 1 and oh == 1:
                                labs.append(("r2done", cf))
                            vstep(f, "v_r2", labs)
                    for j in range(NTJ):
                        vstep(lambda j=j: v.bn_stats(out=bns[:, 0:6],
                                                     in_=r2_v[:, j][:, 0:512]))
                        vstep(lambda j=j: v.bn_stats(out=bns[:, 6:12],
                                                     in_=r2_v[:, j][:, 512:1024]))
                        vstep(lambda j=j: v.bn_aggr(
                            out=lncol[:, 32 + j * 8:32 + j * 8 + 2], in_=bns[:]))
                        vstep(lambda j=j: ts(l2(j, 2), l2(j, 0), -1.0, 0.0,
                                             Op.mult, Op.add),
                              "v_ln", [("ln2", cf, j)])
                        vstep(lambda j=j: v.tensor_reduce(
                            out=l2(j, 3), in_=r2_v[:, j], axis=AX.X, op=Op.max))
                        vstep(lambda j=j: v.tensor_reduce(
                            out=l2(j, 4), in_=r2_v[:, j], axis=AX.X, op=Op.min))
                        vstep(lambda j=j: v.scalar_tensor_tensor(
                            out=l2(j, 3), in0=l2(j, 3), scalar=1.0, in1=l2(j, 2),
                            op0=Op.mult, op1=Op.add))
                        vstep(lambda j=j: v.scalar_tensor_tensor(
                            out=l2(j, 4), in0=l2(j, 4), scalar=-1.0, in1=l2(j, 2),
                            op0=Op.mult, op1=Op.subtract))
                        vstep(lambda j=j: v.tensor_tensor(
                            out=l2(j, 3), in0=l2(j, 3), in1=l2(j, 4), op=Op.max))
                    for j in range(NTJ):
                        ctx.wait(v, "a_rstd", "rstd2", cf, j)
                        vstep(lambda j=j: v.tensor_tensor(
                            out=l2(j, 3), in0=l2(j, 3), in1=l2(j, 1), op=Op.mult))
                        vstep(lambda j=j: ts(l2(j, 3), l2(j, 3), EPS_Q, 0.0,
                                             Op.max, Op.add))
                        vstep(lambda j=j: v.reciprocal(out=l2(j, 5), in_=l2(j, 3)))
                        vstep(lambda j=j: ts(l2(j, 5), l2(j, 5), 127.0, 0.0,
                                             Op.mult, Op.add))
                        vstep(lambda j=j: v.tensor_tensor(
                            out=l2(j, 6), in0=l2(j, 1), in1=l2(j, 5), op=Op.mult))
                        vstep(lambda j=j: ts(l2(j, 3), l2(j, 3), 1.0 / 127.0, 0.0,
                                             Op.mult, Op.add))
                        vstep(lambda j=j: v.tensor_tensor(
                            out=l2(j, 4), in0=l2(j, 3), in1=l2(j, 3), op=Op.mult))
                        vstep(lambda j=j: v.tensor_tensor(
                            out=l2(j, 4), in0=l2(j, 4), in1=l2(j, 3), op=Op.mult))
                        vstep(lambda j=j: v.tensor_tensor(
                            out=l2(j, 7), in0=l2(j, 4), in1=mwc[:, 4:5], op=Op.mult))
                        if c < NCH:
                            ctx.wait(v, "s_tl", "tl", c, NTJ - 1)
                        else:
                            ctx.wait(v, "s_tq", "tq", cf - 1, NTJ - 1)
                        if cf >= 1 and j == 0:
                            ctx.wait(v, "a_hq2", "hq2", cf - 1, NTJ - 1)
                        for ph in range(2):
                            vstep(lambda j=j, ph=ph: ts(
                                t1q[:], r2_v[:, j][:, ph * 512:(ph + 1) * 512],
                                l2(j, 2), l2(j, 6), Op.add, Op.mult))
                            vstep(lambda j=j, ph=ph: ts(
                                tok_v[:, j][:, ph * 512:(ph + 1) * 512],
                                t1q[:], MAGIC, MAGIC, Op.add, Op.subtract),
                                "v_xq" if ph == 1 else None,
                                [("xq", cf, j)] if ph == 1 else ())
                if c < NCH:
                    for n in range(NST):
                        m = c * NST + n
                        ctx.wait(v, "a_bce", "bceB", c, n)
                        def fdB(n=n):
                            bc = bcB_v[:, n % 2].unsqueeze(1).broadcast_to(
                                [128, NDG, TT])
                            return v.tensor_tensor(out=dB_v, in0=u_v, in1=bc,
                                                   op=Op.mult)
                        vstep(fdB, "v_dB", [("dB", c, n), ("dBG", m)])
                        for dg in range(NDG):
                            gidx = m * NDG + dg
                            ctx.wait(v, "s_dA", "dA", c, n, dg)
                            def fsc(n=n, dg=dg, gidx=gidx, c=c):
                                col = n * NDG + dg
                                return v.tensor_tensor_scan(
                                    out=dB_v[:, dg], data0=dA_v[:, gidx % 2],
                                    data1=dB_v[:, dg],
                                    initial=carr_v[:, (c + 1) % 2][:, col:col + 1],
                                    op0=Op.mult, op1=Op.add)
                            vstep(fsc, "v_scan", [("scani", gidx)])
                        def fc(n=n, c=c):
                            src = dB_all[:].rearrange(
                                "p (k t) -> p k t", k=NDG)[:, :, TT - 1]
                            return v.tensor_copy(
                                out=carr_v[:, c % 2][:, n * NDG:(n + 1) * NDG],
                                in_=src)
                        vstep(fc)
                        ctx.wait(v, "a_bce", "bceC", c, n)
                        if m >= 1:
                            ctx.wait(v, "p_ya", "ya", (m - 1) // NST, (m - 1) % NST)
                        def fy(n=n):
                            bc = bcC_v[:, n % 2].unsqueeze(1).broadcast_to(
                                [128, NDG, TT])
                            return v.tensor_tensor(out=yp_v, in0=dB_v, in1=bc,
                                                   op=Op.mult)
                        vstep(fy, "v_ym", [("ym", c, n), ("ymG", m)])
                        if c * NST + n >= 3:
                            p0_drain(8, cap=(sum(NW_T.values()) + 1)
                                     if c == 0 else None)
                        if 1 <= c <= NCH:
                            ps = 99 if c == 1 else 2
                            if n >= ps and 4 * (n - ps) < NPAIR:
                                for q in range(4):
                                    hstep(v, vstep, cf, 4 * (n - ps) + q, c)
                    if c == 1:
                        for idx in range(NPAIR):
                            hstep(v, vstep, cf, idx, c)
                elif 1 <= c <= NCH:
                    for idx in range(NPAIR):
                        hstep(v, vstep, cf, idx, c)
                if 1 <= c <= NCH:
                    ctx.wait(v, "t_d", "dmm", cf, 0)
                    for sj3 in range(STP):
                        st3 = sj3
                        for dh in range(2):
                            def fd(sj3=sj3, st3=st3, dh=dh, cf=cf):
                                sl = r2_v[:, st3][:, dh * 512:(dh + 1) * 512]
                                return v.scalar_tensor_tensor(
                                    out=sl, in0=pd[sj3 * 2 + dh][:],
                                    scalar=qcol_f(cf, st3), in1=sl,
                                    op0=Op.mult, op1=Op.add)
                            vstep(fd, "v_dev", [("devst", cf, 0, sj3, dh)])

        def hstep(v, vstep, cf, idx, c):
            gi, sj = divmod(idx, STP)
            h, ic = divmod(gi, NIC)
            pidx = cf * NPAIR + idx
            ctx.wait(v, "a_rel", "rel", cf, idx)
            ctx.wait(v, "t_u", "u", cf, idx)
            if cf == 0 and idx == 0:
                ctx.wait(v, None, "pA", "d", NW_T["d"] - 1)
            if h == 1:
                ctx.wait_if(v, None, "hq1", cf, sj, ic * ICW // 512)
            else:
                ctx.wait_if(v, None, "hq1", cf - 1, STP + sj, ic * ICW // 512)
            rv = r_rv[:, idx % 2]

            def fh():
                return v.tensor_tensor(
                    out=hf_v[:, sj][:, ic * ICW:(ic + 1) * ICW],
                    in0=rv, in1=pu_of(idx)[:, 0:ICW], op=Op.mult)
            vstep(fh, "v_h", [("h", cf, idx), ("hG", pidx)])

            def fm():
                return v.tensor_reduce(
                    out=maxp_v[:, sj][:, ic:ic + 1],
                    in_=hf_v[:, sj][:, ic * ICW:(ic + 1) * ICW], axis=AX.X,
                    op=Op.max, apply_absolute_value=True)
            vstep(fm)
            if ic == NIC - 1 and sj == STP - 1:
                for sj2 in range(STP):
                    st2 = h * STP + sj2
                    j = st2
                    vstep(lambda sj2=sj2: v.tensor_reduce(
                        out=lncol[:, 6:7], in_=maxp_v[:, sj2], axis=AX.X,
                        op=Op.max))
                    vstep(lambda j=j: v.reciprocal(out=lncol[:, 7:8],
                                                   in_=l2(j, 7)))
                    vstep(lambda: v.tensor_scalar(
                        out=lncol[:, 7:8], in0=lncol[:, 7:8], scalar1=EPS_Q,
                        scalar2=0.0, op0=Op.mult, op1=Op.add))
                    vstep(lambda: v.tensor_tensor(
                        out=lncol[:, 6:7], in0=lncol[:, 6:7],
                        in1=lncol[:, 7:8], op=Op.max))
                    vstep(lambda cf=cf, st2=st2: v.reciprocal(
                        out=qcol_s(cf, st2), in_=lncol[:, 6:7]))
                    vstep(lambda cf=cf, st2=st2: v.tensor_scalar(
                        out=qcol_s(cf, st2), in0=qcol_s(cf, st2), scalar1=127.0,
                        scalar2=0.0, op0=Op.mult, op1=Op.add),
                        labels=[("qcol", cf, st2)])
                    vstep(lambda cf=cf, st2=st2, j=j: v.tensor_tensor(
                        out=qcol_f(cf, st2), in0=l2(j, 7), in1=lncol[:, 6:7],
                        op=Op.mult))
                    vstep(lambda cf=cf, st2=st2: v.tensor_tensor(
                        out=qcol_f(cf, st2), in0=qcol_f(cf, st2), in1=mwc[:, 3:4],
                        op=Op.mult))
                    vstep(lambda cf=cf, st2=st2: v.tensor_scalar(
                        out=qcol_f(cf, st2), in0=qcol_f(cf, st2),
                        scalar1=1.0 / 127.0, scalar2=0.0, op0=Op.mult, op1=Op.add))



        ctx.start(False, sems)
        emit_dve(None)
        emit_act(None)
        emit_pe(None)
        emit_pool(None)
        emit_sync(None)

        ctx.start(True, sems)

        @block.vector
        def _(vector):
            emit_dve(vector)

        @block.scalar
        def _(scalar):
            emit_act(scalar)

        @block.tensor
        def _(tensor):
            emit_pe(tensor)

        @block.gpsimd
        def _(gpsimd):
            emit_pool(gpsimd)

        @block.sync
        def _(sync):
            emit_sync(sync)

    return nc


def _prep_inputs(A_log, W_dt, b_dt, W_xp, b_xp, W_out, b_out, w_gate, w_up,
                 w_down):
    import ml_dtypes
    f32 = np.float32
    d = {}
    d["wgT"] = np.ascontiguousarray(np.asarray(w_gate, f32).T).reshape(NDG, 128, NI)
    d["wuT"] = np.ascontiguousarray(np.asarray(w_up, f32).T).reshape(NDG, 128, NI)
    d["wdT"] = np.ascontiguousarray(np.asarray(w_down, f32).T).reshape(NKI, 128, D)
    d["woT"] = np.ascontiguousarray(np.asarray(W_out, f32).T).reshape(NDG, 128, D)
    d["wxpT"] = np.ascontiguousarray(
        np.asarray(W_xp, f32).T.reshape(NDG, 128, 32).transpose(1, 0, 2)
        .reshape(128, NDG * 32))
    d["alog"] = np.ascontiguousarray(
        np.asarray(A_log, f32).reshape(NDG, 128, NST).transpose(1, 0, 2)
        .reshape(128, NDG * NST))
    d["wdtr"] = np.ascontiguousarray(np.asarray(W_dt, f32)[:, 0][None, :])
    d["bdtr"] = np.ascontiguousarray(np.asarray(b_dt, f32)[None, :])
    d["bxpr"] = np.ascontiguousarray(np.asarray(b_xp, f32)[None, :])
    d["boutr"] = np.ascontiguousarray(np.asarray(b_out, f32)[None, :])
    oh = np.zeros((32, 32, 128), ml_dtypes.bfloat16)
    for n in range(32):
        oh[n, n, :] = 1
    d["oh32"] = np.ascontiguousarray(oh.transpose(1, 0, 2).reshape(32, 32 * 128))
    d["ones1"] = np.ones((1, TT), ml_dtypes.bfloat16)
    d["o128"] = np.ones((128, 128), np.float32)
    return d


_NC_CACHE = {}


def kernel(L0_attn_x, L0_attn_A_log, L0_attn_W_dt, L0_attn_b_dt, L0_attn_W_xp,
           L0_attn_b_xp, L0_attn_W_out, L0_attn_b_out, L0_ffn_w_gate,
           L0_ffn_w_up, L0_ffn_w_down):
    x = np.asarray(L0_attn_x, np.float32)
    B, T = x.shape[0], x.shape[1]
    if T not in _NC_CACHE:
        _NC_CACHE[T] = build_program(T)
    nc = _NC_CACHE[T]
    shared = _prep_inputs(L0_attn_A_log, L0_attn_W_dt, L0_attn_b_dt, L0_attn_W_xp,
                          L0_attn_b_xp, L0_attn_W_out, L0_attn_b_out,
                          L0_ffn_w_gate, L0_ffn_w_up, L0_ffn_w_down)
    in_maps = []
    for b in range(B):
        m = dict(shared)
        m["x"] = np.ascontiguousarray(x[b])
        in_maps.append(m)
    res = run_bass_kernel_spmd(nc, in_maps, core_ids=list(range(B)))
    return np.stack([np.asarray(r["out"]) for r in res.results], axis=0)


# version 19 (final)
# speedup vs baseline: 1.1127x; 1.0199x over previous
import contextlib
import numpy as np
import concourse.bass as bass
import concourse.mybir as mybir
from concourse.bass_utils import run_bass_kernel_spmd
from concourse.alu_op_type import AluOpType as Op

F32 = mybir.dt.float32
BF16 = mybir.dt.bfloat16
AF = mybir.ActivationFunctionType
AX = mybir.AxisListType

MAGIC = float(np.float32(3 * 2**22))
EPS_LN = 1e-5
EPS_Q = 1e-5

D = 1024
NI = 4096
NST = 16
TT = 512
NDG = D // 128
NTJ = TT // 128
NKI = NI // 128
ICW = 512
NIC = NI // ICW
NHALF = 2
STP = NTJ // NHALF
NGI = NHALF * NIC
NPAIR = NGI * STP
PPN = NPAIR // NST
NQ = NI // 512


class Ctx:

    def __init__(self):
        self.lab = {}

    def start(self, pass2, sems=None):
        self.pass2 = pass2
        self.cnt = {}
        self.sems = sems
        self.last = {}

    def inc(self, instr, sem, by=1, labels=(), tag=None):
        c = self.cnt.get(sem, 0) + by
        self.cnt[sem] = c
        if not self.pass2:
            for lb in labels:
                self.lab[tuple(lb)] = (sem, c)
        if self.pass2 and instr is not None:
            instr.then_inc(self.sems[sem], by)
        if tag is not None:
            self.last[tag] = (sem, c)
        return c

    def mark(self, _sem_unused, *label):
        raise RuntimeError("mark replaced by inc labels")

    def wait(self, eng, _sem_unused, *label):
        if not self.pass2:
            return
        ent = self.lab.get(tuple(label))
        assert ent is not None, f"missing label {tuple(label)}"
        sem, v = ent
        if v > 0:
            eng.wait_ge(self.sems[sem], v)

    def wait_if(self, eng, _sem_unused, *label):
        if tuple(label) in self.lab:
            self.wait(eng, None, *label)


def build_program(T):
    NCH = T // TT
    SEG = NCH + 2
    nc = bass.Bass("TRN2", target_bir_lowering=False)

    def din(name, shape, dt=F32):
        return nc.dram_tensor(name, shape, dt, kind="ExternalInput").ap()

    x_h = din("x", [T, D])
    wgT_h = din("wgT", [NDG, 128, NI])
    wuT_h = din("wuT", [NDG, 128, NI])
    wdT_h = din("wdT", [NKI, 128, D])
    woT_h = din("woT", [NDG, 128, D])
    wxpT_h = din("wxpT", [128, NDG * 32])
    alog_h = din("alog", [128, NDG * NST])
    wdt_h = din("wdtr", [1, D])
    bdt_h = din("bdtr", [1, D])
    bxp_h = din("bxpr", [1, 32])
    bout_h = din("boutr", [1, D])
    oh_h = din("oh32", [32, 32 * 128], BF16)
    ones_h = din("ones1", [1, TT], BF16)
    o128_h = din("o128", [128, 128])
    out_h = nc.dram_tensor("out", [T, D], F32, kind="ExternalOutput").ap()

    wgq_s = nc.dram_tensor("wgq_s", [NIC, NDG, 128, ICW], BF16, kind="Internal").ap()
    wuq_s = nc.dram_tensor("wuq_s", [NIC, NDG, 128, ICW], BF16, kind="Internal").ap()
    wdq_s = nc.dram_tensor("wdq_s", [NKI, 128, D], BF16, kind="Internal").ap()
    woq_s = nc.dram_tensor("woq_s", [NDG, 128, D], BF16, kind="Internal").ap()

    ctx = Ctx()
    es = contextlib.ExitStack()
    with es:
        def SB(name, p, f, dt):
            return es.enter_context(nc.sbuf_tensor(name, [p, f], dt))

        def PS(name, p, f):
            return es.enter_context(nc.psum_tensor(name, [p, f], F32))

        woT_r = SB("woT_r", 128, 4 * D, BF16)
        wxpT_b = SB("wxpT_b", 128, NDG * 32, BF16)
        wdt_b = SB("wdt_b", 1, D, BF16)
        bdt_b = SB("bdt_b", 1, D, BF16)
        bxp_b = SB("bxp_b", 1, 32, BF16)
        bout_b = SB("bout_b", 1, D, BF16)
        oh32 = SB("oh32_t", 32, 32 * 128, BF16)
        ones1 = SB("ones1_t", 1, TT, BF16)
        o128 = SB("o128_t", 128, 128, F32)
        negA = SB("negA", 128, NDG * NST, F32)
        ccol = SB("ccol", 128, 4, F32)
        mwc = SB("mwc", 128, 8, F32)
        wsum = SB("wsum", 128, 4, F32)

        xl_r = SB("xl_r", 128, 2 * 512, F32)
        xr_r = SB("xr_r", 128, 2 * 512, F32)
        r2 = SB("r2", 128, NTJ * D, BF16)
        tok = SB("tok", 128, NTJ * D, BF16)
        xlnT = SB("xlnT", 128, NDG * TT, BF16)
        xqT = SB("xqT", 128, NDG * TT, BF16)
        dt_all = SB("dt_all", 128, NDG * TT, BF16)
        u_all = SB("u_all", 128, NDG * TT, BF16)
        bct = SB("bct", 32, TT, BF16)
        bcB = SB("bcB", 128, 2 * TT, BF16)
        bcC = SB("bcC", 128, 2 * TT, BF16)
        dA_r = SB("dA_r", 128, 2 * TT, F32)
        dB_all = SB("dB_all", 128, NDG * TT, BF16)
        yp_all = SB("yp_all", 128, NDG * TT, BF16)
        y_acc = SB("y_acc", 128, NDG * TT, BF16)
        yT_b = SB("yT_b", 128, NDG * TT, BF16)
        carr = SB("carr", 128, 2 * 128, BF16)

        sp2 = SB("sp2", 128, TT, BF16)
        lncol = SB("lncol", 128, 64, F32)
        bns = SB("bns", 128, 12, F32)
        t1q = SB("t1q", 128, 512, F32)
        r_r = SB("r_r", 128, 3 * ICW, F32)
        h_half = SB("h_half", 128, STP * NI, BF16)
        maxp = SB("maxp", 128, STP * NIC, F32)
        qcol = SB("qcol", 128, 16, F32)
        hq_b = SB("hq_b", 128, NI, BF16)
        sp1 = hq_b[:].bitcast(F32)[:, 0:TT]
        hqT = SB("hqT", 128, NKI * STP * 128, BF16)
        wg_r = SB("wg_r", 128, 2 * NDG * ICW, BF16)
        wu_r = SB("wu_r", 128, 2 * NDG * ICW, BF16)
        wd_r = SB("wd_r", 128, 2 * D, BF16)

        gbufA = h_half[:].bitcast(F32)
        gbufB = hqT[:].bitcast(F32)
        qout = y_acc[:]

        pg = PS("pg", 128, ICW)
        pu = PS("pu", 128, ICW)
        pd = [PS(f"pd{i}", 128, 512) for i in range(4)]
        pr = [PS(f"pr{i}", 128, 512) for i in range(2)]

        sem_names = [
            "s_c0", "s_pa0", "s_pa1", "s_wo0", "s_wo1", "s_pb0", "s_pb1",
            "s_pb2", "s_pb3", "s_st", "s_st1", "s_st2", "s_st3",
            "s_xl0", "s_xl1", "s_xr0", "s_xr1", "s_tl", "s_tq", "s_th0", "s_th1",
            "s_wg0", "s_wg1", "s_wu0", "s_wu1", "s_wd0", "s_wd1", "s_wd2",
            "s_tq1", "s_wl0", "s_wl1", "s_wl2", "s_wl3", "s_ws0", "s_ws1", "s_ws2", "s_ws3",
            "s_out",
            "t_p0", "t_wout", "t_bc", "t_dtz", "t_bbc", "t_g", "t_u",
            "t_d",
            "a_s", "a_rstd", "a_sp", "a_spp", "a_bce", "s_dA", "a_rel", "a_hq2",
            "v_s", "v_ln", "v_xln", "v_r2", "v_xq", "v_bce", "v_dt", "v_u",
            "v_dB", "v_scan", "v_ym", "v_h", "v_hq1", "v_dev",
            "p_s", "p_ya", "p_hh",
        ]
        sems = {n: es.enter_context(nc.semaphore(n)) for n in sem_names}
        block = es.enter_context(nc.Block())

        xlnT_v = xlnT[:].rearrange("p (k j t) -> p k j t", k=NDG, j=NTJ)
        xqT_v = xqT[:].rearrange("p (k j t) -> p k j t", k=NDG, j=NTJ)
        dt_v = dt_all[:].rearrange("p (k t) -> p k t", k=NDG)
        dB_v = dB_all[:].rearrange("p (k t) -> p k t", k=NDG)
        u_v = u_all[:].rearrange("p (k t) -> p k t", k=NDG)
        yp_v = yp_all[:].rearrange("p (k t) -> p k t", k=NDG)
        ya_v = y_acc[:].rearrange("p (k j t) -> p k j t", k=NDG, j=NTJ)
        yT_v = yT_b[:].rearrange("p (k j t) -> p k j t", k=NDG, j=NTJ)
        r2_v = r2[:].rearrange("p (j d) -> p j d", j=NTJ)
        tok_v = tok[:].rearrange("p (j d) -> p j d", j=NTJ)
        woT_rv = woT_r[:].rearrange("p (r o) -> p r o", r=4)
        wxpT_v = wxpT_b[:].rearrange("p (k n) -> p k n", k=NDG)
        oh_v = oh32[:].rearrange("q (n p) -> q n p", n=32)
        dA_v = dA_r[:].rearrange("p (r t) -> p r t", r=2)
        bcB_v = bcB[:].rearrange("p (r t) -> p r t", r=2)
        bcC_v = bcC[:].rearrange("p (r t) -> p r t", r=2)
        carr_v = carr[:].rearrange("p (r i) -> p r i", r=2)
        xl_v = xl_r[:].rearrange("p (r d) -> p r d", r=2)
        xr_v = xr_r[:].rearrange("p (r d) -> p r d", r=2)
        r_rv = r_r[:].rearrange("p (r t) -> p r t", r=3)
        hf_v = h_half[:].rearrange("p (s q) -> p s q", s=STP)
        maxp_v = maxp[:].rearrange("p (s i) -> p s i", s=STP)
        hqT_v = hqT[:].rearrange("p (k s t) -> p k s t", k=NKI, s=STP)
        def hqslot(st):
            return hq_b[:] if st % 2 == 0 else xlnT[:]
        wg_rv = wg_r[:].rearrange("p (r k t) -> p r k t", r=2, k=NDG)
        wu_rv = wu_r[:].rearrange("p (r k t) -> p r k t", r=2, k=NDG)
        wd_rv = wd_r[:].rearrange("p (r d) -> p r d", r=2)

        def pg_of(idx):
            return (pg, pd[0], pd[2])[idx % 3]

        def pu_of(idx):
            return (pu, pd[1], pd[3])[idx % 3]

        def qcol_s(c, st):
            o = (c % 2) * 8 + st * 2
            return qcol[:, o:o + 1]

        def qcol_f(c, st):
            o = (c % 2) * 8 + st * 2 + 1
            return qcol[:, o:o + 1]

        def l1(j, i):
            return lncol[:, j * 8 + i:j * 8 + i + 1]

        def l2(j, i):
            return lncol[:, 32 + j * 8 + i:32 + j * 8 + i + 1]

        NW_T = {"g": NDG, "u": NDG, "d": NKI}
        NW_FD = {"g": NI, "u": NI, "d": D}
        WSRC = {"g": wgT_h, "u": wuT_h, "d": wdT_h}
        WSCR = {"g": wgq_s, "u": wuq_s, "d": wdq_s}
        WCOL = {"g": 0, "u": 1, "d": 2}

        _PBPREV = []
        for _w in ("g", "u", "d"):
            for _i in range(NW_T[_w]):
                _PBPREV.append((_w, _i))

        def gb(i):
            return gbufA if i % 2 == 0 else gbufB

        xt_h = x_h.rearrange("(c j p) d -> c j p d", c=NCH, j=NTJ)

        def emit_sync(sync):
            p2 = ctx.pass2

            def dma(out, in_, sem, labels=(), transpose=False):
                i = sync.dma_start(out=out, in_=in_, transpose=transpose) if p2 else None
                ctx.inc(i, sem, 16, labels)

            dma(o128[:], o128_h, "s_c0")
            dma(oh32[:], oh_h, "s_c0")
            dma(ones1[:], ones_h, "s_c0")
            dma(gbufA[:, 0:NDG * 32], wxpT_h, "s_c0")
            dma(gbufA[:, 512:512 + NDG * NST], alog_h, "s_c0")
            dma(gbufB[0:1, 0:D], wdt_h, "s_c0")
            dma(gbufB[0:1, D:2 * D], bdt_h, "s_c0")
            dma(gbufB[0:1, 2 * D:2 * D + 32], bxp_h, "s_c0")
            dma(gbufB[0:1, 3 * D:4 * D], bout_h, "s_c0", labels=[("c0",)])


            for c in range(SEG):
                cf = c - 1
                if c < NCH:
                    for j in range(NTJ):
                        ctx.wait(sync, "v_xln", "xln", c, j)
                        if c > 0 and j == 0:
                            ctx.wait(sync, "v_u", "u", c - 1)
                            ctx.wait(sync, "t_dtz", "dtz", c - 1, NDG - 1)
                            if c >= 2:
                                ctx.wait_if(sync, None, "th", c - 2, 0, 1)
                                ctx.wait_if(sync, None, "th", c - 2, 1, 1)
                        src = tok_v[:, j].rearrange("p (k t) -> p k t", k=NDG)
                        dma(xlnT_v[:, :, j], src, "s_tl", labels=[("tl", c, j)],
                            transpose=True)
                if 1 <= c <= NCH:
                    for j in range(NTJ):
                        ctx.wait(sync, "v_xq", "xq", cf, j)
                        if cf > 0 and j == 0:
                            ctx.wait(sync, "t_u", "gu", cf - 1, NGI - 1)
                        src = tok_v[:, j].rearrange("p (k t) -> p k t", k=NDG)
                        dma(xqT_v[:, :, j], src,
                            "s_tq" if j < 2 else "s_tq1",
                            labels=[("tq", cf, j)], transpose=True)
                    for gi in range(NGI):
                        ic = gi % NIC
                        gg = cf * NGI + gi
                        ctx.wait_if(sync, "t_u", "guG", gg - 2)
                        if cf == 0 and gi <= 1:
                            for uu in range(4 * NW_T["g"] - 4, 4 * NW_T["g"]):
                                ctx.wait(sync, None, "stu", uu)
                        dma(wg_rv[:, gi % 2], wgq_s[ic].rearrange("k p t -> p k t"),
                            f"s_wg{gi % 2}", labels=[("wg", cf, gi)])
                        if cf == 0 and gi <= 1:
                            for uu in range(8 * NW_T["g"] - 4, 8 * NW_T["g"]):
                                ctx.wait(sync, None, "stu", uu)
                        dma(wu_rv[:, gi % 2], wuq_s[ic].rearrange("k p t -> p k t"),
                            f"s_wu{gi % 2}", labels=[("wu", cf, gi)])
                    for h in range(NHALF):
                        for sj in range(STP):
                            st = h * STP + sj
                            sg = cf * NTJ + st
                            ctx.wait(sync, "a_hq2", "hq2", cf, st)
                            if cf == 0:
                                for uu in range(4 * 16 + NW_T["d"] - 4,
                                                4 * 16 + NW_T["d"]):
                                    ctx.wait(sync, None, "stu", uu)
                            if not (cf == 0 and h == 0):
                                ph = (cf, h - 1) if h > 0 else (cf - 1, NHALF - 1)
                                ctx.wait(sync, "t_d", "dmm", ph[0], ph[1])
                            src = hqslot(st).rearrange(
                                "p (k t) -> p k t", k=NKI)
                            dma(hqT_v[:, :, sj], src, f"s_th{st % 2}",
                                labels=[("th", cf, h, sj), ("thg", sg)],
                                transpose=True)
                        for k in range(NKI):
                            kg = (cf * NHALF + h) * NKI + k
                            ctx.wait_if(sync, "t_d", "wdG", kg - 2)
                            if cf == 0:
                                ctx.wait(sync, None, "st", "d", k)
                            dma(wd_rv[:, k % 2], wdq_s[k], f"s_wd{k % 2}",
                                labels=[("wd", cf, h, k)])
            if p2:
                sync.wait_ge(sems["s_out"], ctx.cnt.get("s_out", 0))

        def emit_pool(g):
            p2 = ctx.pass2

            def pstep(f, sem=None, labels=()):
                use = sem or "p_s"
                i = f() if p2 else None
                c = ctx.inc(i, use, 1, labels)
                if p2:
                    g.wait_ge(sems[use], c)

            def dma(out, in_, sem, labels=()):
                i = g.dma_start(out=out, in_=in_) if p2 else None
                ctx.inc(i, sem, 16, labels)

            for hh in range(2):
                dma(xl_v[:, hh], xt_h[0, 0][:, hh * 512:(hh + 1) * 512],
                    f"s_xl{hh}", labels=[("xl", 0, 0, hh)])

            p0ld = []
            p0i2 = 0
            for i in range(NDG):
                def tld(i=i, p0i2=p0i2):
                    if p0i2 == 0:
                        ctx.wait(g, None, "alog_ld")
                        ctx.wait(g, None, "negAexp")
                    if p0i2 == 1:
                        ctx.wait(g, None, "alog_ld")
                    if i >= 2:
                        ctx.wait(g, None, "wos", i - 2)
                    dma(gb(p0i2)[:, 0:D], woT_h[i], f"s_wo{p0i2 % 2}",
                        labels=[("ldwo", i)])
                p0ld.append(tld)
                if i >= 1:
                    def tws(i=i, p0i2=p0i2):
                        ctx.wait(g, None, "ldwo", i - 1)
                        dma(woq_s[i - 1], gb(p0i2 - 1)[:, 0:D],
                            f"s_ws{(i - 1) % 4}", labels=[("wos", i - 1)])
                    p0ld.append(tws)
                p0i2 += 1
            def twsl(p0i2=p0i2):
                ctx.wait(g, None, "ldwo", NDG - 1)
                dma(woq_s[NDG - 1], gb(p0i2 - 1)[:, 0:D],
                    f"s_ws{(NDG - 1) % 4}", labels=[("wos", NDG - 1)])
            p0ld.append(twsl)
            for w in ("g", "u", "d"):
                for i in range(NW_T[w]):
                    def ta2(w=w, i=i, p0i2=p0i2):
                        if p0i2 in (NDG, NDG + 1):
                            ctx.wait(g, None, "wos", p0i2 - 2)
                        ctx.wait_if(g, None, "p0c", p0i2 - 2)
                        dma(gb(p0i2)[:, 0:NW_FD[w]], WSRC[w][i],
                            f"s_pa{p0i2 % 2}", labels=[("ldA", w, i)])
                    p0ld.append(ta2)
                    p0i2 += 1
            PB_UNITS2 = []
            for w in ("g", "u"):
                for i in range(NW_T[w]):
                    for q in range(4):
                        PB_UNITS2.append((w, i, q))
            for i in range(NW_T["d"]):
                PB_UNITS2.append(("d", i, 0))
            for ug, (w, i, q) in enumerate(PB_UNITS2):
                def tbu2(w=w, i=i, q=q, ug=ug):
                    s = ug % 4
                    if ug >= 4:
                        ctx.wait_if(g, None, "pBu", ug - 4)
                    else:
                        ctx.wait(g, "v_s", "pA", "d", NW_T["d"] - 1)
                    srcu = (WSRC[w][i][:, q * D:(q + 1) * D] if w != "d"
                            else WSRC["d"][i])
                    dma(gbufB[:, s * D:(s + 1) * D], srcu,
                        f"s_pb{ug % 4}", labels=[("ldU", ug)])
                p0ld.append(tbu2)

            n_p0ld = len(p0ld)

            def p0ld_drain(kmax, cap=None):
                k = 0
                while p0ld and k < kmax:
                    if cap is not None and n_p0ld - len(p0ld) >= cap:
                        return
                    p0ld.pop(0)()
                    k += 1

            for c in range(SEG):
                cf = c - 1
                if c < NCH:
                    for j in range(1, NTJ):
                        for hh in range(2):
                            hg = (c * NTJ + j) * 2 + hh
                            ctx.wait_if(g, "v_ln", "xlh", hg - 2)
                            dma(xl_v[:, hh],
                                xt_h[c, j][:, hh * 512:(hh + 1) * 512],
                                f"s_xl{hh}", labels=[("xl", c, j, hh)])
                if c + 1 < NCH:
                    for hh in range(2):
                        hg = ((c + 1) * NTJ) * 2 + hh
                        ctx.wait_if(g, "v_ln", "xlh", hg - 2)
                        dma(xl_v[:, hh],
                            xt_h[c + 1, 0][:, hh * 512:(hh + 1) * 512],
                            f"s_xl{hh}", labels=[("xl", c + 1, 0, hh)])
                if 2 <= c <= NCH + 1:
                    ctx.wait(g, "v_dev", "dev", c - 2)
                    ov = out_h.rearrange("(c j p) d -> c j p d", c=NCH, j=NTJ)[c - 2]
                    for j in range(NTJ):
                        dma(ov[j], r2_v[:, j], "s_out",
                            labels=[("out", c - 2)] if j == NTJ - 1 else ())
                if 1 <= c <= NCH:
                    for j in range(NTJ):
                        for oh in range(2):
                            k = cf * NTJ * 2 + j * 2 + oh
                            ctx.wait_if(g, "v_r2", "r2G", k - 2)
                            dma(xr_v[:, (j * 2 + oh) % 2],
                                xt_h[cf, j][:, oh * 512:(oh + 1) * 512],
                                f"s_xr{(j * 2 + oh) % 2}",
                                labels=[("xr", cf, j, oh)])
                if c == 2:
                    p0ld_drain(10 ** 9)
                if c < NCH:
                    for n in range(NST):
                        ctx.wait(g, "v_ym", "ym", c, n)
                        if n == 0:
                            if c > 0:
                                ctx.wait(g, None, "yT", c - 1)
                            pstep(lambda: g.tensor_copy(out=y_acc[:], in_=yp_all[:]),
                                  "p_ya", [("ya", c, n)])
                        else:
                            pstep(lambda: g.tensor_tensor(
                                out=y_acc[:], in0=y_acc[:], in1=yp_all[:], op=Op.add),
                                "p_ya", [("ya", c, n)])
                        p0ld_drain(8, cap=(2 * NDG + sum(NW_T.values()) + 4)
                                   if c == 0 else None)
                if c < NCH:
                    if c >= 1:
                        ctx.wait(g, "t_wout", "wout", c - 1, NTJ - 1, 1)
                    pstep(lambda: g.tensor_copy(out=yT_b[:], in_=y_acc[:]),
                          "p_ya", [("yT", c)])


        def emit_pe(t):
            p2 = ctx.pass2

            def mm(out, lhsT, rhs, start, stop, sem=None, labels=()):
                i = t.matmul(out=out, lhsT=lhsT, rhs=rhs, start=start, stop=stop) if p2 else None
                if sem:
                    ctx.inc(i, sem, 1, labels)

            for c in range(SEG):
                cf = c - 1
                if c < NCH:
                    ctx.wait(t, "s_tl", "tl", c, NTJ - 1)
                    if c >= 1:
                        ctx.wait(t, "a_bce", "bceB", c - 1, NST - 1)
                    for dg in range(NDG):
                        mm(pr[0][0:32, 0:TT], wxpT_v[:, dg],
                           xlnT_v[:, dg].rearrange("p j t -> p (j t)"), dg == 0, False)
                    mm(pr[0][0:32, 0:TT], bxp_b[:], ones1[:], False, True,
                       "t_bc", [("bc", c)])
                    for dg in range(NDG):
                        pp = pr[dg % 2]
                        if dg == 0:
                            ctx.wait(t, "v_bce", "bce", c)
                        elif dg == 1:
                            if c >= 1:
                                ctx.wait(t, "a_bce", "bceC", c - 1, NST - 1)
                        else:
                            ctx.wait(t, "a_spp", "spp", c, dg - 2)
                        mm(pp[:, 0:TT], wdt_b[0:1, dg * 128:(dg + 1) * 128],
                           xlnT[0:1, 0:TT], True, False)
                        mm(pp[:, 0:TT], bdt_b[0:1, dg * 128:(dg + 1) * 128],
                           ones1[:], False, True, "t_dtz", [("dtz", c, dg)])

                def bbc_pair(n, c=c):
                    if n == 0:
                        ctx.wait(t, "a_spp", "spp", c, NDG - 2)
                        ctx.wait(t, "v_bce", "bce", c)
                    else:
                        ctx.wait(t, "a_bce", "bceB", c, n - 1)
                    mm(pr[0][:, 0:TT], oh_v[:, n], bct[:], True, True,
                       "t_bbc", [("bbcB", c, n)])
                    if n == 0:
                        ctx.wait(t, "a_spp", "spp", c, NDG - 1)
                    else:
                        ctx.wait(t, "a_bce", "bceC", c, n - 1)
                    mm(pr[1][:, 0:TT], oh_v[:, NST + n], bct[:], True, True,
                       "t_bbc", [("bbcC", c, n)])

                if c < NCH:
                    bbc_pair(0)
                    if c == 0:
                        for n in range(1, NST):
                            bbc_pair(n)
                        ctx.wait(t, "v_s", "partials")
                        mm(pd[0][:, 0:4], o128[:], wsum[:, 0:4], True, True,
                           "t_p0", [("tot",)])
                if 1 <= c <= NCH:
                    ctx.wait(t, None, "yT", cf)
                    for rnd in range(2):
                        for dg in range(NDG):
                            wog = (cf * 2 + rnd) * NDG + dg
                            ctx.wait(t, None, "wol", cf, rnd, dg)
                            for b in range(4):
                                k = rnd * 4 + b
                                j, oh = k // 2, k % 2
                                if dg == 0:
                                    if rnd == 0:
                                        if c >= 2:
                                            ctx.wait(t, "v_dev", "devst",
                                                     c - 2, 1, b // 2, b % 2)
                                        elif b == 0:
                                            ctx.wait(t, "v_s", "cols")
                                    else:
                                        ctx.wait(t, "v_r2", "r2",
                                                 cf, b // 2, b % 2)
                                mm(pd[b][:], yT_v[:, dg, j],
                                   woT_rv[:, wog % 4, oh * 512:(oh + 1) * 512],
                                   dg == 0, False,
                                   "t_wout" if b == 3 else None,
                                   [("wocG", wog)])
                        for b in range(4):
                            k = rnd * 4 + b
                            j, oh = k // 2, k % 2
                            mm(pd[b][:], ones1[0:1, 0:128],
                               bout_b[0:1, oh * 512:(oh + 1) * 512], False, True,
                               "t_wout", [("wout", cf, j, oh)])
                if 1 <= c < NCH:
                    bbc_pair(1)
                    bbc_pair(2)
                    if c == 1:
                        for m in range(3, NST):
                            bbc_pair(m)
                for gi in range(NGI):
                    h, ic = divmod(gi, NIC)
                    if 1 <= c <= NCH and gi == 0:
                        ctx.wait(t, "s_tq", "tq", cf, 1)
                    if 1 <= c <= NCH and gi == NIC:
                        ctx.wait(t, "s_tq1", "tq", cf, 3)
                    if 1 <= c <= NCH:
                        gg = cf * NGI + gi
                        ctx.wait(t, None, "wg", cf, gi)
                        ctx.wait(t, None, "wu", cf, gi)
                        for sj in range(STP):
                            st = h * STP + sj
                            idx = gi * STP + sj
                            pidx = cf * NPAIR + idx
                            if gi == 0 and sj == 1:
                                ctx.wait(t, "v_r2", "r2", cf, 2, 0)
                                ctx.wait(t, "v_r2", "r2", cf, 2, 1)
                            if gi == 1 and sj == 0:
                                ctx.wait(t, "v_r2", "r2", cf, 3, 0)
                                ctx.wait(t, "v_r2", "r2", cf, 3, 1)
                            ctx.wait_if(t, "a_rel", "relG", pidx - 3)
                            for dg in range(NDG):
                                mm(pg_of(idx)[:, 0:ICW], xqT_v[:, dg, st],
                                   wg_rv[:, gi % 2, dg],
                                   dg == 0, dg == NDG - 1,
                                   "t_g" if dg == NDG - 1 else None,
                                   [("g", cf, idx)])
                            ctx.wait_if(t, "v_h", "hG", pidx - 3)
                            for dg in range(NDG):
                                labs = [("u", cf, idx)]
                                if sj == STP - 1:
                                    labs += [("gu", cf, gi), ("guG", gg)]
                                mm(pu_of(idx)[:, 0:ICW], xqT_v[:, dg, st],
                                   wu_rv[:, gi % 2, dg],
                                   dg == 0, dg == NDG - 1,
                                   "t_u" if dg == NDG - 1 else None, labs)
                    if 2 <= c < NCH and gi % 2 == 1:
                        bbc_pair((gi + 5) // 2)
                if 1 <= c <= NCH:
                    m0 = (NGI - 1 + 5) // 2 + 1
                    for h in range(NHALF):
                        ctx.wait(t, None, "th", cf, h, 0)
                        ctx.wait(t, None, "th", cf, h, STP - 1)
                        for k in range(NKI):
                            if (h == 0 and 2 <= c < NCH and k % 6 == 4
                                    and m0 + (k - 4) // 6 < NST):
                                bbc_pair(m0 + (k - 4) // 6)
                            ctx.wait(t, "s_wd", "wd", cf, h, k)
                            kg = (cf * NHALF + h) * NKI + k
                            for sj in range(STP):
                                for dh in range(2):
                                    pp = pd[sj * 2 + dh]
                                    if k == 0:
                                        if h == 0:
                                            ctx.wait(t, "v_r2", "r2",
                                                     cf, 2 + sj, dh)
                                            b = sj * 2 + dh
                                            li = (NPAIR - 1 if b in (0, 1)
                                                  else NPAIR - 3)
                                            if dh == 0:
                                                ctx.wait(t, "a_rel", "rel",
                                                         cf, li)
                                            else:
                                                ctx.wait(t, "v_h", "h",
                                                         cf, li)
                                        else:
                                            ctx.wait(t, "v_dev", "devst",
                                                     cf, 0, sj, dh)
                                    labs = [("wdG", kg)]
                                    if k == NKI - 1:
                                        labs.append(("dmm", cf, h))
                                    mm(pp[:], hqT_v[:, k, sj],
                                       wd_rv[:, k % 2, dh * 512:(dh + 1) * 512],
                                       k == 0, k == NKI - 1,
                                       "t_d" if (sj == STP - 1 and dh == 1) else None,
                                       labs)
                        if h == 0 and 2 <= c < NCH:
                            for m in range(m0 + 5, NST):
                                bbc_pair(m)

        def emit_act(a):
            p2 = ctx.pass2

            def adma(out, in_, sem, labels=()):
                i = a.dma_start(out=out, in_=in_) if p2 else None
                ctx.inc(i, sem, 16, labels)

            pstq = []
            PB_UNITS3 = []
            for w in ("g", "u"):
                for i in range(NW_T[w]):
                    for q in range(4):
                        PB_UNITS3.append((w, i, q))
            for i in range(NW_T["d"]):
                PB_UNITS3.append(("d", i, 0))
            for ug, (w, i, q) in enumerate(PB_UNITS3):
                def tst(w=w, i=i, q=q, ug=ug):
                    s = ug % 4
                    ctx.wait(a, "v_s", "pBu", ug)
                    srcu = wd_rv[:, ug % 2]
                    sem = "s_st" if ug % 4 == 0 else f"s_st{ug % 4}"
                    labs = [("stu", ug)]
                    if w == "d":
                        labs.append(("st", "d", i))
                        adma(WSCR["d"][i], srcu, sem, labels=labs)
                    else:
                        if q == 3:
                            labs.append(("st", w, i))
                        adma(WSCR[w][2 * q:2 * q + 2, i]
                             .rearrange("ic p t -> p ic t"),
                             srcu.rearrange("p (ic t) -> p ic t", ic=2),
                             sem, labels=labs)
                pstq.append(tst)

            def pst_drain(kmax):
                k = 0
                while pstq and k < kmax:
                    pstq.pop(0)()
                    k += 1

            def wol_round(cf, rnd):
                for dg in range(NDG):
                    wog = (cf * 2 + rnd) * NDG + dg
                    ctx.wait_if(a, None, "wocG", wog - 4)
                    if cf == 0 and rnd == 0:
                        ctx.wait(a, f"s_ws{dg % 4}", "wos", 4 + dg % 4)
                    adma(woT_rv[:, wog % 4], woq_s[dg], f"s_wl{wog % 4}",
                         labels=[("wol", cf, rnd, dg)])

            def astep(f, sem=None, labels=()):
                use = sem or "a_s"
                i = f() if p2 else None
                c = ctx.inc(i, use, 1, labels)
                if p2:
                    a.wait_ge(sems[use], c)

            def act(out, in_, func, bias=None, scale=None):
                kw = {}
                if bias is not None:
                    kw["bias"] = bias
                if scale is not None:
                    kw["scale"] = scale
                return a.activation(out=out, in_=in_, func=func, **kw)

            def relu_step(cf, idx):
                pidx = cf * NPAIR + idx
                ctx.wait(a, "t_g", "g", cf, idx)
                ctx.wait_if(a, "v_h", "hG", pidx - 3)
                astep(lambda idx=idx: act(r_rv[:, idx % 3],
                                          pg_of(idx)[:, 0:ICW], AF.Relu),
                      None, [("relG0", pidx)])
                astep(lambda idx=idx: act(r_rv[:, idx % 3], r_rv[:, idx % 3],
                                          AF.Square),
                      "a_rel", [("rel", cf, idx), ("relG", pidx)])

            act_queue = []
            act_slot = [0]

            def hq2_drain(kmax):
                k = 0
                while act_queue and k < kmax:
                    cfq, st, sj, q = act_queue.pop(0)
                    qg = act_slot[0]
                    act_slot[0] += 1
                    h = st // STP
                    lastpiece = 2 * (h * NIC + (q * 512 + 511) // ICW) + sj
                    ctx.wait(a, "v_h", "h", cfq, lastpiece)
                    ctx.wait(a, None, "qcol", cfq, st)
                    if q == 0:
                        ctx.wait(a, None, "xq", cfq, NTJ - 1)
                    astep(lambda st=st, q=q, qg=qg, cfq=cfq, sj=sj: act(
                        t1q[:],
                        hf_v[:, sj][:, q * 512:(q + 1) * 512],
                        AF.Identity, scale=qcol_s(cfq, st), bias=ccol[:, 3:4]),
                          None, [("hq1", cfq, st, q)])
                    if q == 0:
                        sg = cfq * NTJ + st
                        ctx.wait_if(a, "s_th", "thg", sg - 2)
                    if st % 2 == 1 and q == 0:
                        ctx.wait_if(a, None, "u", cfq + 1)
                        ctx.wait_if(a, None, "dtz", cfq + 1, NDG - 1)
                    if st % 2 == 0 and q == 0:
                        ctx.wait_if(a, None, "dt", cfq + 1, NDG - 1)
                    labels = []
                    if q == NQ - 1:
                        labels.append(("hq2", cfq, st))
                    astep(lambda st=st, q=q, qg=qg: act(
                        hqslot(st)[:, q * 512:(q + 1) * 512],
                        t1q[:], AF.Identity, bias=ccol[:, 2:3]),
                          "a_hq2", labels)
                    k += 1

            ctx.wait(a, "v_s", "alog_ld")
            astep(lambda: act(negA[:], gbufA[:, 512:512 + NDG * NST], AF.Exp),
                  "a_rstd", [("negAexp",)])

            for c in range(SEG):
                cf = c - 1
                if c < NCH:
                    for j in range(NTJ):
                        ctx.wait(a, "v_ln", "ln1", c, j)
                        astep(lambda j=j: act(l1(j, 1), l1(j, 1), AF.Ln,
                                              bias=ccol[:, 0:1]))
                        astep(lambda j=j: act(l1(j, 1), l1(j, 1), AF.Exp,
                                              scale=ccol[:, 1:2]),
                              "a_rstd", [("rstd1", c, j)])
                if c < NCH:
                    for dg in range(NDG):
                        pp = pr[dg % 2]
                        ctx.wait(a, "t_dtz", "dtz", c, dg)
                        if dg >= 1:
                            ctx.wait(a, None, "dt", c, dg - 1)
                        elif c >= 1:
                            ctx.wait(a, None, "dt", c - 1, NDG - 1)
                        if dg == 0 and c >= 2:
                            ctx.wait_if(a, None, "th", c - 2, 1, 0)
                        astep(lambda pp=pp: act(sp1[:], pp[:, 0:TT], AF.Abs))
                        astep(lambda pp=pp: act(sp2[:], pp[:, 0:TT], AF.Relu),
                              "a_spp", [("spp", c, dg)])
                        astep(lambda: act(sp1[:], sp1[:], AF.Exp, scale=-1.0))
                        astep(lambda: act(sp1[:], sp1[:], AF.Ln, bias=1.0),
                              "a_sp", [("sp", c, dg)])
                if c == 2:
                    pst_drain(10 ** 9)
                if 1 <= c <= NCH:
                    wol_round(cf, 0)
                    wol_round(cf, 1)
                if 1 <= c <= NCH:
                    for j in range(NTJ):
                        ctx.wait(a, "v_ln", "ln2", cf, j)
                        astep(lambda j=j: act(l2(j, 1), l2(j, 1), AF.Ln,
                                              bias=ccol[:, 0:1]))
                        astep(lambda j=j: act(l2(j, 1), l2(j, 1), AF.Exp,
                                              scale=ccol[:, 1:2]),
                              "a_rstd", [("rstd2", cf, j)])
                if c < NCH:
                    for n in range(NST):
                        m = c * NST + n
                        ctx.wait(a, "t_bbc", "bbcB", c, n)
                        ctx.wait_if(a, "v_dB", "dBG", m - 2)
                        astep(lambda n=n: act(bcB_v[:, n % 2], pr[0][:, 0:TT],
                                              AF.Copy),
                              "a_bce", [("bceB", c, n)])
                        ctx.wait(a, "t_bbc", "bbcC", c, n)
                        ctx.wait_if(a, "v_ym", "ymG", m - 2)
                        astep(lambda n=n: act(bcC_v[:, n % 2], pr[1][:, 0:TT],
                                              AF.Copy),
                              "a_bce", [("bceC", c, n)])
                        for dg in range(NDG):
                            gidx = m * NDG + dg
                            ctx.wait(a, "v_dt", "dt", c, dg)
                            ctx.wait_if(a, "v_scan", "scani", gidx - 2)
                            astep(lambda n=n, dg=dg, gidx=gidx: act(
                                dA_v[:, gidx % 2], dt_v[:, dg], AF.Exp,
                                scale=negA[:, dg * NST + n:dg * NST + n + 1]),
                                "s_dA", [("dA", c, n, dg)])
                        if c == 1:
                            pst_drain(8)
                        if 1 <= c <= NCH:
                            ps = 99 if c == 1 else 2
                            rs = 99 if c == 1 else 2
                            if n == ps + 4:
                                act_queue.extend(hq_entries(cf, 0))
                            if c >= 2 and n == 10:
                                act_queue.extend(hq_entries(cf, 1))
                            if n >= ps + 4:
                                hq2_drain(4)
                            if n >= rs and 4 * (n - rs) < NPAIR:
                                for q in range(4):
                                    relu_step(cf, 4 * (n - rs) + q)
                elif 1 <= c <= NCH:
                    for idx in range(NPAIR):
                        relu_step(cf, idx)
                        if idx == NPAIR // 2:
                            act_queue.extend(hq_entries(cf, 0))
                        if idx >= NPAIR // 2:
                            hq2_drain(1)
                if c == 1:
                    pst_drain(10 ** 9)
                    for idx in range(NPAIR):
                        relu_step(cf, idx)
                        if idx == NPAIR // 2:
                            act_queue.extend(hq_entries(cf, 0))
                        if idx >= NPAIR // 2:
                            hq2_drain(1)
                if 1 <= c <= NCH:
                    hq2_drain(10 ** 9)
                    if c == 1 or c == NCH:
                        act_queue.extend(hq_entries(cf, 1))
                        hq2_drain(10 ** 9)

        def hq_entries(cf, h):
            out = []
            for q in range(NQ):
                for sj in range(STP):
                    out.append((cf, h * STP + sj, sj, q))
            return out


        def vstep_factory(v):
            def vstep(f, sem=None, labels=()):
                use = sem or "v_s"
                i = f() if ctx.pass2 else None
                c = ctx.inc(i, use, 1, labels)
                if ctx.pass2:
                    v.wait_ge(sems[use], c)
            return vstep

        def emit_dve(v):
            p2 = ctx.pass2
            vstep = vstep_factory(v)

            def ts(out, in0, s1, s2, op0, op1):
                return v.tensor_scalar(out=out, in0=in0, scalar1=s1, scalar2=s2,
                                       op0=op0, op1=op1)

            ctx.wait(v, "s_scr", "c0")
            vstep(lambda: v.memset(ccol[:, 0:1], EPS_LN))
            vstep(lambda: v.memset(ccol[:, 1:2], -0.5))
            vstep(lambda: v.memset(ccol[:, 2:3], -MAGIC))
            vstep(lambda: v.memset(ccol[:, 3:4], MAGIC))
            vstep(lambda: v.memset(carr_v[:, 1], 0.0))
            vstep(lambda: v.memset(wsum[:], 0.0))
            vstep(lambda: v.tensor_copy(out=wxpT_b[:], in_=gbufA[:, 0:NDG * 32]))
            vstep(lambda: v.tensor_copy(out=wdt_b[:], in_=gbufB[0:1, 0:D]))
            vstep(lambda: v.tensor_copy(out=bdt_b[:], in_=gbufB[0:1, D:2 * D]))
            vstep(lambda: v.tensor_copy(out=bxp_b[:], in_=gbufB[0:1, 2 * D:2 * D + 32]))
            vstep(lambda: v.tensor_copy(out=bout_b[:], in_=gbufB[0:1, 3 * D:4 * D]),
                  labels=[("alog_ld",)])
            ctx.wait(v, "a_rstd", "negAexp")
            vstep(lambda: ts(negA[:], negA[:], -1.0, 0.0, Op.mult, Op.add))
            p0q = []
            p0i = NDG
            for w in ("g", "u", "d"):
                col = WCOL[w]
                for i in range(NW_T[w]):
                    def ta(w=w, i=i, p0i=p0i, col=col):
                        ctx.wait(v, None, "ldA", w, i)
                        vstep(lambda: v.tensor_reduce(
                            out=wsum[:, 3:4], in_=gb(p0i)[:, 0:NW_FD[w]],
                            axis=AX.X, op=Op.add, apply_absolute_value=True),
                            labels=[("p0c", p0i)])
                        vstep(lambda: v.tensor_tensor(
                            out=wsum[:, col:col + 1], in0=wsum[:, col:col + 1],
                            in1=wsum[:, 3:4], op=Op.add), labels=[("pA", w, i)])
                    p0q.append(ta)
                    p0i += 1

            def tpart():
                vstep(lambda: v.memset(wsum[:, 3:4], 0.0),
                      labels=[("partials",)])
            p0q.append(tpart)
            NG = float(NDG * 128 * NI)
            NDW = float(NKI * 128 * D)

            def tmeans():
                ctx.wait(v, "t_p0", "tot")
                vstep(lambda: ts(mwc[:, 5:6], pd[0][:, 0:1], 1.0 / NG, EPS_Q,
                                 Op.mult, Op.max))
                vstep(lambda: ts(mwc[:, 6:7], pd[0][:, 1:2], 1.0 / NG, EPS_Q,
                                 Op.mult, Op.max))
                vstep(lambda: ts(mwc[:, 3:4], pd[0][:, 2:3], 1.0 / NDW, EPS_Q,
                                 Op.mult, Op.max))
                vstep(lambda: v.reciprocal(out=mwc[:, 0:1], in_=mwc[:, 5:6]))
                vstep(lambda: v.reciprocal(out=mwc[:, 1:2], in_=mwc[:, 6:7]))
                vstep(lambda: v.reciprocal(out=mwc[:, 2:3], in_=mwc[:, 3:4]))
                vstep(lambda: v.tensor_tensor(out=mwc[:, 4:5], in0=mwc[:, 5:6],
                                              in1=mwc[:, 5:6], op=Op.mult))
                vstep(lambda: v.tensor_tensor(out=mwc[:, 4:5], in0=mwc[:, 4:5],
                                              in1=mwc[:, 6:7], op=Op.mult),
                      labels=[("cols",)])
            p0q.append(tmeans)
            PB_UNITS = []
            for w in ("g", "u"):
                for i in range(NW_T[w]):
                    for q in range(4):
                        PB_UNITS.append((w, i, q))
            for i in range(NW_T["d"]):
                PB_UNITS.append(("d", i, 0))
            for ug, (w, i, q) in enumerate(PB_UNITS):
                def tbu(w=w, i=i, q=q, ug=ug):
                    col = WCOL[w]
                    s = ug % 4
                    gbd = gbufB[:, s * D:(s + 1) * D]
                    g16 = wd_rv[:, ug % 2]
                    ctx.wait(v, None, "ldU", ug)
                    if ug >= 2:
                        ctx.wait(v, None, "stu", ug - 2)
                    vstep(lambda: ts(gbd, gbd, mwc[:, col:col + 1], MAGIC,
                                     Op.mult, Op.add))
                    vstep(lambda: ts(gbd, gbd, MAGIC, -1.0,
                                     Op.subtract, Op.max))
                    labs = [("pBu", ug)]
                    if w == "d" or q == 3:
                        labs.append(("pBq", w, i))
                    vstep(lambda: ts(g16, gbd, 1.0, 0.0, Op.min, Op.add),
                          labels=labs)
                p0q.append(tbu)

            n_p0q = len(p0q)

            def p0_drain(kmax, cap=None):
                k = 0
                while p0q and k < kmax:
                    if cap is not None and n_p0q - len(p0q) >= cap:
                        return
                    p0q.pop(0)()
                    k += 1

            for c in range(SEG):
                cf = c - 1
                if c == 1:
                    p0_drain(2)
                if c == 2:
                    p0_drain(10 ** 9)
                if c < NCH:
                    for j in range(NTJ):
                        for hh in range(2):
                            ctx.wait(v, None, "xl", c, j, hh)
                            vstep(lambda j=j, hh=hh: v.bn_stats(
                                out=bns[:, hh * 6:hh * 6 + 6], in_=xl_v[:, hh]))
                        vstep(lambda j=j: v.bn_aggr(
                            out=lncol[:, j * 8:j * 8 + 2], in_=bns[:]))
                        vstep(lambda j=j: ts(l1(j, 2), l1(j, 0), -1.0, 0.0,
                                             Op.mult, Op.add),
                              "v_ln", [("ln1", c, j)])
                        ctx.wait(v, "a_rstd", "rstd1", c, j)
                        if c >= 2:
                            ctx.wait(v, "s_tq", "tq", c - 2, NTJ - 1)
                        jg = c * NTJ + j
                        for hh in range(2):
                            hg = (c * NTJ + j) * 2 + hh
                            labs = [("xlh", hg)]
                            if hh == 1:
                                labs += [("xln", c, j), ("xlnG", jg)]
                            vstep(lambda j=j, c=c, hh=hh: ts(
                                tok_v[:, j][:, hh * 512:(hh + 1) * 512],
                                xl_v[:, hh], l1(j, 2), l1(j, 1),
                                Op.add, Op.mult),
                                "v_ln" if hh == 0 else "v_xln", labs)
                if 2 <= c <= NCH + 1:
                    cd = c - 2
                    ctx.wait(v, "t_d", "dmm", cd, 1)
                    for sj in range(STP):
                        st = 2 + sj
                        for dh in range(2):
                            def f(sj=sj, st=st, dh=dh, cd=cd):
                                sl = r2_v[:, st][:, dh * 512:(dh + 1) * 512]
                                return v.scalar_tensor_tensor(
                                    out=sl, in0=pd[sj * 2 + dh][:],
                                    scalar=qcol_f(cd, st), in1=sl,
                                    op0=Op.mult, op1=Op.add)
                            labs = [("devst", cd, 1, sj, dh)]
                            if sj == STP - 1 and dh == 1:
                                labs.append(("dev", cd))
                            vstep(f, "v_dev", labs)
                if c < NCH:
                    ctx.wait(v, "t_bc", "bc", c)
                    vstep(lambda: v.tensor_copy(out=bct[:], in_=pr[0][0:32, 0:TT]),
                          "v_bce", [("bce", c)])
                    for dg in range(NDG):
                        ctx.wait(v, "a_sp", "sp", c, dg)
                        vstep(lambda dg=dg: v.tensor_tensor(
                            out=dt_v[:, dg], in0=sp1[:], in1=sp2[:], op=Op.add),
                            "v_dt", [("dt", c, dg)])
                    ctx.wait(v, "s_tl", "tl", c, NTJ - 1)
                    vstep(lambda: v.tensor_tensor(
                        out=u_all[:], in0=dt_all[:], in1=xlnT[:], op=Op.mult),
                        "v_u", [("u", c)])
                if 1 <= c <= NCH:
                    for j in range(NTJ):
                        for oh in range(2):
                            k = cf * NTJ * 2 + j * 2 + oh
                            ctx.wait(v, "t_wout", "wout", cf, j, oh)
                            ctx.wait(v, "s_x", "xr", cf, j, oh)
                            if j == 0 and oh == 0 and cf >= 1:
                                ctx.wait(v, "s_out", "out", cf - 1)
                            def f(j=j, oh=oh):
                                sl = r2_v[:, j][:, oh * 512:(oh + 1) * 512]
                                return v.scalar_tensor_tensor(
                                    out=sl, in0=pd[(j * 2 + oh) % 4][:], scalar=1.0,
                                    in1=xr_v[:, (j * 2 + oh) % 2],
                                    op0=Op.mult, op1=Op.add)
                            labs = [("r2", cf, j, oh), ("r2G", k)]
                            if j == NTJ - 1 and oh == 1:
                                labs.append(("r2done", cf))
                            vstep(f, "v_r2", labs)
                    for j in range(NTJ):
                        vstep(lambda j=j: v.bn_stats(out=bns[:, 0:6],
                                                     in_=r2_v[:, j][:, 0:512]))
                        vstep(lambda j=j: v.bn_stats(out=bns[:, 6:12],
                                                     in_=r2_v[:, j][:, 512:1024]))
                        vstep(lambda j=j: v.bn_aggr(
                            out=lncol[:, 32 + j * 8:32 + j * 8 + 2], in_=bns[:]))
                        vstep(lambda j=j: ts(l2(j, 2), l2(j, 0), -1.0, 0.0,
                                             Op.mult, Op.add),
                              "v_ln", [("ln2", cf, j)])
                        vstep(lambda j=j: v.tensor_reduce(
                            out=l2(j, 3), in_=r2_v[:, j], axis=AX.X, op=Op.max))
                        vstep(lambda j=j: v.tensor_reduce(
                            out=l2(j, 4), in_=r2_v[:, j], axis=AX.X, op=Op.min))
                        vstep(lambda j=j: v.scalar_tensor_tensor(
                            out=l2(j, 3), in0=l2(j, 3), scalar=1.0, in1=l2(j, 2),
                            op0=Op.mult, op1=Op.add))
                        vstep(lambda j=j: v.scalar_tensor_tensor(
                            out=l2(j, 4), in0=l2(j, 4), scalar=-1.0, in1=l2(j, 2),
                            op0=Op.mult, op1=Op.subtract))
                        vstep(lambda j=j: v.tensor_tensor(
                            out=l2(j, 3), in0=l2(j, 3), in1=l2(j, 4), op=Op.max))
                    for j in range(NTJ):
                        ctx.wait(v, "a_rstd", "rstd2", cf, j)
                        vstep(lambda j=j: v.tensor_tensor(
                            out=l2(j, 3), in0=l2(j, 3), in1=l2(j, 1), op=Op.mult))
                        vstep(lambda j=j: ts(l2(j, 3), l2(j, 3), EPS_Q, 0.0,
                                             Op.max, Op.add))
                        vstep(lambda j=j: v.reciprocal(out=l2(j, 5), in_=l2(j, 3)))
                        vstep(lambda j=j: ts(l2(j, 5), l2(j, 5), 127.0, 0.0,
                                             Op.mult, Op.add))
                        vstep(lambda j=j: v.tensor_tensor(
                            out=l2(j, 6), in0=l2(j, 1), in1=l2(j, 5), op=Op.mult))
                        vstep(lambda j=j: ts(l2(j, 3), l2(j, 3), 1.0 / 127.0, 0.0,
                                             Op.mult, Op.add))
                        vstep(lambda j=j: v.tensor_tensor(
                            out=l2(j, 4), in0=l2(j, 3), in1=l2(j, 3), op=Op.mult))
                        vstep(lambda j=j: v.tensor_tensor(
                            out=l2(j, 4), in0=l2(j, 4), in1=l2(j, 3), op=Op.mult))
                        vstep(lambda j=j: v.tensor_tensor(
                            out=l2(j, 7), in0=l2(j, 4), in1=mwc[:, 4:5], op=Op.mult))
                        if c < NCH:
                            ctx.wait(v, "s_tl", "tl", c, NTJ - 1)
                        else:
                            ctx.wait(v, "s_tq", "tq", cf - 1, NTJ - 1)
                        if cf >= 1 and j == 0:
                            ctx.wait(v, "a_hq2", "hq2", cf - 1, NTJ - 1)
                        for ph in range(2):
                            vstep(lambda j=j, ph=ph: ts(
                                t1q[:], r2_v[:, j][:, ph * 512:(ph + 1) * 512],
                                l2(j, 2), l2(j, 6), Op.add, Op.mult))
                            vstep(lambda j=j, ph=ph: ts(
                                tok_v[:, j][:, ph * 512:(ph + 1) * 512],
                                t1q[:], MAGIC, MAGIC, Op.add, Op.subtract),
                                "v_xq" if ph == 1 else None,
                                [("xq", cf, j)] if ph == 1 else ())
                if c < NCH:
                    for n in range(NST):
                        m = c * NST + n
                        ctx.wait(v, "a_bce", "bceB", c, n)
                        def fdB(n=n):
                            bc = bcB_v[:, n % 2].unsqueeze(1).broadcast_to(
                                [128, NDG, TT])
                            return v.tensor_tensor(out=dB_v, in0=u_v, in1=bc,
                                                   op=Op.mult)
                        vstep(fdB, "v_dB", [("dB", c, n), ("dBG", m)])
                        for dg in range(NDG):
                            gidx = m * NDG + dg
                            ctx.wait(v, "s_dA", "dA", c, n, dg)
                            def fsc(n=n, dg=dg, gidx=gidx, c=c):
                                col = n * NDG + dg
                                return v.tensor_tensor_scan(
                                    out=dB_v[:, dg], data0=dA_v[:, gidx % 2],
                                    data1=dB_v[:, dg],
                                    initial=carr_v[:, (c + 1) % 2][:, col:col + 1],
                                    op0=Op.mult, op1=Op.add)
                            vstep(fsc, "v_scan", [("scani", gidx)])
                        def fc(n=n, c=c):
                            src = dB_all[:].rearrange(
                                "p (k t) -> p k t", k=NDG)[:, :, TT - 1]
                            return v.tensor_copy(
                                out=carr_v[:, c % 2][:, n * NDG:(n + 1) * NDG],
                                in_=src)
                        vstep(fc)
                        ctx.wait(v, "a_bce", "bceC", c, n)
                        if m >= 1:
                            ctx.wait(v, "p_ya", "ya", (m - 1) // NST, (m - 1) % NST)
                        def fy(n=n):
                            bc = bcC_v[:, n % 2].unsqueeze(1).broadcast_to(
                                [128, NDG, TT])
                            return v.tensor_tensor(out=yp_v, in0=dB_v, in1=bc,
                                                   op=Op.mult)
                        vstep(fy, "v_ym", [("ym", c, n), ("ymG", m)])
                        if c * NST + n >= 3:
                            p0_drain(8, cap=(sum(NW_T.values()) + 1)
                                     if c == 0 else None)
                        if 1 <= c <= NCH:
                            ps = 99 if c == 1 else 2
                            if n >= ps and 4 * (n - ps) < NPAIR:
                                for q in range(4):
                                    hstep(v, vstep, cf, 4 * (n - ps) + q, c)
                    if c == 1:
                        for idx in range(NPAIR):
                            hstep(v, vstep, cf, idx, c)
                elif 1 <= c <= NCH:
                    for idx in range(NPAIR):
                        hstep(v, vstep, cf, idx, c)
                if 1 <= c <= NCH:
                    ctx.wait(v, "t_d", "dmm", cf, 0)
                    for sj3 in range(STP):
                        st3 = sj3
                        for dh in range(2):
                            def fd(sj3=sj3, st3=st3, dh=dh, cf=cf):
                                sl = r2_v[:, st3][:, dh * 512:(dh + 1) * 512]
                                return v.scalar_tensor_tensor(
                                    out=sl, in0=pd[sj3 * 2 + dh][:],
                                    scalar=qcol_f(cf, st3), in1=sl,
                                    op0=Op.mult, op1=Op.add)
                            vstep(fd, "v_dev", [("devst", cf, 0, sj3, dh)])

        def hstep(v, vstep, cf, idx, c):
            gi, sj = divmod(idx, STP)
            h, ic = divmod(gi, NIC)
            pidx = cf * NPAIR + idx
            ctx.wait(v, "a_rel", "rel", cf, idx)
            ctx.wait(v, "t_u", "u", cf, idx)
            if cf == 0 and idx == 0:
                ctx.wait(v, None, "pA", "d", NW_T["d"] - 1)
            if h == 1:
                ctx.wait_if(v, None, "hq1", cf, sj, ic * ICW // 512)
            else:
                ctx.wait_if(v, None, "hq1", cf - 1, STP + sj, ic * ICW // 512)
            rv = r_rv[:, idx % 3]

            def fh():
                return v.tensor_tensor(
                    out=hf_v[:, sj][:, ic * ICW:(ic + 1) * ICW],
                    in0=rv, in1=pu_of(idx)[:, 0:ICW], op=Op.mult)
            vstep(fh, "v_h", [("h", cf, idx), ("hG", pidx)])

            def fm():
                return v.tensor_reduce(
                    out=maxp_v[:, sj][:, ic:ic + 1],
                    in_=hf_v[:, sj][:, ic * ICW:(ic + 1) * ICW], axis=AX.X,
                    op=Op.max, apply_absolute_value=True)
            vstep(fm)
            if ic == NIC - 1 and sj == STP - 1:
                for sj2 in range(STP):
                    st2 = h * STP + sj2
                    j = st2
                    vstep(lambda sj2=sj2: v.tensor_reduce(
                        out=lncol[:, 6:7], in_=maxp_v[:, sj2], axis=AX.X,
                        op=Op.max))
                    vstep(lambda j=j: v.reciprocal(out=lncol[:, 7:8],
                                                   in_=l2(j, 7)))
                    vstep(lambda: v.tensor_scalar(
                        out=lncol[:, 7:8], in0=lncol[:, 7:8], scalar1=EPS_Q,
                        scalar2=0.0, op0=Op.mult, op1=Op.add))
                    vstep(lambda: v.tensor_tensor(
                        out=lncol[:, 6:7], in0=lncol[:, 6:7],
                        in1=lncol[:, 7:8], op=Op.max))
                    vstep(lambda cf=cf, st2=st2: v.reciprocal(
                        out=qcol_s(cf, st2), in_=lncol[:, 6:7]))
                    vstep(lambda cf=cf, st2=st2: v.tensor_scalar(
                        out=qcol_s(cf, st2), in0=qcol_s(cf, st2), scalar1=127.0,
                        scalar2=0.0, op0=Op.mult, op1=Op.add),
                        labels=[("qcol", cf, st2)])
                    vstep(lambda cf=cf, st2=st2, j=j: v.tensor_tensor(
                        out=qcol_f(cf, st2), in0=l2(j, 7), in1=lncol[:, 6:7],
                        op=Op.mult))
                    vstep(lambda cf=cf, st2=st2: v.tensor_tensor(
                        out=qcol_f(cf, st2), in0=qcol_f(cf, st2), in1=mwc[:, 3:4],
                        op=Op.mult))
                    vstep(lambda cf=cf, st2=st2: v.tensor_scalar(
                        out=qcol_f(cf, st2), in0=qcol_f(cf, st2),
                        scalar1=1.0 / 127.0, scalar2=0.0, op0=Op.mult, op1=Op.add))



        ctx.start(False, sems)
        emit_dve(None)
        emit_act(None)
        emit_pe(None)
        emit_pool(None)
        emit_sync(None)

        ctx.start(True, sems)

        @block.vector
        def _(vector):
            emit_dve(vector)

        @block.scalar
        def _(scalar):
            emit_act(scalar)

        @block.tensor
        def _(tensor):
            emit_pe(tensor)

        @block.gpsimd
        def _(gpsimd):
            emit_pool(gpsimd)

        @block.sync
        def _(sync):
            emit_sync(sync)

    return nc


def _prep_inputs(A_log, W_dt, b_dt, W_xp, b_xp, W_out, b_out, w_gate, w_up,
                 w_down):
    import ml_dtypes
    f32 = np.float32
    d = {}
    d["wgT"] = np.ascontiguousarray(np.asarray(w_gate, f32).T).reshape(NDG, 128, NI)
    d["wuT"] = np.ascontiguousarray(np.asarray(w_up, f32).T).reshape(NDG, 128, NI)
    d["wdT"] = np.ascontiguousarray(np.asarray(w_down, f32).T).reshape(NKI, 128, D)
    d["woT"] = np.ascontiguousarray(np.asarray(W_out, f32).T).reshape(NDG, 128, D)
    d["wxpT"] = np.ascontiguousarray(
        np.asarray(W_xp, f32).T.reshape(NDG, 128, 32).transpose(1, 0, 2)
        .reshape(128, NDG * 32))
    d["alog"] = np.ascontiguousarray(
        np.asarray(A_log, f32).reshape(NDG, 128, NST).transpose(1, 0, 2)
        .reshape(128, NDG * NST))
    d["wdtr"] = np.ascontiguousarray(np.asarray(W_dt, f32)[:, 0][None, :])
    d["bdtr"] = np.ascontiguousarray(np.asarray(b_dt, f32)[None, :])
    d["bxpr"] = np.ascontiguousarray(np.asarray(b_xp, f32)[None, :])
    d["boutr"] = np.ascontiguousarray(np.asarray(b_out, f32)[None, :])
    oh = np.zeros((32, 32, 128), ml_dtypes.bfloat16)
    for n in range(32):
        oh[n, n, :] = 1
    d["oh32"] = np.ascontiguousarray(oh.transpose(1, 0, 2).reshape(32, 32 * 128))
    d["ones1"] = np.ones((1, TT), ml_dtypes.bfloat16)
    d["o128"] = np.ones((128, 128), np.float32)
    return d


_NC_CACHE = {}


def kernel(L0_attn_x, L0_attn_A_log, L0_attn_W_dt, L0_attn_b_dt, L0_attn_W_xp,
           L0_attn_b_xp, L0_attn_W_out, L0_attn_b_out, L0_ffn_w_gate,
           L0_ffn_w_up, L0_ffn_w_down):
    x = np.asarray(L0_attn_x, np.float32)
    B, T = x.shape[0], x.shape[1]
    if T not in _NC_CACHE:
        _NC_CACHE[T] = build_program(T)
    nc = _NC_CACHE[T]
    shared = _prep_inputs(L0_attn_A_log, L0_attn_W_dt, L0_attn_b_dt, L0_attn_W_xp,
                          L0_attn_b_xp, L0_attn_W_out, L0_attn_b_out,
                          L0_ffn_w_gate, L0_ffn_w_up, L0_ffn_w_down)
    in_maps = []
    for b in range(B):
        m = dict(shared)
        m["x"] = np.ascontiguousarray(x[b])
        in_maps.append(m)
    res = run_bass_kernel_spmd(nc, in_maps, core_ids=list(range(B)))
    return np.stack([np.asarray(r["out"]) for r in res.results], axis=0)


# version 20
# speedup vs baseline: 1.1236x; 1.0097x over previous
import contextlib
import numpy as np
import concourse.bass as bass
import concourse.mybir as mybir
from concourse.bass_utils import run_bass_kernel_spmd
from concourse.alu_op_type import AluOpType as Op

F32 = mybir.dt.float32
BF16 = mybir.dt.bfloat16
AF = mybir.ActivationFunctionType
AX = mybir.AxisListType

MAGIC = float(np.float32(3 * 2**22))
EPS_LN = 1e-5
EPS_Q = 1e-5

D = 1024
NI = 4096
NST = 16
TT = 512
NDG = D // 128
NTJ = TT // 128
NKI = NI // 128
ICW = 512
NIC = NI // ICW
NHALF = 2
STP = NTJ // NHALF
NGI = NHALF * NIC
NPAIR = NGI * STP
PPN = NPAIR // NST
NQ = NI // 512


class Ctx:

    def __init__(self):
        self.lab = {}

    def start(self, pass2, sems=None):
        self.pass2 = pass2
        self.cnt = {}
        self.sems = sems
        self.last = {}

    def inc(self, instr, sem, by=1, labels=(), tag=None):
        c = self.cnt.get(sem, 0) + by
        self.cnt[sem] = c
        if not self.pass2:
            for lb in labels:
                self.lab[tuple(lb)] = (sem, c)
        if self.pass2 and instr is not None:
            instr.then_inc(self.sems[sem], by)
        if tag is not None:
            self.last[tag] = (sem, c)
        return c

    def mark(self, _sem_unused, *label):
        raise RuntimeError("mark replaced by inc labels")

    def wait(self, eng, _sem_unused, *label):
        if not self.pass2:
            return
        ent = self.lab.get(tuple(label))
        assert ent is not None, f"missing label {tuple(label)}"
        sem, v = ent
        if v > 0:
            eng.wait_ge(self.sems[sem], v)

    def wait_if(self, eng, _sem_unused, *label):
        if tuple(label) in self.lab:
            self.wait(eng, None, *label)


def build_program(T):
    NCH = T // TT
    SEG = NCH + 2
    nc = bass.Bass("TRN2", target_bir_lowering=False)

    def din(name, shape, dt=F32):
        return nc.dram_tensor(name, shape, dt, kind="ExternalInput").ap()

    x_h = din("x", [T, D])
    wgT_h = din("wgT", [NDG, 128, NI])
    wuT_h = din("wuT", [NDG, 128, NI])
    wdT_h = din("wdT", [NKI, 128, D])
    woT_h = din("woT", [NDG, 128, D])
    wxpT_h = din("wxpT", [128, NDG * 32])
    alog_h = din("alog", [128, NDG * NST])
    wdt_h = din("wdtr", [1, D])
    bdt_h = din("bdtr", [1, D])
    bxp_h = din("bxpr", [1, 32])
    bout_h = din("boutr", [1, D])
    oh_h = din("oh32", [32, 32 * 128], BF16)
    ones_h = din("ones1", [1, TT], BF16)
    o128_h = din("o128", [128, 128])
    out_h = nc.dram_tensor("out", [T, D], F32, kind="ExternalOutput").ap()

    wgq_s = nc.dram_tensor("wgq_s", [NIC, NDG, 128, ICW], BF16, kind="Internal").ap()
    wuq_s = nc.dram_tensor("wuq_s", [NIC, NDG, 128, ICW], BF16, kind="Internal").ap()
    wdq_s = nc.dram_tensor("wdq_s", [NKI, 128, D], BF16, kind="Internal").ap()
    woq_s = nc.dram_tensor("woq_s", [NDG, 128, D], BF16, kind="Internal").ap()

    ctx = Ctx()
    es = contextlib.ExitStack()
    with es:
        def SB(name, p, f, dt):
            return es.enter_context(nc.sbuf_tensor(name, [p, f], dt))

        def PS(name, p, f):
            return es.enter_context(nc.psum_tensor(name, [p, f], F32))

        woT_r = SB("woT_r", 128, 4 * D, BF16)
        wxpT_b = SB("wxpT_b", 128, NDG * 32, BF16)
        wdt_b = SB("wdt_b", 1, D, BF16)
        bdt_b = SB("bdt_b", 1, D, BF16)
        bxp_b = SB("bxp_b", 1, 32, BF16)
        bout_b = SB("bout_b", 1, D, BF16)
        oh32 = SB("oh32_t", 32, 32 * 128, BF16)
        ones1 = SB("ones1_t", 1, TT, BF16)
        o128 = SB("o128_t", 128, 128, F32)
        negA = SB("negA", 128, NDG * NST, F32)
        ccol = SB("ccol", 128, 4, F32)
        mwc = SB("mwc", 128, 8, F32)
        wsum = SB("wsum", 128, 4, F32)

        xl_r = SB("xl_r", 128, 2 * 512, F32)
        xr_r = SB("xr_r", 128, 2 * 512, F32)
        r2 = SB("r2", 128, NTJ * D, BF16)
        tok = SB("tok", 128, NTJ * D, BF16)
        xlnT = SB("xlnT", 128, NDG * TT, BF16)
        xqT = SB("xqT", 128, NDG * TT, BF16)
        dt_all = SB("dt_all", 128, NDG * TT, BF16)
        u_all = SB("u_all", 128, NDG * TT, BF16)
        bct = SB("bct", 32, TT, BF16)
        bcB = SB("bcB", 128, 2 * TT, BF16)
        bcC = SB("bcC", 128, 2 * TT, BF16)
        dA_r = SB("dA_r", 128, 2 * TT, F32)
        dB_all = SB("dB_all", 128, NDG * TT, BF16)
        yp_all = SB("yp_all", 128, NDG * TT, BF16)
        y_acc = SB("y_acc", 128, NDG * TT, BF16)
        yT_b = SB("yT_b", 128, NDG * TT, BF16)
        carr = SB("carr", 128, 2 * 128, BF16)

        sp2 = SB("sp2", 128, TT, BF16)
        lncol = SB("lncol", 128, 64, F32)
        bns = SB("bns", 128, 12, F32)
        t1q = SB("t1q", 128, 512, F32)
        r_r = SB("r_r", 128, 3 * ICW, F32)
        h_half = SB("h_half", 128, STP * NI, BF16)
        maxp = SB("maxp", 128, STP * NIC, F32)
        qcol = SB("qcol", 128, 16, F32)
        hq_b = SB("hq_b", 128, NI, BF16)
        sp1 = hq_b[:].bitcast(F32)[:, 0:TT]
        hqT = SB("hqT", 128, NKI * STP * 128, BF16)
        wg_r = SB("wg_r", 128, 2 * NDG * ICW, BF16)
        wu_r = SB("wu_r", 128, 2 * NDG * ICW, BF16)
        wd_r = SB("wd_r", 128, 2 * D, BF16)

        gbufA = h_half[:].bitcast(F32)
        gbufB = hqT[:].bitcast(F32)
        qout = y_acc[:]

        pg = PS("pg", 128, ICW)
        pu = PS("pu", 128, ICW)
        pd = [PS(f"pd{i}", 128, 512) for i in range(4)]
        pr = [PS(f"pr{i}", 128, 512) for i in range(2)]

        sem_names = [
            "s_c0", "s_pa0", "s_pa1", "s_wo0", "s_wo1", "s_pb0", "s_pb1",
            "s_pb2", "s_pb3", "s_st", "s_st1", "s_st2", "s_st3",
            "s_xl0", "s_xl1", "s_xr0", "s_xr1", "s_tl", "s_tq", "s_th0", "s_th1",
            "s_wg0", "s_wg1", "s_wu0", "s_wu1", "s_wd0", "s_wd1", "s_wd2",
            "s_tq1", "s_wl0", "s_wl1", "s_wl2", "s_wl3", "s_ws0", "s_ws1", "s_ws2", "s_ws3",
            "s_out",
            "t_p0", "t_wout", "t_bc", "t_dtz", "t_bbc", "t_g", "t_u",
            "t_d",
            "a_s", "a_rstd", "a_sp", "a_spp", "a_bce", "s_dA", "a_rel", "a_hq2",
            "v_s", "v_ln", "v_xln", "v_r2", "v_xq", "v_bce", "v_dt", "v_u",
            "v_dB", "v_scan", "v_ym", "v_h", "v_hq1", "v_dev",
            "p_s", "p_ya", "p_hh",
        ]
        sems = {n: es.enter_context(nc.semaphore(n)) for n in sem_names}
        block = es.enter_context(nc.Block())

        xlnT_v = xlnT[:].rearrange("p (k j t) -> p k j t", k=NDG, j=NTJ)
        xqT_v = xqT[:].rearrange("p (k j t) -> p k j t", k=NDG, j=NTJ)
        dt_v = dt_all[:].rearrange("p (k t) -> p k t", k=NDG)
        dB_v = dB_all[:].rearrange("p (k t) -> p k t", k=NDG)
        u_v = u_all[:].rearrange("p (k t) -> p k t", k=NDG)
        yp_v = yp_all[:].rearrange("p (k t) -> p k t", k=NDG)
        ya_v = y_acc[:].rearrange("p (k j t) -> p k j t", k=NDG, j=NTJ)
        yT_v = yT_b[:].rearrange("p (k j t) -> p k j t", k=NDG, j=NTJ)
        r2_v = r2[:].rearrange("p (j d) -> p j d", j=NTJ)
        tok_v = tok[:].rearrange("p (j d) -> p j d", j=NTJ)
        woT_rv = woT_r[:].rearrange("p (r o) -> p r o", r=4)
        wxpT_v = wxpT_b[:].rearrange("p (k n) -> p k n", k=NDG)
        oh_v = oh32[:].rearrange("q (n p) -> q n p", n=32)
        dA_v = dA_r[:].rearrange("p (r t) -> p r t", r=2)
        bcB_v = bcB[:].rearrange("p (r t) -> p r t", r=2)
        bcC_v = bcC[:].rearrange("p (r t) -> p r t", r=2)
        carr_v = carr[:].rearrange("p (r i) -> p r i", r=2)
        xl_v = xl_r[:].rearrange("p (r d) -> p r d", r=2)
        xr_v = xr_r[:].rearrange("p (r d) -> p r d", r=2)
        r_rv = r_r[:].rearrange("p (r t) -> p r t", r=3)
        hf_v = h_half[:].rearrange("p (s q) -> p s q", s=STP)
        maxp_v = maxp[:].rearrange("p (s i) -> p s i", s=STP)
        hqT_v = hqT[:].rearrange("p (k s t) -> p k s t", k=NKI, s=STP)
        def hqslot(st):
            return hq_b[:] if st % 2 == 0 else xlnT[:]
        wg_rv = wg_r[:].rearrange("p (r k t) -> p r k t", r=2, k=NDG)
        wu_rv = wu_r[:].rearrange("p (r k t) -> p r k t", r=2, k=NDG)
        wd_rv = wd_r[:].rearrange("p (r d) -> p r d", r=2)

        def pg_of(idx):
            return (pg, pd[0], pd[2])[idx % 3]

        def pu_of(idx):
            return (pu, pd[1], pd[3])[idx % 3]

        def qcol_s(c, st):
            o = (c % 2) * 8 + st * 2
            return qcol[:, o:o + 1]

        def qcol_f(c, st):
            o = (c % 2) * 8 + st * 2 + 1
            return qcol[:, o:o + 1]

        def l1(j, i):
            return lncol[:, j * 8 + i:j * 8 + i + 1]

        def l2(j, i):
            return lncol[:, 32 + j * 8 + i:32 + j * 8 + i + 1]

        NW_T = {"g": NDG, "u": NDG, "d": NKI}
        NW_FD = {"g": NI, "u": NI, "d": D}
        WSRC = {"g": wgT_h, "u": wuT_h, "d": wdT_h}
        WSCR = {"g": wgq_s, "u": wuq_s, "d": wdq_s}
        WCOL = {"g": 0, "u": 1, "d": 2}

        _PBPREV = []
        for _w in ("g", "u", "d"):
            for _i in range(NW_T[_w]):
                _PBPREV.append((_w, _i))

        def gb(i):
            return gbufA if i % 2 == 0 else gbufB

        xt_h = x_h.rearrange("(c j p) d -> c j p d", c=NCH, j=NTJ)

        def emit_sync(sync):
            p2 = ctx.pass2

            def dma(out, in_, sem, labels=(), transpose=False):
                i = sync.dma_start(out=out, in_=in_, transpose=transpose) if p2 else None
                ctx.inc(i, sem, 16, labels)

            dma(o128[:], o128_h, "s_c0")
            dma(oh32[:], oh_h, "s_c0")
            dma(ones1[:], ones_h, "s_c0")
            dma(gbufA[:, 0:NDG * 32], wxpT_h, "s_c0")
            dma(gbufA[:, 512:512 + NDG * NST], alog_h, "s_c0")
            dma(gbufB[0:1, 0:D], wdt_h, "s_c0")
            dma(gbufB[0:1, D:2 * D], bdt_h, "s_c0")
            dma(gbufB[0:1, 2 * D:2 * D + 32], bxp_h, "s_c0")
            dma(gbufB[0:1, 3 * D:4 * D], bout_h, "s_c0", labels=[("c0",)])


            for c in range(SEG):
                cf = c - 1
                if c < NCH:
                    for j in range(NTJ):
                        ctx.wait(sync, "v_xln", "xln", c, j)
                        if c > 0 and j == 0:
                            ctx.wait(sync, "v_u", "u", c - 1)
                            ctx.wait(sync, "t_dtz", "dtz", c - 1, NDG - 1)
                            if c >= 2:
                                ctx.wait_if(sync, None, "th", c - 2, 0, 1)
                                ctx.wait_if(sync, None, "th", c - 2, 1, 1)
                        src = tok_v[:, j].rearrange("p (k t) -> p k t", k=NDG)
                        dma(xlnT_v[:, :, j], src, "s_tl", labels=[("tl", c, j)],
                            transpose=True)
                if 1 <= c <= NCH:
                    for j in range(NTJ):
                        ctx.wait(sync, "v_xq", "xq", cf, j)
                        if cf > 0 and j == 0:
                            ctx.wait(sync, "t_u", "gu", cf - 1, NGI - 1)
                        src = tok_v[:, j].rearrange("p (k t) -> p k t", k=NDG)
                        dma(xqT_v[:, :, j], src,
                            "s_tq" if j < 2 else "s_tq1",
                            labels=[("tq", cf, j)], transpose=True)
                    for gi in range(NGI):
                        ic = gi % NIC
                        gg = cf * NGI + gi
                        ctx.wait_if(sync, "t_u", "guG", gg - 2)
                        if cf == 0 and gi <= 1:
                            for uu in range(4 * NW_T["g"] - 4, 4 * NW_T["g"]):
                                ctx.wait(sync, None, "stu", uu)
                        dma(wg_rv[:, gi % 2], wgq_s[ic].rearrange("k p t -> p k t"),
                            f"s_wg{gi % 2}", labels=[("wg", cf, gi)])
                        if cf == 0 and gi <= 1:
                            for uu in range(8 * NW_T["g"] - 4, 8 * NW_T["g"]):
                                ctx.wait(sync, None, "stu", uu)
                        dma(wu_rv[:, gi % 2], wuq_s[ic].rearrange("k p t -> p k t"),
                            f"s_wu{gi % 2}", labels=[("wu", cf, gi)])
                    for h in range(NHALF):
                        for sj in range(STP):
                            st = h * STP + sj
                            sg = cf * NTJ + st
                            ctx.wait(sync, "a_hq2", "hq2", cf, st)
                            if cf == 0:
                                for uu in range(4 * 16 + NW_T["d"] - 4,
                                                4 * 16 + NW_T["d"]):
                                    ctx.wait(sync, None, "stu", uu)
                            if not (cf == 0 and h == 0):
                                ph = (cf, h - 1) if h > 0 else (cf - 1, NHALF - 1)
                                ctx.wait(sync, "t_d", "dmm", ph[0], ph[1])
                            src = hqslot(st).rearrange(
                                "p (k t) -> p k t", k=NKI)
                            dma(hqT_v[:, :, sj], src, f"s_th{st % 2}",
                                labels=[("th", cf, h, sj), ("thg", sg)],
                                transpose=True)
                        for k in range(NKI):
                            kg = (cf * NHALF + h) * NKI + k
                            ctx.wait_if(sync, "t_d", "wdG", kg - 2)
                            if cf == 0:
                                ctx.wait(sync, None, "st", "d", k)
                            dma(wd_rv[:, k % 2], wdq_s[k], f"s_wd{k % 2}",
                                labels=[("wd", cf, h, k)])
            if p2:
                sync.wait_ge(sems["s_out"], ctx.cnt.get("s_out", 0))

        def emit_pool(g):
            p2 = ctx.pass2

            def pstep(f, sem=None, labels=()):
                use = sem or "p_s"
                i = f() if p2 else None
                c = ctx.inc(i, use, 1, labels)
                if p2:
                    g.wait_ge(sems[use], c)

            def dma(out, in_, sem, labels=()):
                i = g.dma_start(out=out, in_=in_) if p2 else None
                ctx.inc(i, sem, 16, labels)

            for hh in range(2):
                dma(xl_v[:, hh], xt_h[0, 0][:, hh * 512:(hh + 1) * 512],
                    f"s_xl{hh}", labels=[("xl", 0, 0, hh)])

            p0ld = []
            p0i2 = 0
            for i in range(NDG):
                def tld(i=i, p0i2=p0i2):
                    if p0i2 == 0:
                        ctx.wait(g, None, "alog_ld")
                        ctx.wait(g, None, "negAexp")
                    if p0i2 == 1:
                        ctx.wait(g, None, "alog_ld")
                    if i >= 2:
                        ctx.wait(g, None, "wos", i - 2)
                    dma(gb(p0i2)[:, 0:D], woT_h[i], f"s_wo{p0i2 % 2}",
                        labels=[("ldwo", i)])
                p0ld.append(tld)
                if i >= 1:
                    def tws(i=i, p0i2=p0i2):
                        ctx.wait(g, None, "ldwo", i - 1)
                        dma(woq_s[i - 1], gb(p0i2 - 1)[:, 0:D],
                            f"s_ws{(i - 1) % 4}", labels=[("wos", i - 1)])
                    p0ld.append(tws)
                p0i2 += 1
            def twsl(p0i2=p0i2):
                ctx.wait(g, None, "ldwo", NDG - 1)
                dma(woq_s[NDG - 1], gb(p0i2 - 1)[:, 0:D],
                    f"s_ws{(NDG - 1) % 4}", labels=[("wos", NDG - 1)])
            p0ld.append(twsl)
            for w in ("g", "u", "d"):
                for i in range(NW_T[w]):
                    def ta2(w=w, i=i, p0i2=p0i2):
                        if p0i2 in (NDG, NDG + 1):
                            ctx.wait(g, None, "wos", p0i2 - 2)
                        ctx.wait_if(g, None, "p0c", p0i2 - 2)
                        dma(gb(p0i2)[:, 0:NW_FD[w]], WSRC[w][i],
                            f"s_pa{p0i2 % 2}", labels=[("ldA", w, i)])
                    p0ld.append(ta2)
                    p0i2 += 1
            PB_UNITS2 = []
            for w in ("g", "u"):
                for i in range(NW_T[w]):
                    for q in range(4):
                        PB_UNITS2.append((w, i, q))
            for i in range(NW_T["d"]):
                PB_UNITS2.append(("d", i, 0))
            for ug, (w, i, q) in enumerate(PB_UNITS2):
                def tbu2(w=w, i=i, q=q, ug=ug):
                    s = ug % 4
                    if ug >= 4:
                        ctx.wait_if(g, None, "pBu", ug - 4)
                    else:
                        ctx.wait(g, "v_s", "pA", "d", NW_T["d"] - 1)
                    srcu = (WSRC[w][i][:, q * D:(q + 1) * D] if w != "d"
                            else WSRC["d"][i])
                    dma(gbufB[:, s * D:(s + 1) * D], srcu,
                        f"s_pb{ug % 4}", labels=[("ldU", ug)])
                p0ld.append(tbu2)

            n_p0ld = len(p0ld)

            def p0ld_drain(kmax, cap=None):
                k = 0
                while p0ld and k < kmax:
                    if cap is not None and n_p0ld - len(p0ld) >= cap:
                        return
                    p0ld.pop(0)()
                    k += 1

            for c in range(SEG):
                cf = c - 1
                if c < NCH:
                    for j in range(1, NTJ):
                        for hh in range(2):
                            hg = (c * NTJ + j) * 2 + hh
                            ctx.wait_if(g, "v_ln", "xlh", hg - 2)
                            dma(xl_v[:, hh],
                                xt_h[c, j][:, hh * 512:(hh + 1) * 512],
                                f"s_xl{hh}", labels=[("xl", c, j, hh)])
                if c + 1 < NCH:
                    for hh in range(2):
                        hg = ((c + 1) * NTJ) * 2 + hh
                        ctx.wait_if(g, "v_ln", "xlh", hg - 2)
                        dma(xl_v[:, hh],
                            xt_h[c + 1, 0][:, hh * 512:(hh + 1) * 512],
                            f"s_xl{hh}", labels=[("xl", c + 1, 0, hh)])
                if 2 <= c <= NCH + 1:
                    ctx.wait(g, "v_dev", "dev", c - 2)
                    ov = out_h.rearrange("(c j p) d -> c j p d", c=NCH, j=NTJ)[c - 2]
                    for j in range(NTJ):
                        dma(ov[j], r2_v[:, j], "s_out",
                            labels=[("out", c - 2)] if j == NTJ - 1 else ())
                if 1 <= c <= NCH:
                    for j in range(NTJ):
                        for oh in range(2):
                            k = cf * NTJ * 2 + j * 2 + oh
                            ctx.wait_if(g, "v_r2", "r2G", k - 2)
                            dma(xr_v[:, (j * 2 + oh) % 2],
                                xt_h[cf, j][:, oh * 512:(oh + 1) * 512],
                                f"s_xr{(j * 2 + oh) % 2}",
                                labels=[("xr", cf, j, oh)])
                if c == 2:
                    p0ld_drain(10 ** 9)
                if c < NCH:
                    for n in range(NST):
                        ctx.wait(g, "v_ym", "ym", c, n)
                        if n == 0:
                            if c > 0:
                                ctx.wait(g, None, "yT", c - 1)
                            pstep(lambda: g.tensor_copy(out=y_acc[:], in_=yp_all[:]),
                                  "p_ya", [("ya", c, n)])
                        else:
                            pstep(lambda: g.tensor_tensor(
                                out=y_acc[:], in0=y_acc[:], in1=yp_all[:], op=Op.add),
                                "p_ya", [("ya", c, n)])
                        p0ld_drain(8, cap=(2 * NDG + sum(NW_T.values()) + 4)
                                   if c == 0 else None)
                if c < NCH:
                    if c >= 1:
                        ctx.wait(g, "t_wout", "wout", c - 1, NTJ - 1, 1)
                    pstep(lambda: g.tensor_copy(out=yT_b[:], in_=y_acc[:]),
                          "p_ya", [("yT", c)])


        def emit_pe(t):
            p2 = ctx.pass2

            def mm(out, lhsT, rhs, start, stop, sem=None, labels=()):
                i = t.matmul(out=out, lhsT=lhsT, rhs=rhs, start=start, stop=stop) if p2 else None
                if sem:
                    ctx.inc(i, sem, 1, labels)

            for c in range(SEG):
                cf = c - 1
                if c < NCH:
                    ctx.wait(t, "s_tl", "tl", c, NTJ - 1)
                    if c >= 1:
                        ctx.wait(t, "a_bce", "bceB", c - 1, NST - 1)
                    for dg in range(NDG):
                        mm(pr[0][0:32, 0:TT], wxpT_v[:, dg],
                           xlnT_v[:, dg].rearrange("p j t -> p (j t)"), dg == 0, False)
                    mm(pr[0][0:32, 0:TT], bxp_b[:], ones1[:], False, True,
                       "t_bc", [("bc", c)])
                    for dg in range(NDG):
                        pp = (pr[0], pr[1], pg, pu)[dg % 4]
                        if dg == 0:
                            ctx.wait(t, "v_bce", "bce", c)
                        elif dg == 1:
                            if c >= 1:
                                ctx.wait(t, "a_bce", "bceC", c - 1, NST - 1)
                        elif dg == 2:
                            ctx.wait_if(t, "a_rel", "rel", c - 2, NPAIR - 2)
                        elif dg == 3:
                            ctx.wait_if(t, "v_h", "h", c - 2, NPAIR - 2)
                        else:
                            ctx.wait(t, "a_spp", "spp", c, dg - 4)
                        mm(pp[:, 0:TT], wdt_b[0:1, dg * 128:(dg + 1) * 128],
                           xlnT[0:1, 0:TT], True, False)
                        mm(pp[:, 0:TT], bdt_b[0:1, dg * 128:(dg + 1) * 128],
                           ones1[:], False, True, "t_dtz", [("dtz", c, dg)])

                def bbc_pair(n, c=c):
                    if n == 0:
                        ctx.wait(t, "a_spp", "spp", c, NDG - 4)
                        ctx.wait(t, "v_bce", "bce", c)
                    else:
                        ctx.wait(t, "a_bce", "bceB", c, n - 1)
                    mm(pr[0][:, 0:TT], oh_v[:, n], bct[:], True, True,
                       "t_bbc", [("bbcB", c, n)])
                    if n == 0:
                        ctx.wait(t, "a_spp", "spp", c, NDG - 3)
                    else:
                        ctx.wait(t, "a_bce", "bceC", c, n - 1)
                    mm(pr[1][:, 0:TT], oh_v[:, NST + n], bct[:], True, True,
                       "t_bbc", [("bbcC", c, n)])

                if c < NCH:
                    bbc_pair(0)
                    if c == 0:
                        for n in range(1, NST):
                            bbc_pair(n)
                        ctx.wait(t, "v_s", "partials")
                        mm(pd[0][:, 0:4], o128[:], wsum[:, 0:4], True, True,
                           "t_p0", [("tot",)])
                if 1 <= c <= NCH:
                    ctx.wait(t, None, "yT", cf)
                    for rnd in range(2):
                        for dg in range(NDG):
                            wog = (cf * 2 + rnd) * NDG + dg
                            ctx.wait(t, None, "wol", cf, rnd, dg)
                            for b in range(4):
                                k = rnd * 4 + b
                                j, oh = k // 2, k % 2
                                if dg == 0:
                                    if rnd == 0:
                                        if c >= 2:
                                            ctx.wait(t, "v_dev", "devst",
                                                     c - 2, 1, b // 2, b % 2)
                                        elif b == 0:
                                            ctx.wait(t, "v_s", "cols")
                                    else:
                                        ctx.wait(t, "v_r2", "r2",
                                                 cf, b // 2, b % 2)
                                mm(pd[b][:], yT_v[:, dg, j],
                                   woT_rv[:, wog % 4, oh * 512:(oh + 1) * 512],
                                   dg == 0, False,
                                   "t_wout" if b == 3 else None,
                                   [("wocG", wog)])
                        for b in range(4):
                            k = rnd * 4 + b
                            j, oh = k // 2, k % 2
                            mm(pd[b][:], ones1[0:1, 0:128],
                               bout_b[0:1, oh * 512:(oh + 1) * 512], False, True,
                               "t_wout", [("wout", cf, j, oh)])
                if 1 <= c < NCH:
                    bbc_pair(1)
                    bbc_pair(2)
                    if c == 1:
                        for m in range(3, NST):
                            bbc_pair(m)
                for gi in range(NGI):
                    h, ic = divmod(gi, NIC)
                    if 1 <= c <= NCH and gi == 0:
                        ctx.wait(t, "s_tq", "tq", cf, 1)
                    if 1 <= c <= NCH and gi == NIC:
                        ctx.wait(t, "s_tq1", "tq", cf, 3)
                    if 1 <= c <= NCH:
                        gg = cf * NGI + gi
                        ctx.wait(t, None, "wg", cf, gi)
                        ctx.wait(t, None, "wu", cf, gi)
                        for sj in range(STP):
                            st = h * STP + sj
                            idx = gi * STP + sj
                            pidx = cf * NPAIR + idx
                            if gi == 0 and sj == 0:
                                ctx.wait_if(t, "a_spp", "spp", c, NDG - 2)
                                ctx.wait_if(t, "a_spp", "spp", c, NDG - 1)
                            if gi == 0 and sj == 1:
                                ctx.wait(t, "v_r2", "r2", cf, 2, 0)
                                ctx.wait(t, "v_r2", "r2", cf, 2, 1)
                            if gi == 1 and sj == 0:
                                ctx.wait(t, "v_r2", "r2", cf, 3, 0)
                                ctx.wait(t, "v_r2", "r2", cf, 3, 1)
                            ctx.wait_if(t, "a_rel", "relG", pidx - 3)
                            for dg in range(NDG):
                                mm(pg_of(idx)[:, 0:ICW], xqT_v[:, dg, st],
                                   wg_rv[:, gi % 2, dg],
                                   dg == 0, dg == NDG - 1,
                                   "t_g" if dg == NDG - 1 else None,
                                   [("g", cf, idx)])
                            ctx.wait_if(t, "v_h", "hG", pidx - 3)
                            for dg in range(NDG):
                                labs = [("u", cf, idx)]
                                if sj == STP - 1:
                                    labs += [("gu", cf, gi), ("guG", gg)]
                                mm(pu_of(idx)[:, 0:ICW], xqT_v[:, dg, st],
                                   wu_rv[:, gi % 2, dg],
                                   dg == 0, dg == NDG - 1,
                                   "t_u" if dg == NDG - 1 else None, labs)
                    if 2 <= c < NCH and gi % 2 == 1:
                        bbc_pair((gi + 5) // 2)
                if 1 <= c <= NCH:
                    m0 = (NGI - 1 + 5) // 2 + 1
                    for h in range(NHALF):
                        ctx.wait(t, None, "th", cf, h, 0)
                        ctx.wait(t, None, "th", cf, h, STP - 1)
                        for k in range(NKI):
                            if (h == 0 and 2 <= c < NCH and k % 6 == 4
                                    and m0 + (k - 4) // 6 < NST):
                                bbc_pair(m0 + (k - 4) // 6)
                            ctx.wait(t, "s_wd", "wd", cf, h, k)
                            kg = (cf * NHALF + h) * NKI + k
                            for sj in range(STP):
                                for dh in range(2):
                                    pp = pd[sj * 2 + dh]
                                    if k == 0:
                                        if h == 0:
                                            ctx.wait(t, "v_r2", "r2",
                                                     cf, 2 + sj, dh)
                                            b = sj * 2 + dh
                                            li = (NPAIR - 1 if b in (0, 1)
                                                  else NPAIR - 3)
                                            if dh == 0:
                                                ctx.wait(t, "a_rel", "rel",
                                                         cf, li)
                                            else:
                                                ctx.wait(t, "v_h", "h",
                                                         cf, li)
                                        else:
                                            ctx.wait(t, "v_dev", "devst",
                                                     cf, 0, sj, dh)
                                    labs = [("wdG", kg)]
                                    if k == NKI - 1:
                                        labs.append(("dmm", cf, h))
                                    mm(pp[:], hqT_v[:, k, sj],
                                       wd_rv[:, k % 2, dh * 512:(dh + 1) * 512],
                                       k == 0, k == NKI - 1,
                                       "t_d" if (sj == STP - 1 and dh == 1) else None,
                                       labs)
                        if h == 0 and 2 <= c < NCH:
                            for m in range(m0 + 5, NST):
                                bbc_pair(m)

        def emit_act(a):
            p2 = ctx.pass2

            def adma(out, in_, sem, labels=()):
                i = a.dma_start(out=out, in_=in_) if p2 else None
                ctx.inc(i, sem, 16, labels)

            pstq = []
            PB_UNITS3 = []
            for w in ("g", "u"):
                for i in range(NW_T[w]):
                    for q in range(4):
                        PB_UNITS3.append((w, i, q))
            for i in range(NW_T["d"]):
                PB_UNITS3.append(("d", i, 0))
            for ug, (w, i, q) in enumerate(PB_UNITS3):
                def tst(w=w, i=i, q=q, ug=ug):
                    s = ug % 4
                    ctx.wait(a, "v_s", "pBu", ug)
                    srcu = wd_rv[:, ug % 2]
                    sem = "s_st" if ug % 4 == 0 else f"s_st{ug % 4}"
                    labs = [("stu", ug)]
                    if w == "d":
                        labs.append(("st", "d", i))
                        adma(WSCR["d"][i], srcu, sem, labels=labs)
                    else:
                        if q == 3:
                            labs.append(("st", w, i))
                        adma(WSCR[w][2 * q:2 * q + 2, i]
                             .rearrange("ic p t -> p ic t"),
                             srcu.rearrange("p (ic t) -> p ic t", ic=2),
                             sem, labels=labs)
                pstq.append(tst)

            def pst_drain(kmax):
                k = 0
                while pstq and k < kmax:
                    pstq.pop(0)()
                    k += 1

            def wol_round(cf, rnd):
                for dg in range(NDG):
                    wog = (cf * 2 + rnd) * NDG + dg
                    ctx.wait_if(a, None, "wocG", wog - 4)
                    if cf == 0 and rnd == 0:
                        ctx.wait(a, f"s_ws{dg % 4}", "wos", 4 + dg % 4)
                    adma(woT_rv[:, wog % 4], woq_s[dg], f"s_wl{wog % 4}",
                         labels=[("wol", cf, rnd, dg)])

            def astep(f, sem=None, labels=()):
                use = sem or "a_s"
                i = f() if p2 else None
                c = ctx.inc(i, use, 1, labels)
                if p2:
                    a.wait_ge(sems[use], c)

            def act(out, in_, func, bias=None, scale=None):
                kw = {}
                if bias is not None:
                    kw["bias"] = bias
                if scale is not None:
                    kw["scale"] = scale
                return a.activation(out=out, in_=in_, func=func, **kw)

            def relu_step(cf, idx):
                pidx = cf * NPAIR + idx
                ctx.wait(a, "t_g", "g", cf, idx)
                ctx.wait_if(a, "v_h", "hG", pidx - 3)
                astep(lambda idx=idx: act(r_rv[:, idx % 3],
                                          pg_of(idx)[:, 0:ICW], AF.Relu),
                      None, [("relG0", pidx)])
                astep(lambda idx=idx: act(r_rv[:, idx % 3], r_rv[:, idx % 3],
                                          AF.Square),
                      "a_rel", [("rel", cf, idx), ("relG", pidx)])

            act_queue = []
            act_slot = [0]

            def hq2_drain(kmax):
                k = 0
                while act_queue and k < kmax:
                    cfq, st, sj, q = act_queue.pop(0)
                    qg = act_slot[0]
                    act_slot[0] += 1
                    h = st // STP
                    lastpiece = 2 * (h * NIC + (q * 512 + 511) // ICW) + sj
                    ctx.wait(a, "v_h", "h", cfq, lastpiece)
                    ctx.wait(a, None, "qcol", cfq, st)
                    if q == 0:
                        ctx.wait(a, None, "xq", cfq, NTJ - 1)
                    astep(lambda st=st, q=q, qg=qg, cfq=cfq, sj=sj: act(
                        t1q[:],
                        hf_v[:, sj][:, q * 512:(q + 1) * 512],
                        AF.Identity, scale=qcol_s(cfq, st), bias=ccol[:, 3:4]),
                          None, [("hq1", cfq, st, q)])
                    if q == 0:
                        sg = cfq * NTJ + st
                        ctx.wait_if(a, "s_th", "thg", sg - 2)
                    if st % 2 == 1 and q == 0:
                        ctx.wait_if(a, None, "u", cfq + 1)
                        ctx.wait_if(a, None, "dtz", cfq + 1, NDG - 1)
                    if st % 2 == 0 and q == 0:
                        ctx.wait_if(a, None, "dt", cfq + 1, NDG - 1)
                    labels = []
                    if q == NQ - 1:
                        labels.append(("hq2", cfq, st))
                    astep(lambda st=st, q=q, qg=qg: act(
                        hqslot(st)[:, q * 512:(q + 1) * 512],
                        t1q[:], AF.Identity, bias=ccol[:, 2:3]),
                          "a_hq2", labels)
                    k += 1

            ctx.wait(a, "v_s", "alog_ld")
            astep(lambda: act(negA[:], gbufA[:, 512:512 + NDG * NST], AF.Exp),
                  "a_rstd", [("negAexp",)])

            for c in range(SEG):
                cf = c - 1
                if c < NCH:
                    for j in range(NTJ):
                        ctx.wait(a, "v_ln", "ln1", c, j)
                        astep(lambda j=j: act(l1(j, 1), l1(j, 1), AF.Ln,
                                              bias=ccol[:, 0:1]))
                        astep(lambda j=j: act(l1(j, 1), l1(j, 1), AF.Exp,
                                              scale=ccol[:, 1:2]),
                              "a_rstd", [("rstd1", c, j)])
                if c < NCH:
                    for dg in range(NDG):
                        pp = (pr[0], pr[1], pg, pu)[dg % 4]
                        ctx.wait(a, "t_dtz", "dtz", c, dg)
                        if dg >= 1:
                            ctx.wait(a, None, "dt", c, dg - 1)
                        elif c >= 1:
                            ctx.wait(a, None, "dt", c - 1, NDG - 1)
                        if dg == 0 and c >= 2:
                            ctx.wait_if(a, None, "th", c - 2, 1, 0)
                        astep(lambda pp=pp: act(sp1[:], pp[:, 0:TT], AF.Abs))
                        astep(lambda pp=pp: act(sp2[:], pp[:, 0:TT], AF.Relu),
                              "a_spp", [("spp", c, dg)])
                        astep(lambda: act(sp1[:], sp1[:], AF.Exp, scale=-1.0))
                        astep(lambda: act(sp1[:], sp1[:], AF.Ln, bias=1.0),
                              "a_sp", [("sp", c, dg)])
                if c == 2:
                    pst_drain(10 ** 9)
                if 1 <= c <= NCH:
                    wol_round(cf, 0)
                    wol_round(cf, 1)
                if 1 <= c <= NCH:
                    for j in range(NTJ):
                        ctx.wait(a, "v_ln", "ln2", cf, j)
                        astep(lambda j=j: act(l2(j, 1), l2(j, 1), AF.Ln,
                                              bias=ccol[:, 0:1]))
                        astep(lambda j=j: act(l2(j, 1), l2(j, 1), AF.Exp,
                                              scale=ccol[:, 1:2]),
                              "a_rstd", [("rstd2", cf, j)])
                if c < NCH:
                    for n in range(NST):
                        m = c * NST + n
                        ctx.wait(a, "t_bbc", "bbcB", c, n)
                        ctx.wait_if(a, "v_dB", "dBG", m - 2)
                        astep(lambda n=n: act(bcB_v[:, n % 2], pr[0][:, 0:TT],
                                              AF.Copy),
                              "a_bce", [("bceB", c, n)])
                        ctx.wait(a, "t_bbc", "bbcC", c, n)
                        ctx.wait_if(a, "v_ym", "ymG", m - 2)
                        astep(lambda n=n: act(bcC_v[:, n % 2], pr[1][:, 0:TT],
                                              AF.Copy),
                              "a_bce", [("bceC", c, n)])
                        for dg in range(NDG):
                            gidx = m * NDG + dg
                            ctx.wait(a, "v_dt", "dt", c, dg)
                            ctx.wait_if(a, "v_scan", "scani", gidx - 2)
                            astep(lambda n=n, dg=dg, gidx=gidx: act(
                                dA_v[:, gidx % 2], dt_v[:, dg], AF.Exp,
                                scale=negA[:, dg * NST + n:dg * NST + n + 1]),
                                "s_dA", [("dA", c, n, dg)])
                        if c == 1:
                            pst_drain(8)
                        if 1 <= c <= NCH:
                            ps = 99 if c == 1 else 2
                            rs = 99 if c == 1 else 2
                            if n == ps + 4:
                                act_queue.extend(hq_entries(cf, 0))
                            if c >= 2 and n == 10:
                                act_queue.extend(hq_entries(cf, 1))
                            if n >= ps + 4:
                                hq2_drain(4)
                            if n >= rs and 4 * (n - rs) < NPAIR:
                                for q in range(4):
                                    relu_step(cf, 4 * (n - rs) + q)
                elif 1 <= c <= NCH:
                    for idx in range(NPAIR):
                        relu_step(cf, idx)
                        if idx == NPAIR // 2:
                            act_queue.extend(hq_entries(cf, 0))
                        if idx >= NPAIR // 2:
                            hq2_drain(1)
                if c == 1:
                    pst_drain(10 ** 9)
                    for idx in range(NPAIR):
                        relu_step(cf, idx)
                        if idx == NPAIR // 2:
                            act_queue.extend(hq_entries(cf, 0))
                        if idx >= NPAIR // 2:
                            hq2_drain(1)
                if 1 <= c <= NCH:
                    hq2_drain(10 ** 9)
                    if c == 1 or c == NCH:
                        act_queue.extend(hq_entries(cf, 1))
                        hq2_drain(10 ** 9)

        def hq_entries(cf, h):
            out = []
            for q in range(NQ):
                for sj in range(STP):
                    out.append((cf, h * STP + sj, sj, q))
            return out


        def vstep_factory(v):
            def vstep(f, sem=None, labels=()):
                use = sem or "v_s"
                i = f() if ctx.pass2 else None
                c = ctx.inc(i, use, 1, labels)
                if ctx.pass2:
                    v.wait_ge(sems[use], c)
            return vstep

        def emit_dve(v):
            p2 = ctx.pass2
            vstep = vstep_factory(v)

            def ts(out, in0, s1, s2, op0, op1):
                return v.tensor_scalar(out=out, in0=in0, scalar1=s1, scalar2=s2,
                                       op0=op0, op1=op1)

            ctx.wait(v, "s_scr", "c0")
            vstep(lambda: v.memset(ccol[:, 0:1], EPS_LN))
            vstep(lambda: v.memset(ccol[:, 1:2], -0.5))
            vstep(lambda: v.memset(ccol[:, 2:3], -MAGIC))
            vstep(lambda: v.memset(ccol[:, 3:4], MAGIC))
            vstep(lambda: v.memset(carr_v[:, 1], 0.0))
            vstep(lambda: v.memset(wsum[:], 0.0))
            vstep(lambda: v.tensor_copy(out=wxpT_b[:], in_=gbufA[:, 0:NDG * 32]))
            vstep(lambda: v.tensor_copy(out=wdt_b[:], in_=gbufB[0:1, 0:D]))
            vstep(lambda: v.tensor_copy(out=bdt_b[:], in_=gbufB[0:1, D:2 * D]))
            vstep(lambda: v.tensor_copy(out=bxp_b[:], in_=gbufB[0:1, 2 * D:2 * D + 32]))
            vstep(lambda: v.tensor_copy(out=bout_b[:], in_=gbufB[0:1, 3 * D:4 * D]),
                  labels=[("alog_ld",)])
            ctx.wait(v, "a_rstd", "negAexp")
            vstep(lambda: ts(negA[:], negA[:], -1.0, 0.0, Op.mult, Op.add))
            p0q = []
            p0i = NDG
            for w in ("g", "u", "d"):
                col = WCOL[w]
                for i in range(NW_T[w]):
                    def ta(w=w, i=i, p0i=p0i, col=col):
                        ctx.wait(v, None, "ldA", w, i)
                        vstep(lambda: v.tensor_reduce(
                            out=wsum[:, 3:4], in_=gb(p0i)[:, 0:NW_FD[w]],
                            axis=AX.X, op=Op.add, apply_absolute_value=True),
                            labels=[("p0c", p0i)])
                        vstep(lambda: v.tensor_tensor(
                            out=wsum[:, col:col + 1], in0=wsum[:, col:col + 1],
                            in1=wsum[:, 3:4], op=Op.add), labels=[("pA", w, i)])
                    p0q.append(ta)
                    p0i += 1

            def tpart():
                vstep(lambda: v.memset(wsum[:, 3:4], 0.0),
                      labels=[("partials",)])
            p0q.append(tpart)
            NG = float(NDG * 128 * NI)
            NDW = float(NKI * 128 * D)

            def tmeans():
                ctx.wait(v, "t_p0", "tot")
                vstep(lambda: ts(mwc[:, 5:6], pd[0][:, 0:1], 1.0 / NG, EPS_Q,
                                 Op.mult, Op.max))
                vstep(lambda: ts(mwc[:, 6:7], pd[0][:, 1:2], 1.0 / NG, EPS_Q,
                                 Op.mult, Op.max))
                vstep(lambda: ts(mwc[:, 3:4], pd[0][:, 2:3], 1.0 / NDW, EPS_Q,
                                 Op.mult, Op.max))
                vstep(lambda: v.reciprocal(out=mwc[:, 0:1], in_=mwc[:, 5:6]))
                vstep(lambda: v.reciprocal(out=mwc[:, 1:2], in_=mwc[:, 6:7]))
                vstep(lambda: v.reciprocal(out=mwc[:, 2:3], in_=mwc[:, 3:4]))
                vstep(lambda: v.tensor_tensor(out=mwc[:, 4:5], in0=mwc[:, 5:6],
                                              in1=mwc[:, 5:6], op=Op.mult))
                vstep(lambda: v.tensor_tensor(out=mwc[:, 4:5], in0=mwc[:, 4:5],
                                              in1=mwc[:, 6:7], op=Op.mult),
                      labels=[("cols",)])
            p0q.append(tmeans)
            PB_UNITS = []
            for w in ("g", "u"):
                for i in range(NW_T[w]):
                    for q in range(4):
                        PB_UNITS.append((w, i, q))
            for i in range(NW_T["d"]):
                PB_UNITS.append(("d", i, 0))
            for ug, (w, i, q) in enumerate(PB_UNITS):
                def tbu(w=w, i=i, q=q, ug=ug):
                    col = WCOL[w]
                    s = ug % 4
                    gbd = gbufB[:, s * D:(s + 1) * D]
                    g16 = wd_rv[:, ug % 2]
                    ctx.wait(v, None, "ldU", ug)
                    if ug >= 2:
                        ctx.wait(v, None, "stu", ug - 2)
                    vstep(lambda: ts(gbd, gbd, mwc[:, col:col + 1], MAGIC,
                                     Op.mult, Op.add))
                    vstep(lambda: ts(gbd, gbd, MAGIC, -1.0,
                                     Op.subtract, Op.max))
                    labs = [("pBu", ug)]
                    if w == "d" or q == 3:
                        labs.append(("pBq", w, i))
                    vstep(lambda: ts(g16, gbd, 1.0, 0.0, Op.min, Op.add),
                          labels=labs)
                p0q.append(tbu)

            n_p0q = len(p0q)

            def p0_drain(kmax, cap=None):
                k = 0
                while p0q and k < kmax:
                    if cap is not None and n_p0q - len(p0q) >= cap:
                        return
                    p0q.pop(0)()
                    k += 1

            for c in range(SEG):
                cf = c - 1
                if c == 1:
                    p0_drain(2)
                if c == 2:
                    p0_drain(10 ** 9)
                if c < NCH:
                    for j in range(NTJ):
                        for hh in range(2):
                            ctx.wait(v, None, "xl", c, j, hh)
                            vstep(lambda j=j, hh=hh: v.bn_stats(
                                out=bns[:, hh * 6:hh * 6 + 6], in_=xl_v[:, hh]))
                        vstep(lambda j=j: v.bn_aggr(
                            out=lncol[:, j * 8:j * 8 + 2], in_=bns[:]))
                        vstep(lambda j=j: ts(l1(j, 2), l1(j, 0), -1.0, 0.0,
                                             Op.mult, Op.add),
                              "v_ln", [("ln1", c, j)])
                        ctx.wait(v, "a_rstd", "rstd1", c, j)
                        if c >= 2:
                            ctx.wait(v, "s_tq", "tq", c - 2, NTJ - 1)
                        jg = c * NTJ + j
                        for hh in range(2):
                            hg = (c * NTJ + j) * 2 + hh
                            labs = [("xlh", hg)]
                            if hh == 1:
                                labs += [("xln", c, j), ("xlnG", jg)]
                            vstep(lambda j=j, c=c, hh=hh: ts(
                                tok_v[:, j][:, hh * 512:(hh + 1) * 512],
                                xl_v[:, hh], l1(j, 2), l1(j, 1),
                                Op.add, Op.mult),
                                "v_ln" if hh == 0 else "v_xln", labs)
                if 2 <= c <= NCH + 1:
                    cd = c - 2
                    ctx.wait(v, "t_d", "dmm", cd, 1)
                    for sj in range(STP):
                        st = 2 + sj
                        for dh in range(2):
                            def f(sj=sj, st=st, dh=dh, cd=cd):
                                sl = r2_v[:, st][:, dh * 512:(dh + 1) * 512]
                                return v.scalar_tensor_tensor(
                                    out=sl, in0=pd[sj * 2 + dh][:],
                                    scalar=qcol_f(cd, st), in1=sl,
                                    op0=Op.mult, op1=Op.add)
                            labs = [("devst", cd, 1, sj, dh)]
                            if sj == STP - 1 and dh == 1:
                                labs.append(("dev", cd))
                            vstep(f, "v_dev", labs)
                if c < NCH:
                    ctx.wait(v, "t_bc", "bc", c)
                    vstep(lambda: v.tensor_copy(out=bct[:], in_=pr[0][0:32, 0:TT]),
                          "v_bce", [("bce", c)])
                    for dg in range(NDG):
                        ctx.wait(v, "a_sp", "sp", c, dg)
                        vstep(lambda dg=dg: v.tensor_tensor(
                            out=dt_v[:, dg], in0=sp1[:], in1=sp2[:], op=Op.add),
                            "v_dt", [("dt", c, dg)])
                    ctx.wait(v, "s_tl", "tl", c, NTJ - 1)
                    vstep(lambda: v.tensor_tensor(
                        out=u_all[:], in0=dt_all[:], in1=xlnT[:], op=Op.mult),
                        "v_u", [("u", c)])
                if 1 <= c <= NCH:
                    for j in range(NTJ):
                        for oh in range(2):
                            k = cf * NTJ * 2 + j * 2 + oh
                            ctx.wait(v, "t_wout", "wout", cf, j, oh)
                            ctx.wait(v, "s_x", "xr", cf, j, oh)
                            if j == 0 and oh == 0 and cf >= 1:
                                ctx.wait(v, "s_out", "out", cf - 1)
                            def f(j=j, oh=oh):
                                sl = r2_v[:, j][:, oh * 512:(oh + 1) * 512]
                                return v.scalar_tensor_tensor(
                                    out=sl, in0=pd[(j * 2 + oh) % 4][:], scalar=1.0,
                                    in1=xr_v[:, (j * 2 + oh) % 2],
                                    op0=Op.mult, op1=Op.add)
                            labs = [("r2", cf, j, oh), ("r2G", k)]
                            if j == NTJ - 1 and oh == 1:
                                labs.append(("r2done", cf))
                            vstep(f, "v_r2", labs)
                    for j in range(NTJ):
                        vstep(lambda j=j: v.bn_stats(out=bns[:, 0:6],
                                                     in_=r2_v[:, j][:, 0:512]))
                        vstep(lambda j=j: v.bn_stats(out=bns[:, 6:12],
                                                     in_=r2_v[:, j][:, 512:1024]))
                        vstep(lambda j=j: v.bn_aggr(
                            out=lncol[:, 32 + j * 8:32 + j * 8 + 2], in_=bns[:]))
                        vstep(lambda j=j: ts(l2(j, 2), l2(j, 0), -1.0, 0.0,
                                             Op.mult, Op.add),
                              "v_ln", [("ln2", cf, j)])
                        vstep(lambda j=j: v.tensor_reduce(
                            out=l2(j, 3), in_=r2_v[:, j], axis=AX.X, op=Op.max))
                        vstep(lambda j=j: v.tensor_reduce(
                            out=l2(j, 4), in_=r2_v[:, j], axis=AX.X, op=Op.min))
                        vstep(lambda j=j: v.scalar_tensor_tensor(
                            out=l2(j, 3), in0=l2(j, 3), scalar=1.0, in1=l2(j, 2),
                            op0=Op.mult, op1=Op.add))
                        vstep(lambda j=j: v.scalar_tensor_tensor(
                            out=l2(j, 4), in0=l2(j, 4), scalar=-1.0, in1=l2(j, 2),
                            op0=Op.mult, op1=Op.subtract))
                        vstep(lambda j=j: v.tensor_tensor(
                            out=l2(j, 3), in0=l2(j, 3), in1=l2(j, 4), op=Op.max))
                    for j in range(NTJ):
                        ctx.wait(v, "a_rstd", "rstd2", cf, j)
                        vstep(lambda j=j: v.tensor_tensor(
                            out=l2(j, 3), in0=l2(j, 3), in1=l2(j, 1), op=Op.mult))
                        vstep(lambda j=j: ts(l2(j, 3), l2(j, 3), EPS_Q, 0.0,
                                             Op.max, Op.add))
                        vstep(lambda j=j: v.reciprocal(out=l2(j, 5), in_=l2(j, 3)))
                        vstep(lambda j=j: ts(l2(j, 5), l2(j, 5), 127.0, 0.0,
                                             Op.mult, Op.add))
                        vstep(lambda j=j: v.tensor_tensor(
                            out=l2(j, 6), in0=l2(j, 1), in1=l2(j, 5), op=Op.mult))
                        vstep(lambda j=j: ts(l2(j, 3), l2(j, 3), 1.0 / 127.0, 0.0,
                                             Op.mult, Op.add))
                        vstep(lambda j=j: v.tensor_tensor(
                            out=l2(j, 4), in0=l2(j, 3), in1=l2(j, 3), op=Op.mult))
                        vstep(lambda j=j: v.tensor_tensor(
                            out=l2(j, 4), in0=l2(j, 4), in1=l2(j, 3), op=Op.mult))
                        vstep(lambda j=j: v.tensor_tensor(
                            out=l2(j, 7), in0=l2(j, 4), in1=mwc[:, 4:5], op=Op.mult))
                        if c < NCH:
                            ctx.wait(v, "s_tl", "tl", c, NTJ - 1)
                        else:
                            ctx.wait(v, "s_tq", "tq", cf - 1, NTJ - 1)
                        if cf >= 1 and j == 0:
                            ctx.wait(v, "a_hq2", "hq2", cf - 1, NTJ - 1)
                        for ph in range(2):
                            vstep(lambda j=j, ph=ph: ts(
                                t1q[:], r2_v[:, j][:, ph * 512:(ph + 1) * 512],
                                l2(j, 2), l2(j, 6), Op.add, Op.mult))
                            vstep(lambda j=j, ph=ph: ts(
                                tok_v[:, j][:, ph * 512:(ph + 1) * 512],
                                t1q[:], MAGIC, MAGIC, Op.add, Op.subtract),
                                "v_xq" if ph == 1 else None,
                                [("xq", cf, j)] if ph == 1 else ())
                if c < NCH:
                    for n in range(NST):
                        m = c * NST + n
                        ctx.wait(v, "a_bce", "bceB", c, n)
                        def fdB(n=n):
                            bc = bcB_v[:, n % 2].unsqueeze(1).broadcast_to(
                                [128, NDG, TT])
                            return v.tensor_tensor(out=dB_v, in0=u_v, in1=bc,
                                                   op=Op.mult)
                        vstep(fdB, "v_dB", [("dB", c, n), ("dBG", m)])
                        for dg in range(NDG):
                            gidx = m * NDG + dg
                            ctx.wait(v, "s_dA", "dA", c, n, dg)
                            def fsc(n=n, dg=dg, gidx=gidx, c=c):
                                col = n * NDG + dg
                                return v.tensor_tensor_scan(
                                    out=dB_v[:, dg], data0=dA_v[:, gidx % 2],
                                    data1=dB_v[:, dg],
                                    initial=carr_v[:, (c + 1) % 2][:, col:col + 1],
                                    op0=Op.mult, op1=Op.add)
                            vstep(fsc, "v_scan", [("scani", gidx)])
                        def fc(n=n, c=c):
                            src = dB_all[:].rearrange(
                                "p (k t) -> p k t", k=NDG)[:, :, TT - 1]
                            return v.tensor_copy(
                                out=carr_v[:, c % 2][:, n * NDG:(n + 1) * NDG],
                                in_=src)
                        vstep(fc)
                        ctx.wait(v, "a_bce", "bceC", c, n)
                        if m >= 1:
                            ctx.wait(v, "p_ya", "ya", (m - 1) // NST, (m - 1) % NST)
                        def fy(n=n):
                            bc = bcC_v[:, n % 2].unsqueeze(1).broadcast_to(
                                [128, NDG, TT])
                            return v.tensor_tensor(out=yp_v, in0=dB_v, in1=bc,
                                                   op=Op.mult)
                        vstep(fy, "v_ym", [("ym", c, n), ("ymG", m)])
                        if c * NST + n >= 3:
                            p0_drain(8, cap=(sum(NW_T.values()) + 1)
                                     if c == 0 else None)
                        if 1 <= c <= NCH:
                            ps = 99 if c == 1 else 2
                            if n >= ps and 4 * (n - ps) < NPAIR:
                                for q in range(4):
                                    hstep(v, vstep, cf, 4 * (n - ps) + q, c)
                    if c == 1:
                        for idx in range(NPAIR):
                            hstep(v, vstep, cf, idx, c)
                elif 1 <= c <= NCH:
                    for idx in range(NPAIR):
                        hstep(v, vstep, cf, idx, c)
                if 1 <= c <= NCH:
                    ctx.wait(v, "t_d", "dmm", cf, 0)
                    for sj3 in range(STP):
                        st3 = sj3
                        for dh in range(2):
                            def fd(sj3=sj3, st3=st3, dh=dh, cf=cf):
                                sl = r2_v[:, st3][:, dh * 512:(dh + 1) * 512]
                                return v.scalar_tensor_tensor(
                                    out=sl, in0=pd[sj3 * 2 + dh][:],
                                    scalar=qcol_f(cf, st3), in1=sl,
                                    op0=Op.mult, op1=Op.add)
                            vstep(fd, "v_dev", [("devst", cf, 0, sj3, dh)])

        def hstep(v, vstep, cf, idx, c):
            gi, sj = divmod(idx, STP)
            h, ic = divmod(gi, NIC)
            pidx = cf * NPAIR + idx
            ctx.wait(v, "a_rel", "rel", cf, idx)
            ctx.wait(v, "t_u", "u", cf, idx)
            if cf == 0 and idx == 0:
                ctx.wait(v, None, "pA", "d", NW_T["d"] - 1)
            if h == 1:
                ctx.wait_if(v, None, "hq1", cf, sj, ic * ICW // 512)
            else:
                ctx.wait_if(v, None, "hq1", cf - 1, STP + sj, ic * ICW // 512)
            rv = r_rv[:, idx % 3]

            def fh():
                return v.tensor_tensor(
                    out=hf_v[:, sj][:, ic * ICW:(ic + 1) * ICW],
                    in0=rv, in1=pu_of(idx)[:, 0:ICW], op=Op.mult)
            vstep(fh, "v_h", [("h", cf, idx), ("hG", pidx)])

            def fm():
                return v.tensor_reduce(
                    out=maxp_v[:, sj][:, ic:ic + 1],
                    in_=hf_v[:, sj][:, ic * ICW:(ic + 1) * ICW], axis=AX.X,
                    op=Op.max, apply_absolute_value=True)
            vstep(fm)
            if ic == NIC - 1 and sj == STP - 1:
                for sj2 in range(STP):
                    st2 = h * STP + sj2
                    j = st2
                    vstep(lambda sj2=sj2: v.tensor_reduce(
                        out=lncol[:, 6:7], in_=maxp_v[:, sj2], axis=AX.X,
                        op=Op.max))
                    vstep(lambda j=j: v.reciprocal(out=lncol[:, 7:8],
                                                   in_=l2(j, 7)))
                    vstep(lambda: v.tensor_scalar(
                        out=lncol[:, 7:8], in0=lncol[:, 7:8], scalar1=EPS_Q,
                        scalar2=0.0, op0=Op.mult, op1=Op.add))
                    vstep(lambda: v.tensor_tensor(
                        out=lncol[:, 6:7], in0=lncol[:, 6:7],
                        in1=lncol[:, 7:8], op=Op.max))
                    vstep(lambda cf=cf, st2=st2: v.reciprocal(
                        out=qcol_s(cf, st2), in_=lncol[:, 6:7]))
                    vstep(lambda cf=cf, st2=st2: v.tensor_scalar(
                        out=qcol_s(cf, st2), in0=qcol_s(cf, st2), scalar1=127.0,
                        scalar2=0.0, op0=Op.mult, op1=Op.add),
                        labels=[("qcol", cf, st2)])
                    vstep(lambda cf=cf, st2=st2, j=j: v.tensor_tensor(
                        out=qcol_f(cf, st2), in0=l2(j, 7), in1=lncol[:, 6:7],
                        op=Op.mult))
                    vstep(lambda cf=cf, st2=st2: v.tensor_tensor(
                        out=qcol_f(cf, st2), in0=qcol_f(cf, st2), in1=mwc[:, 3:4],
                        op=Op.mult))
                    vstep(lambda cf=cf, st2=st2: v.tensor_scalar(
                        out=qcol_f(cf, st2), in0=qcol_f(cf, st2),
                        scalar1=1.0 / 127.0, scalar2=0.0, op0=Op.mult, op1=Op.add))



        ctx.start(False, sems)
        emit_dve(None)
        emit_act(None)
        emit_pe(None)
        emit_pool(None)
        emit_sync(None)

        ctx.start(True, sems)

        @block.vector
        def _(vector):
            emit_dve(vector)

        @block.scalar
        def _(scalar):
            emit_act(scalar)

        @block.tensor
        def _(tensor):
            emit_pe(tensor)

        @block.gpsimd
        def _(gpsimd):
            emit_pool(gpsimd)

        @block.sync
        def _(sync):
            emit_sync(sync)

    return nc


def _prep_inputs(A_log, W_dt, b_dt, W_xp, b_xp, W_out, b_out, w_gate, w_up,
                 w_down):
    import ml_dtypes
    f32 = np.float32
    d = {}
    d["wgT"] = np.ascontiguousarray(np.asarray(w_gate, f32).T).reshape(NDG, 128, NI)
    d["wuT"] = np.ascontiguousarray(np.asarray(w_up, f32).T).reshape(NDG, 128, NI)
    d["wdT"] = np.ascontiguousarray(np.asarray(w_down, f32).T).reshape(NKI, 128, D)
    d["woT"] = np.ascontiguousarray(np.asarray(W_out, f32).T).reshape(NDG, 128, D)
    d["wxpT"] = np.ascontiguousarray(
        np.asarray(W_xp, f32).T.reshape(NDG, 128, 32).transpose(1, 0, 2)
        .reshape(128, NDG * 32))
    d["alog"] = np.ascontiguousarray(
        np.asarray(A_log, f32).reshape(NDG, 128, NST).transpose(1, 0, 2)
        .reshape(128, NDG * NST))
    d["wdtr"] = np.ascontiguousarray(np.asarray(W_dt, f32)[:, 0][None, :])
    d["bdtr"] = np.ascontiguousarray(np.asarray(b_dt, f32)[None, :])
    d["bxpr"] = np.ascontiguousarray(np.asarray(b_xp, f32)[None, :])
    d["boutr"] = np.ascontiguousarray(np.asarray(b_out, f32)[None, :])
    oh = np.zeros((32, 32, 128), ml_dtypes.bfloat16)
    for n in range(32):
        oh[n, n, :] = 1
    d["oh32"] = np.ascontiguousarray(oh.transpose(1, 0, 2).reshape(32, 32 * 128))
    d["ones1"] = np.ones((1, TT), ml_dtypes.bfloat16)
    d["o128"] = np.ones((128, 128), np.float32)
    return d


_NC_CACHE = {}


def kernel(L0_attn_x, L0_attn_A_log, L0_attn_W_dt, L0_attn_b_dt, L0_attn_W_xp,
           L0_attn_b_xp, L0_attn_W_out, L0_attn_b_out, L0_ffn_w_gate,
           L0_ffn_w_up, L0_ffn_w_down):
    x = np.asarray(L0_attn_x, np.float32)
    B, T = x.shape[0], x.shape[1]
    if T not in _NC_CACHE:
        _NC_CACHE[T] = build_program(T)
    nc = _NC_CACHE[T]
    shared = _prep_inputs(L0_attn_A_log, L0_attn_W_dt, L0_attn_b_dt, L0_attn_W_xp,
                          L0_attn_b_xp, L0_attn_W_out, L0_attn_b_out,
                          L0_ffn_w_gate, L0_ffn_w_up, L0_ffn_w_down)
    in_maps = []
    for b in range(B):
        m = dict(shared)
        m["x"] = np.ascontiguousarray(x[b])
        in_maps.append(m)
    res = run_bass_kernel_spmd(nc, in_maps, core_ids=list(range(B)))
    return np.stack([np.asarray(r["out"]) for r in res.results], axis=0)


# version 21
# speedup vs baseline: 1.1331x; 1.0085x over previous
import contextlib
import numpy as np
import concourse.bass as bass
import concourse.mybir as mybir
from concourse.bass_utils import run_bass_kernel_spmd
from concourse.alu_op_type import AluOpType as Op

F32 = mybir.dt.float32
BF16 = mybir.dt.bfloat16
AF = mybir.ActivationFunctionType
AX = mybir.AxisListType

MAGIC = float(np.float32(3 * 2**22))
EPS_LN = 1e-5
EPS_Q = 1e-5

D = 1024
NI = 4096
NST = 16
TT = 512
NDG = D // 128
NTJ = TT // 128
NKI = NI // 128
ICW = 512
NIC = NI // ICW
NHALF = 2
STP = NTJ // NHALF
NGI = NHALF * NIC
NPAIR = NGI * STP
PPN = NPAIR // NST
NQ = NI // 512


class Ctx:

    def __init__(self):
        self.lab = {}

    def start(self, pass2, sems=None):
        self.pass2 = pass2
        self.cnt = {}
        self.sems = sems
        self.last = {}

    def inc(self, instr, sem, by=1, labels=(), tag=None):
        c = self.cnt.get(sem, 0) + by
        self.cnt[sem] = c
        if not self.pass2:
            for lb in labels:
                self.lab[tuple(lb)] = (sem, c)
        if self.pass2 and instr is not None:
            instr.then_inc(self.sems[sem], by)
        if tag is not None:
            self.last[tag] = (sem, c)
        return c

    def mark(self, _sem_unused, *label):
        raise RuntimeError("mark replaced by inc labels")

    def wait(self, eng, _sem_unused, *label):
        if not self.pass2:
            return
        ent = self.lab.get(tuple(label))
        assert ent is not None, f"missing label {tuple(label)}"
        sem, v = ent
        if v > 0:
            eng.wait_ge(self.sems[sem], v)

    def wait_if(self, eng, _sem_unused, *label):
        if tuple(label) in self.lab:
            self.wait(eng, None, *label)


def build_program(T):
    NCH = T // TT
    SEG = NCH + 2
    nc = bass.Bass("TRN2", target_bir_lowering=False)

    def din(name, shape, dt=F32):
        return nc.dram_tensor(name, shape, dt, kind="ExternalInput").ap()

    x_h = din("x", [T, D])
    wgT_h = din("wgT", [NDG, 128, NI])
    wuT_h = din("wuT", [NDG, 128, NI])
    wdT_h = din("wdT", [NKI, 128, D])
    woT_h = din("woT", [NDG, 128, D])
    wxpT_h = din("wxpT", [128, NDG * 32])
    alog_h = din("alog", [128, NDG * NST])
    wdt_h = din("wdtr", [1, D])
    bdt_h = din("bdtr", [1, D])
    bxp_h = din("bxpr", [1, 32])
    bout_h = din("boutr", [1, D])
    oh_h = din("oh32", [32, 32 * 128], BF16)
    ones_h = din("ones1", [1, TT], BF16)
    o128_h = din("o128", [128, 128])
    out_h = nc.dram_tensor("out", [T, D], F32, kind="ExternalOutput").ap()

    wgq_s = nc.dram_tensor("wgq_s", [NIC, NDG, 128, ICW], BF16, kind="Internal").ap()
    wuq_s = nc.dram_tensor("wuq_s", [NIC, NDG, 128, ICW], BF16, kind="Internal").ap()
    wdq_s = nc.dram_tensor("wdq_s", [NKI, 128, D], BF16, kind="Internal").ap()
    woq_s = nc.dram_tensor("woq_s", [NDG, 128, D], BF16, kind="Internal").ap()

    ctx = Ctx()
    es = contextlib.ExitStack()
    with es:
        def SB(name, p, f, dt):
            return es.enter_context(nc.sbuf_tensor(name, [p, f], dt))

        def PS(name, p, f):
            return es.enter_context(nc.psum_tensor(name, [p, f], F32))

        woT_r = SB("woT_r", 128, 4 * D, BF16)
        wxpT_b = SB("wxpT_b", 128, NDG * 32, BF16)
        wdt_b = SB("wdt_b", 1, D, BF16)
        bdt_b = SB("bdt_b", 1, D, BF16)
        bxp_b = SB("bxp_b", 1, 32, BF16)
        bout_b = SB("bout_b", 1, D, BF16)
        oh32 = SB("oh32_t", 32, 32 * 128, BF16)
        ones1 = SB("ones1_t", 1, TT, BF16)
        o128 = SB("o128_t", 128, 128, F32)
        negA = SB("negA", 128, NDG * NST, F32)
        ccol = SB("ccol", 128, 4, F32)
        mwc = SB("mwc", 128, 8, F32)
        wsum = SB("wsum", 128, 4, F32)

        xl_r = SB("xl_r", 128, 2 * 512, F32)
        xr_r = SB("xr_r", 128, 2 * 512, F32)
        r2 = SB("r2", 128, NTJ * D, BF16)
        tok = SB("tok", 128, NTJ * D, BF16)
        xlnT = SB("xlnT", 128, NDG * TT, BF16)
        xqT = SB("xqT", 128, NDG * TT, BF16)
        dt_all = SB("dt_all", 128, NDG * TT, BF16)
        u_all = SB("u_all", 128, NDG * TT, BF16)
        bct = SB("bct", 32, TT, BF16)
        bcB = SB("bcB", 128, 2 * TT, BF16)
        bcC = SB("bcC", 128, 2 * TT, BF16)
        dA_r = SB("dA_r", 128, 2 * TT, F32)
        dB_all = SB("dB_all", 128, NDG * TT, BF16)
        yp_all = SB("yp_all", 128, NDG * TT, BF16)
        y_acc = SB("y_acc", 128, NDG * TT, BF16)
        yT_b = SB("yT_b", 128, NDG * TT, BF16)
        carr = SB("carr", 128, 2 * 128, BF16)

        sp2 = SB("sp2", 128, TT, BF16)
        lncol = SB("lncol", 128, 64, F32)
        bns = SB("bns", 128, 12, F32)
        t1q = SB("t1q", 128, 512, F32)
        r_r = SB("r_r", 128, 3 * ICW, F32)
        h_half = SB("h_half", 128, STP * NI, BF16)
        maxp = SB("maxp", 128, STP * NIC, F32)
        qcol = SB("qcol", 128, 16, F32)
        hq_b = SB("hq_b", 128, NI, BF16)
        sp1 = hq_b[:].bitcast(F32)[:, 0:TT]
        hqT = SB("hqT", 128, NKI * STP * 128, BF16)
        wg_r = SB("wg_r", 128, 2 * NDG * ICW, BF16)
        wu_r = SB("wu_r", 128, 2 * NDG * ICW, BF16)
        wd_r = SB("wd_r", 128, 2 * D, BF16)

        gbufA = h_half[:].bitcast(F32)
        gbufB = hqT[:].bitcast(F32)
        qout = y_acc[:]

        pg = PS("pg", 128, ICW)
        pu = PS("pu", 128, ICW)
        pd = [PS(f"pd{i}", 128, 512) for i in range(4)]
        pr = [PS(f"pr{i}", 128, 512) for i in range(2)]

        sem_names = [
            "s_c0", "s_pa0", "s_pa1", "s_wo0", "s_wo1", "s_pb0", "s_pb1",
            "s_pb2", "s_pb3", "s_st", "s_st1", "s_st2", "s_st3",
            "s_xl0", "s_xl1", "s_xr0", "s_xr1", "s_tl", "s_tq", "s_th0", "s_th1",
            "s_wg0", "s_wg1", "s_wu0", "s_wu1", "s_wd0", "s_wd1", "s_wd2",
            "s_tq1", "s_wl0", "s_wl1", "s_wl2", "s_wl3", "s_ws0", "s_ws1", "s_ws2", "s_ws3",
            "s_out",
            "t_p0", "t_wout", "t_bc", "t_dtz", "t_bbc", "t_g", "t_u",
            "t_d",
            "a_s", "a_rstd", "a_sp", "a_spp", "a_bce", "s_dA", "a_rel", "a_hq2",
            "v_s", "v_ln", "v_xln", "v_r2", "v_xq", "v_bce", "v_dt", "v_u",
            "v_dB", "v_scan", "v_ym", "v_h", "v_hq1", "v_dev",
            "p_s", "p_ya", "p_hh",
        ]
        sems = {n: es.enter_context(nc.semaphore(n)) for n in sem_names}
        block = es.enter_context(nc.Block())

        xlnT_v = xlnT[:].rearrange("p (k j t) -> p k j t", k=NDG, j=NTJ)
        xqT_v = xqT[:].rearrange("p (k j t) -> p k j t", k=NDG, j=NTJ)
        dt_v = dt_all[:].rearrange("p (k t) -> p k t", k=NDG)
        dB_v = dB_all[:].rearrange("p (k t) -> p k t", k=NDG)
        u_v = u_all[:].rearrange("p (k t) -> p k t", k=NDG)
        yp_v = yp_all[:].rearrange("p (k t) -> p k t", k=NDG)
        ya_v = y_acc[:].rearrange("p (k j t) -> p k j t", k=NDG, j=NTJ)
        yT_v = yT_b[:].rearrange("p (k j t) -> p k j t", k=NDG, j=NTJ)
        r2_v = r2[:].rearrange("p (j d) -> p j d", j=NTJ)
        tok_v = tok[:].rearrange("p (j d) -> p j d", j=NTJ)
        woT_rv = woT_r[:].rearrange("p (r o) -> p r o", r=4)
        wxpT_v = wxpT_b[:].rearrange("p (k n) -> p k n", k=NDG)
        oh_v = oh32[:].rearrange("q (n p) -> q n p", n=32)
        dA_v = dA_r[:].rearrange("p (r t) -> p r t", r=2)
        bcB_v = bcB[:].rearrange("p (r t) -> p r t", r=2)
        bcC_v = bcC[:].rearrange("p (r t) -> p r t", r=2)
        carr_v = carr[:].rearrange("p (r i) -> p r i", r=2)
        xl_v = xl_r[:].rearrange("p (r d) -> p r d", r=2)
        xr_v = xr_r[:].rearrange("p (r d) -> p r d", r=2)
        r_rv = r_r[:].rearrange("p (r t) -> p r t", r=3)
        hf_v = h_half[:].rearrange("p (s q) -> p s q", s=STP)
        maxp_v = maxp[:].rearrange("p (s i) -> p s i", s=STP)
        hqT_v = hqT[:].rearrange("p (k s t) -> p k s t", k=NKI, s=STP)
        def hqslot(st):
            return hq_b[:] if st % 2 == 0 else xlnT[:]
        wg_rv = wg_r[:].rearrange("p (r k t) -> p r k t", r=2, k=NDG)
        wu_rv = wu_r[:].rearrange("p (r k t) -> p r k t", r=2, k=NDG)
        wd_rv = wd_r[:].rearrange("p (r d) -> p r d", r=2)

        def pg_of(idx):
            return (pg, pd[0], pd[2])[idx % 3]

        def pu_of(idx):
            return (pu, pd[1], pd[3])[idx % 3]

        def qcol_s(c, st):
            o = (c % 2) * 8 + st * 2
            return qcol[:, o:o + 1]

        def qcol_f(c, st):
            o = (c % 2) * 8 + st * 2 + 1
            return qcol[:, o:o + 1]

        def l1(j, i):
            return lncol[:, j * 8 + i:j * 8 + i + 1]

        def l2(j, i):
            return lncol[:, 32 + j * 8 + i:32 + j * 8 + i + 1]

        NW_T = {"g": NDG, "u": NDG, "d": NKI}
        NW_FD = {"g": NI, "u": NI, "d": D}
        WSRC = {"g": wgT_h, "u": wuT_h, "d": wdT_h}
        WSCR = {"g": wgq_s, "u": wuq_s, "d": wdq_s}
        WCOL = {"g": 0, "u": 1, "d": 2}

        _PBPREV = []
        for _w in ("g", "u", "d"):
            for _i in range(NW_T[_w]):
                _PBPREV.append((_w, _i))

        def gb(i):
            return gbufA if i % 2 == 0 else gbufB

        xt_h = x_h.rearrange("(c j p) d -> c j p d", c=NCH, j=NTJ)

        def emit_sync(sync):
            p2 = ctx.pass2

            def dma(out, in_, sem, labels=(), transpose=False):
                i = sync.dma_start(out=out, in_=in_, transpose=transpose) if p2 else None
                ctx.inc(i, sem, 16, labels)

            dma(o128[:], o128_h, "s_c0")
            dma(oh32[:], oh_h, "s_c0")
            dma(ones1[:], ones_h, "s_c0")
            dma(gbufA[:, 0:NDG * 32], wxpT_h, "s_c0")
            dma(gbufA[:, 512:512 + NDG * NST], alog_h, "s_c0")
            dma(gbufB[0:1, 0:D], wdt_h, "s_c0")
            dma(gbufB[0:1, D:2 * D], bdt_h, "s_c0")
            dma(gbufB[0:1, 2 * D:2 * D + 32], bxp_h, "s_c0")
            dma(gbufB[0:1, 3 * D:4 * D], bout_h, "s_c0", labels=[("c0",)])


            for c in range(SEG):
                cf = c - 1
                if c < NCH:
                    for j in range(NTJ):
                        ctx.wait(sync, "v_xln", "xln", c, j)
                        if c > 0 and j == 0:
                            ctx.wait(sync, "v_u", "u", c - 1)
                            ctx.wait(sync, "t_dtz", "dtz", c - 1, NDG - 1)
                            if c >= 2:
                                ctx.wait_if(sync, None, "th", c - 2, 0, 1)
                                ctx.wait_if(sync, None, "th", c - 2, 1, 1)
                        src = tok_v[:, j].rearrange("p (k t) -> p k t", k=NDG)
                        dma(xlnT_v[:, :, j], src, "s_tl", labels=[("tl", c, j)],
                            transpose=True)
                if 1 <= c <= NCH:
                    for j in range(NTJ):
                        ctx.wait(sync, "v_xq", "xq", cf, j)
                        if cf > 0 and j == 0:
                            ctx.wait(sync, "t_u", "gu", cf - 1, NGI - 1)
                        src = tok_v[:, j].rearrange("p (k t) -> p k t", k=NDG)
                        dma(xqT_v[:, :, j], src,
                            "s_tq" if j < 2 else "s_tq1",
                            labels=[("tq", cf, j)], transpose=True)
                    for gi in range(NGI):
                        ic = gi % NIC
                        gg = cf * NGI + gi
                        ctx.wait_if(sync, "t_u", "guG", gg - 2)
                        if cf == 0 and gi <= 1:
                            for uu in range(4 * NW_T["g"] - 4, 4 * NW_T["g"]):
                                ctx.wait(sync, None, "stu", uu)
                        dma(wg_rv[:, gi % 2], wgq_s[ic].rearrange("k p t -> p k t"),
                            f"s_wg{gi % 2}", labels=[("wg", cf, gi)])
                        if cf == 0 and gi <= 1:
                            for uu in range(8 * NW_T["g"] - 4, 8 * NW_T["g"]):
                                ctx.wait(sync, None, "stu", uu)
                        dma(wu_rv[:, gi % 2], wuq_s[ic].rearrange("k p t -> p k t"),
                            f"s_wu{gi % 2}", labels=[("wu", cf, gi)])
                    for h in range(NHALF):
                        for sj in range(STP):
                            st = h * STP + sj
                            sg = cf * NTJ + st
                            ctx.wait(sync, "a_hq2", "hq2", cf, st)
                            if cf == 0:
                                for uu in range(4 * 16 + NW_T["d"] - 4,
                                                4 * 16 + NW_T["d"]):
                                    ctx.wait(sync, None, "stu", uu)
                            if not (cf == 0 and h == 0):
                                ph = (cf, h - 1) if h > 0 else (cf - 1, NHALF - 1)
                                ctx.wait(sync, "t_d", "dmm", ph[0], ph[1])
                            src = hqslot(st).rearrange(
                                "p (k t) -> p k t", k=NKI)
                            dma(hqT_v[:, :, sj], src, f"s_th{st % 2}",
                                labels=[("th", cf, h, sj), ("thg", sg)],
                                transpose=True)
                        for k in range(NKI):
                            kg = (cf * NHALF + h) * NKI + k
                            ctx.wait_if(sync, "t_d", "wdG", kg - 2)
                            if cf == 0:
                                ctx.wait(sync, None, "st", "d", k)
                            dma(wd_rv[:, k % 2], wdq_s[k], f"s_wd{k % 2}",
                                labels=[("wd", cf, h, k)])
            if p2:
                sync.wait_ge(sems["s_out"], ctx.cnt.get("s_out", 0))

        def emit_pool(g):
            p2 = ctx.pass2

            def pstep(f, sem=None, labels=()):
                use = sem or "p_s"
                i = f() if p2 else None
                c = ctx.inc(i, use, 1, labels)
                if p2:
                    g.wait_ge(sems[use], c)

            def dma(out, in_, sem, labels=()):
                i = g.dma_start(out=out, in_=in_) if p2 else None
                ctx.inc(i, sem, 16, labels)

            for hh in range(2):
                dma(xl_v[:, hh], xt_h[0, 0][:, hh * 512:(hh + 1) * 512],
                    f"s_xl{hh}", labels=[("xl", 0, 0, hh)])

            p0ld = []
            p0i2 = 0
            for i in range(NDG):
                def tld(i=i, p0i2=p0i2):
                    if p0i2 == 0:
                        ctx.wait(g, None, "alog_ld")
                        ctx.wait(g, None, "negAexp")
                    if p0i2 == 1:
                        ctx.wait(g, None, "alog_ld")
                    if i >= 2:
                        ctx.wait(g, None, "wos", i - 2)
                    dma(gb(p0i2)[:, 0:D], woT_h[i], f"s_wo{p0i2 % 2}",
                        labels=[("ldwo", i)])
                p0ld.append(tld)
                if i >= 1:
                    def tws(i=i, p0i2=p0i2):
                        ctx.wait(g, None, "ldwo", i - 1)
                        dma(woq_s[i - 1], gb(p0i2 - 1)[:, 0:D],
                            f"s_ws{(i - 1) % 4}", labels=[("wos", i - 1)])
                    p0ld.append(tws)
                p0i2 += 1
            def twsl(p0i2=p0i2):
                ctx.wait(g, None, "ldwo", NDG - 1)
                dma(woq_s[NDG - 1], gb(p0i2 - 1)[:, 0:D],
                    f"s_ws{(NDG - 1) % 4}", labels=[("wos", NDG - 1)])
            p0ld.append(twsl)
            for w in ("g", "u", "d"):
                for i in range(NW_T[w]):
                    def ta2(w=w, i=i, p0i2=p0i2):
                        if p0i2 in (NDG, NDG + 1):
                            ctx.wait(g, None, "wos", p0i2 - 2)
                        ctx.wait_if(g, None, "p0c", p0i2 - 2)
                        dma(gb(p0i2)[:, 0:NW_FD[w]], WSRC[w][i],
                            f"s_pa{p0i2 % 2}", labels=[("ldA", w, i)])
                    p0ld.append(ta2)
                    p0i2 += 1
            PB_UNITS2 = []
            for w in ("g", "u"):
                for i in range(NW_T[w]):
                    for q in range(4):
                        PB_UNITS2.append((w, i, q))
            for i in range(NW_T["d"]):
                PB_UNITS2.append(("d", i, 0))
            for ug, (w, i, q) in enumerate(PB_UNITS2):
                def tbu2(w=w, i=i, q=q, ug=ug):
                    s = ug % 4
                    if ug >= 4:
                        ctx.wait_if(g, None, "pBu", ug - 4)
                    else:
                        ctx.wait(g, "v_s", "pA", "d", NW_T["d"] - 1)
                    srcu = (WSRC[w][i][:, q * D:(q + 1) * D] if w != "d"
                            else WSRC["d"][i])
                    dma(gbufB[:, s * D:(s + 1) * D], srcu,
                        f"s_pb{ug % 4}", labels=[("ldU", ug)])
                p0ld.append(tbu2)

            n_p0ld = len(p0ld)

            def p0ld_drain(kmax, cap=None):
                k = 0
                while p0ld and k < kmax:
                    if cap is not None and n_p0ld - len(p0ld) >= cap:
                        return
                    p0ld.pop(0)()
                    k += 1

            for c in range(SEG):
                cf = c - 1
                if c < NCH:
                    for j in range(1, NTJ):
                        for hh in range(2):
                            hg = (c * NTJ + j) * 2 + hh
                            ctx.wait_if(g, "v_ln", "xlh", hg - 2)
                            dma(xl_v[:, hh],
                                xt_h[c, j][:, hh * 512:(hh + 1) * 512],
                                f"s_xl{hh}", labels=[("xl", c, j, hh)])
                if c + 1 < NCH:
                    for hh in range(2):
                        hg = ((c + 1) * NTJ) * 2 + hh
                        ctx.wait_if(g, "v_ln", "xlh", hg - 2)
                        dma(xl_v[:, hh],
                            xt_h[c + 1, 0][:, hh * 512:(hh + 1) * 512],
                            f"s_xl{hh}", labels=[("xl", c + 1, 0, hh)])
                if 2 <= c <= NCH + 1:
                    ctx.wait(g, "v_dev", "dev", c - 2)
                    ov = out_h.rearrange("(c j p) d -> c j p d", c=NCH, j=NTJ)[c - 2]
                    for j in range(NTJ):
                        dma(ov[j], r2_v[:, j], "s_out",
                            labels=[("out", c - 2)] if j == NTJ - 1 else ())
                if 1 <= c <= NCH:
                    for j in range(NTJ):
                        for oh in range(2):
                            k = cf * NTJ * 2 + j * 2 + oh
                            ctx.wait_if(g, "v_r2", "r2G", k - 2)
                            dma(xr_v[:, (j * 2 + oh) % 2],
                                xt_h[cf, j][:, oh * 512:(oh + 1) * 512],
                                f"s_xr{(j * 2 + oh) % 2}",
                                labels=[("xr", cf, j, oh)])
                if c == 2:
                    p0ld_drain(10 ** 9)
                if c < NCH:
                    for n in range(NST):
                        ctx.wait(g, "v_ym", "ym", c, n)
                        if n == 0:
                            if c > 0:
                                ctx.wait(g, None, "yT", c - 1)
                            pstep(lambda: g.tensor_copy(out=y_acc[:], in_=yp_all[:]),
                                  "p_ya", [("ya", c, n)])
                        else:
                            pstep(lambda: g.tensor_tensor(
                                out=y_acc[:], in0=y_acc[:], in1=yp_all[:], op=Op.add),
                                "p_ya", [("ya", c, n)])
                        p0ld_drain(8, cap=(2 * NDG + sum(NW_T.values()) + 4)
                                   if c == 0 else None)
                if c < NCH:
                    if c >= 1:
                        ctx.wait(g, "t_wout", "wout", c - 1, NTJ - 1, 1)
                    pstep(lambda: g.tensor_copy(out=yT_b[:], in_=y_acc[:]),
                          "p_ya", [("yT", c)])


        def emit_pe(t):
            p2 = ctx.pass2

            def mm(out, lhsT, rhs, start, stop, sem=None, labels=()):
                i = t.matmul(out=out, lhsT=lhsT, rhs=rhs, start=start, stop=stop) if p2 else None
                if sem:
                    ctx.inc(i, sem, 1, labels)

            for c in range(SEG):
                cf = c - 1
                if c < NCH:
                    ctx.wait(t, "s_tl", "tl", c, NTJ - 1)
                    if c >= 1:
                        ctx.wait(t, "a_bce", "bceB", c - 1, NST - 1)
                    for dg in range(NDG):
                        mm(pr[0][0:32, 0:TT], wxpT_v[:, dg],
                           xlnT_v[:, dg].rearrange("p j t -> p (j t)"), dg == 0, False)
                    mm(pr[0][0:32, 0:TT], bxp_b[:], ones1[:], False, True,
                       "t_bc", [("bc", c)])
                    for dg in range(NDG):
                        pp = pr[dg % 2]
                        if dg == 0:
                            ctx.wait(t, "v_bce", "bce", c)
                        elif dg == 1:
                            if c >= 1:
                                ctx.wait(t, "a_bce", "bceC", c - 1, NST - 1)
                        else:
                            ctx.wait(t, "a_spp", "spp", c, dg - 2)
                        mm(pp[:, 0:TT], wdt_b[0:1, dg * 128:(dg + 1) * 128],
                           xlnT[0:1, 0:TT], True, False)
                        mm(pp[:, 0:TT], bdt_b[0:1, dg * 128:(dg + 1) * 128],
                           ones1[:], False, True, "t_dtz", [("dtz", c, dg)])

                def bbc_pair(n, c=c):
                    if n == 0:
                        ctx.wait(t, "a_spp", "spp", c, NDG - 2)
                        ctx.wait(t, "v_bce", "bce", c)
                    else:
                        ctx.wait(t, "a_bce", "bceB", c, n - 1)
                    mm(pr[0][:, 0:TT], oh_v[:, n], bct[:], True, True,
                       "t_bbc", [("bbcB", c, n)])
                    if n == 0:
                        ctx.wait(t, "a_spp", "spp", c, NDG - 1)
                    else:
                        ctx.wait(t, "a_bce", "bceC", c, n - 1)
                    mm(pr[1][:, 0:TT], oh_v[:, NST + n], bct[:], True, True,
                       "t_bbc", [("bbcC", c, n)])

                if c < NCH:
                    bbc_pair(0)
                    if c == 0:
                        for n in range(1, NST):
                            bbc_pair(n)
                        ctx.wait(t, "v_s", "partials")
                        mm(pd[0][:, 0:4], o128[:], wsum[:, 0:4], True, True,
                           "t_p0", [("tot",)])
                if 1 <= c <= NCH:
                    ctx.wait(t, None, "yT", cf)
                    for rnd in range(2):
                        for dg in range(NDG):
                            wog = (cf * 2 + rnd) * NDG + dg
                            ctx.wait(t, None, "wol", cf, rnd, dg)
                            for b in range(4):
                                k = rnd * 4 + b
                                j, oh = k // 2, k % 2
                                if dg == 0:
                                    if rnd == 0:
                                        if c >= 2:
                                            ctx.wait(t, "v_dev", "devst",
                                                     c - 2, 1, b // 2, b % 2)
                                        elif b == 0:
                                            ctx.wait(t, "v_s", "cols")
                                    else:
                                        ctx.wait(t, "v_r2", "r2",
                                                 cf, b // 2, b % 2)
                                mm(pd[b][:], yT_v[:, dg, j],
                                   woT_rv[:, wog % 4, oh * 512:(oh + 1) * 512],
                                   dg == 0, False,
                                   "t_wout" if b == 3 else None,
                                   [("wocG", wog)])
                        for b in range(4):
                            k = rnd * 4 + b
                            j, oh = k // 2, k % 2
                            mm(pd[b][:], ones1[0:1, 0:128],
                               bout_b[0:1, oh * 512:(oh + 1) * 512], False, True,
                               "t_wout", [("wout", cf, j, oh)])
                if 1 <= c < NCH:
                    bbc_pair(1)
                    bbc_pair(2)
                    if c == 1:
                        for m in range(3, NST):
                            bbc_pair(m)
                for gi in range(NGI):
                    h, ic = divmod(gi, NIC)
                    if 1 <= c <= NCH and gi == 0:
                        ctx.wait(t, "s_tq", "tq", cf, 1)
                    if 1 <= c <= NCH and gi == NIC:
                        ctx.wait(t, "s_tq1", "tq", cf, 3)
                    if 1 <= c <= NCH:
                        gg = cf * NGI + gi
                        ctx.wait(t, None, "wg", cf, gi)
                        ctx.wait(t, None, "wu", cf, gi)
                        for sj in range(STP):
                            st = h * STP + sj
                            idx = gi * STP + sj
                            pidx = cf * NPAIR + idx
                            if gi == 0 and sj == 1:
                                ctx.wait(t, "v_r2", "r2", cf, 2, 0)
                                ctx.wait(t, "v_r2", "r2", cf, 2, 1)
                            if gi == 1 and sj == 0:
                                ctx.wait(t, "v_r2", "r2", cf, 3, 0)
                                ctx.wait(t, "v_r2", "r2", cf, 3, 1)
                            ctx.wait_if(t, "a_rel", "relG", pidx - 3)
                            for dg in range(NDG):
                                mm(pg_of(idx)[:, 0:ICW], xqT_v[:, dg, st],
                                   wg_rv[:, gi % 2, dg],
                                   dg == 0, dg == NDG - 1,
                                   "t_g" if dg == NDG - 1 else None,
                                   [("g", cf, idx)])
                            ctx.wait_if(t, "v_h", "hG", pidx - 3)
                            for dg in range(NDG):
                                labs = [("u", cf, idx)]
                                if sj == STP - 1:
                                    labs += [("gu", cf, gi), ("guG", gg)]
                                mm(pu_of(idx)[:, 0:ICW], xqT_v[:, dg, st],
                                   wu_rv[:, gi % 2, dg],
                                   dg == 0, dg == NDG - 1,
                                   "t_u" if dg == NDG - 1 else None, labs)
                    if 2 <= c < NCH and gi % 2 == 1:
                        bbc_pair((gi + 5) // 2)
                if 1 <= c <= NCH:
                    m0 = (NGI - 1 + 5) // 2 + 1
                    for h in range(NHALF):
                        ctx.wait(t, None, "th", cf, h, 0)
                        ctx.wait(t, None, "th", cf, h, STP - 1)
                        for k in range(NKI):
                            if (h == 0 and 2 <= c < NCH and k % 6 == 4
                                    and m0 + (k - 4) // 6 < NST):
                                bbc_pair(m0 + (k - 4) // 6)
                            ctx.wait(t, "s_wd", "wd", cf, h, k)
                            kg = (cf * NHALF + h) * NKI + k
                            for sj in range(STP):
                                for dh in range(2):
                                    pp = pd[sj * 2 + dh]
                                    if k == 0:
                                        if h == 0:
                                            ctx.wait(t, "v_r2", "r2",
                                                     cf, 2 + sj, dh)
                                            b = sj * 2 + dh
                                            li = (NPAIR - 1 if b in (0, 1)
                                                  else NPAIR - 3)
                                            if dh == 0:
                                                ctx.wait(t, "a_rel", "rel",
                                                         cf, li)
                                            else:
                                                ctx.wait(t, "v_h", "h",
                                                         cf, li)
                                        else:
                                            ctx.wait(t, "v_dev", "devst",
                                                     cf, 0, sj, dh)
                                    labs = [("wdG", kg)]
                                    if k == NKI - 1:
                                        labs.append(("dmm", cf, h))
                                    mm(pp[:], hqT_v[:, k, sj],
                                       wd_rv[:, k % 2, dh * 512:(dh + 1) * 512],
                                       k == 0, k == NKI - 1,
                                       "t_d" if (sj == STP - 1 and dh == 1) else None,
                                       labs)
                        if h == 0 and 2 <= c < NCH:
                            for m in range(m0 + 5, NST):
                                bbc_pair(m)

        def emit_act(a):
            p2 = ctx.pass2

            def adma(out, in_, sem, labels=()):
                i = a.dma_start(out=out, in_=in_) if p2 else None
                ctx.inc(i, sem, 16, labels)

            pstq = []
            PB_UNITS3 = []
            for w in ("g", "u"):
                for i in range(NW_T[w]):
                    for q in range(4):
                        PB_UNITS3.append((w, i, q))
            for i in range(NW_T["d"]):
                PB_UNITS3.append(("d", i, 0))
            for ug, (w, i, q) in enumerate(PB_UNITS3):
                def tst(w=w, i=i, q=q, ug=ug):
                    s = ug % 4
                    ctx.wait(a, "v_s", "pBu", ug)
                    srcu = wd_rv[:, ug % 2]
                    sem = "s_st" if ug % 4 == 0 else f"s_st{ug % 4}"
                    labs = [("stu", ug)]
                    if w == "d":
                        labs.append(("st", "d", i))
                        adma(WSCR["d"][i], srcu, sem, labels=labs)
                    else:
                        if q == 3:
                            labs.append(("st", w, i))
                        adma(WSCR[w][2 * q:2 * q + 2, i]
                             .rearrange("ic p t -> p ic t"),
                             srcu.rearrange("p (ic t) -> p ic t", ic=2),
                             sem, labels=labs)
                pstq.append(tst)

            def pst_drain(kmax):
                k = 0
                while pstq and k < kmax:
                    pstq.pop(0)()
                    k += 1

            def wol_round(cf, rnd):
                for dg in range(NDG):
                    wog = (cf * 2 + rnd) * NDG + dg
                    ctx.wait_if(a, None, "wocG", wog - 4)
                    if cf == 0 and rnd == 0:
                        ctx.wait(a, f"s_ws{dg % 4}", "wos", 4 + dg % 4)
                    adma(woT_rv[:, wog % 4], woq_s[dg], f"s_wl{wog % 4}",
                         labels=[("wol", cf, rnd, dg)])

            def astep(f, sem=None, labels=()):
                use = sem or "a_s"
                i = f() if p2 else None
                c = ctx.inc(i, use, 1, labels)
                if p2:
                    a.wait_ge(sems[use], c)

            def act(out, in_, func, bias=None, scale=None):
                kw = {}
                if bias is not None:
                    kw["bias"] = bias
                if scale is not None:
                    kw["scale"] = scale
                return a.activation(out=out, in_=in_, func=func, **kw)

            def relu_step(cf, idx):
                pidx = cf * NPAIR + idx
                ctx.wait(a, "t_g", "g", cf, idx)
                ctx.wait_if(a, "v_h", "hG", pidx - 3)
                astep(lambda idx=idx: act(r_rv[:, idx % 3],
                                          pg_of(idx)[:, 0:ICW], AF.Relu),
                      None, [("relG0", pidx)])
                astep(lambda idx=idx: act(r_rv[:, idx % 3], r_rv[:, idx % 3],
                                          AF.Square),
                      "a_rel", [("rel", cf, idx), ("relG", pidx)])

            act_queue = []
            act_slot = [0]

            def hq2_drain(kmax):
                k = 0
                while act_queue and k < kmax:
                    cfq, st, sj, q = act_queue.pop(0)
                    qg = act_slot[0]
                    act_slot[0] += 1
                    h = st // STP
                    lastpiece = 2 * (h * NIC + (q * 512 + 511) // ICW) + sj
                    ctx.wait(a, "v_h", "h", cfq, lastpiece)
                    ctx.wait(a, None, "qcol", cfq, st)
                    if q == 0:
                        ctx.wait(a, None, "xq", cfq, NTJ - 1)
                    astep(lambda st=st, q=q, qg=qg, cfq=cfq, sj=sj: act(
                        t1q[:],
                        hf_v[:, sj][:, q * 512:(q + 1) * 512],
                        AF.Identity, scale=qcol_s(cfq, st), bias=ccol[:, 3:4]),
                          None, [("hq1", cfq, st, q)])
                    if q == 0:
                        sg = cfq * NTJ + st
                        ctx.wait_if(a, "s_th", "thg", sg - 2)
                    if st % 2 == 1 and q == 0:
                        ctx.wait_if(a, None, "u", cfq + 1)
                        ctx.wait_if(a, None, "dtz", cfq + 1, NDG - 1)
                    if st % 2 == 0 and q == 0:
                        ctx.wait_if(a, None, "dt", cfq + 1, NDG - 1)
                    labels = []
                    if q == NQ - 1:
                        labels.append(("hq2", cfq, st))
                    astep(lambda st=st, q=q, qg=qg: act(
                        hqslot(st)[:, q * 512:(q + 1) * 512],
                        t1q[:], AF.Identity, bias=ccol[:, 2:3]),
                          "a_hq2", labels)
                    k += 1

            ctx.wait(a, "v_s", "alog_ld")
            astep(lambda: act(negA[:], gbufA[:, 512:512 + NDG * NST], AF.Exp),
                  "a_rstd", [("negAexp",)])

            for c in range(SEG):
                cf = c - 1
                if c < NCH:
                    for j in range(NTJ):
                        ctx.wait(a, "v_ln", "ln1", c, j)
                        astep(lambda j=j: act(l1(j, 1), l1(j, 1), AF.Ln,
                                              bias=ccol[:, 0:1]))
                        astep(lambda j=j: act(l1(j, 1), l1(j, 1), AF.Exp,
                                              scale=ccol[:, 1:2]),
                              "a_rstd", [("rstd1", c, j)])
                if c < NCH:
                    for dg in range(NDG):
                        pp = pr[dg % 2]
                        ctx.wait(a, "t_dtz", "dtz", c, dg)
                        if dg >= 1:
                            ctx.wait(a, None, "dt", c, dg - 1)
                        elif c >= 1:
                            ctx.wait(a, None, "dt", c - 1, NDG - 1)
                        if dg == 0 and c >= 2:
                            ctx.wait_if(a, None, "th", c - 2, 1, 0)
                        astep(lambda pp=pp: act(sp1[:], pp[:, 0:TT], AF.Abs))
                        astep(lambda pp=pp: act(sp2[:], pp[:, 0:TT], AF.Relu),
                              "a_spp", [("spp", c, dg)])
                        astep(lambda: act(sp1[:], sp1[:], AF.Exp, scale=-1.0))
                        astep(lambda: act(sp1[:], sp1[:], AF.Ln, bias=1.0),
                              "a_sp", [("sp", c, dg)])
                if c == 2:
                    pst_drain(10 ** 9)
                if 1 <= c <= NCH:
                    wol_round(cf, 0)
                    wol_round(cf, 1)
                if 1 <= c <= NCH:
                    for j in range(NTJ):
                        ctx.wait(a, "v_ln", "ln2", cf, j)
                        astep(lambda j=j: act(l2(j, 1), l2(j, 1), AF.Ln,
                                              bias=ccol[:, 0:1]))
                        astep(lambda j=j: act(l2(j, 1), l2(j, 1), AF.Exp,
                                              scale=ccol[:, 1:2]),
                              "a_rstd", [("rstd2", cf, j)])
                if c < NCH:
                    for n in range(NST):
                        m = c * NST + n
                        ctx.wait(a, "t_bbc", "bbcB", c, n)
                        ctx.wait_if(a, "v_dB", "dBG", m - 2)
                        astep(lambda n=n: act(bcB_v[:, n % 2], pr[0][:, 0:TT],
                                              AF.Copy),
                              "a_bce", [("bceB", c, n)])
                        ctx.wait(a, "t_bbc", "bbcC", c, n)
                        ctx.wait_if(a, "v_ym", "ymG", m - 2)
                        astep(lambda n=n: act(bcC_v[:, n % 2], pr[1][:, 0:TT],
                                              AF.Copy),
                              "a_bce", [("bceC", c, n)])
                        for dg in range(NDG):
                            gidx = m * NDG + dg
                            ctx.wait(a, "v_dt", "dt", c, dg)
                            ctx.wait_if(a, "v_scan", "scani", gidx - 2)
                            astep(lambda n=n, dg=dg, gidx=gidx: act(
                                dA_v[:, gidx % 2], dt_v[:, dg], AF.Exp,
                                scale=negA[:, dg * NST + n:dg * NST + n + 1]),
                                "s_dA", [("dA", c, n, dg)])
                        if c == 1:
                            pst_drain(8)
                        if 1 <= c <= NCH:
                            ps = 99 if c == 1 else 1
                            rs = 99 if c == 1 else 1
                            if n == ps + 4:
                                act_queue.extend(hq_entries(cf, 0))
                            if c >= 2 and n == 9:
                                act_queue.extend(hq_entries(cf, 1))
                            if n >= ps + 4:
                                hq2_drain(4)
                            if n >= rs and 4 * (n - rs) < NPAIR:
                                for q in range(4):
                                    relu_step(cf, 4 * (n - rs) + q)
                elif 1 <= c <= NCH:
                    for idx in range(NPAIR):
                        relu_step(cf, idx)
                        if idx == NPAIR // 2:
                            act_queue.extend(hq_entries(cf, 0))
                        if idx >= NPAIR // 2:
                            hq2_drain(1)
                if c == 1:
                    pst_drain(10 ** 9)
                    for idx in range(NPAIR):
                        relu_step(cf, idx)
                        if idx == NPAIR // 2:
                            act_queue.extend(hq_entries(cf, 0))
                        if idx >= NPAIR // 2:
                            hq2_drain(1)
                if 1 <= c <= NCH:
                    hq2_drain(10 ** 9)
                    if c == 1 or c == NCH:
                        act_queue.extend(hq_entries(cf, 1))
                        hq2_drain(10 ** 9)

        def hq_entries(cf, h):
            out = []
            for q in range(NQ):
                for sj in range(STP):
                    out.append((cf, h * STP + sj, sj, q))
            return out


        def vstep_factory(v):
            def vstep(f, sem=None, labels=()):
                use = sem or "v_s"
                i = f() if ctx.pass2 else None
                c = ctx.inc(i, use, 1, labels)
                if ctx.pass2:
                    v.wait_ge(sems[use], c)
            return vstep

        def emit_dve(v):
            p2 = ctx.pass2
            vstep = vstep_factory(v)

            def ts(out, in0, s1, s2, op0, op1):
                return v.tensor_scalar(out=out, in0=in0, scalar1=s1, scalar2=s2,
                                       op0=op0, op1=op1)

            ctx.wait(v, "s_scr", "c0")
            vstep(lambda: v.memset(ccol[:, 0:1], EPS_LN))
            vstep(lambda: v.memset(ccol[:, 1:2], -0.5))
            vstep(lambda: v.memset(ccol[:, 2:3], -MAGIC))
            vstep(lambda: v.memset(ccol[:, 3:4], MAGIC))
            vstep(lambda: v.memset(carr_v[:, 1], 0.0))
            vstep(lambda: v.memset(wsum[:], 0.0))
            vstep(lambda: v.tensor_copy(out=wxpT_b[:], in_=gbufA[:, 0:NDG * 32]))
            vstep(lambda: v.tensor_copy(out=wdt_b[:], in_=gbufB[0:1, 0:D]))
            vstep(lambda: v.tensor_copy(out=bdt_b[:], in_=gbufB[0:1, D:2 * D]))
            vstep(lambda: v.tensor_copy(out=bxp_b[:], in_=gbufB[0:1, 2 * D:2 * D + 32]))
            vstep(lambda: v.tensor_copy(out=bout_b[:], in_=gbufB[0:1, 3 * D:4 * D]),
                  labels=[("alog_ld",)])
            ctx.wait(v, "a_rstd", "negAexp")
            vstep(lambda: ts(negA[:], negA[:], -1.0, 0.0, Op.mult, Op.add))
            p0q = []
            p0i = NDG
            for w in ("g", "u", "d"):
                col = WCOL[w]
                for i in range(NW_T[w]):
                    def ta(w=w, i=i, p0i=p0i, col=col):
                        ctx.wait(v, None, "ldA", w, i)
                        vstep(lambda: v.tensor_reduce(
                            out=wsum[:, 3:4], in_=gb(p0i)[:, 0:NW_FD[w]],
                            axis=AX.X, op=Op.add, apply_absolute_value=True),
                            labels=[("p0c", p0i)])
                        vstep(lambda: v.tensor_tensor(
                            out=wsum[:, col:col + 1], in0=wsum[:, col:col + 1],
                            in1=wsum[:, 3:4], op=Op.add), labels=[("pA", w, i)])
                    p0q.append(ta)
                    p0i += 1

            def tpart():
                vstep(lambda: v.memset(wsum[:, 3:4], 0.0),
                      labels=[("partials",)])
            p0q.append(tpart)
            NG = float(NDG * 128 * NI)
            NDW = float(NKI * 128 * D)

            def tmeans():
                ctx.wait(v, "t_p0", "tot")
                vstep(lambda: ts(mwc[:, 5:6], pd[0][:, 0:1], 1.0 / NG, EPS_Q,
                                 Op.mult, Op.max))
                vstep(lambda: ts(mwc[:, 6:7], pd[0][:, 1:2], 1.0 / NG, EPS_Q,
                                 Op.mult, Op.max))
                vstep(lambda: ts(mwc[:, 3:4], pd[0][:, 2:3], 1.0 / NDW, EPS_Q,
                                 Op.mult, Op.max))
                vstep(lambda: v.reciprocal(out=mwc[:, 0:1], in_=mwc[:, 5:6]))
                vstep(lambda: v.reciprocal(out=mwc[:, 1:2], in_=mwc[:, 6:7]))
                vstep(lambda: v.reciprocal(out=mwc[:, 2:3], in_=mwc[:, 3:4]))
                vstep(lambda: v.tensor_tensor(out=mwc[:, 4:5], in0=mwc[:, 5:6],
                                              in1=mwc[:, 5:6], op=Op.mult))
                vstep(lambda: v.tensor_tensor(out=mwc[:, 4:5], in0=mwc[:, 4:5],
                                              in1=mwc[:, 6:7], op=Op.mult),
                      labels=[("cols",)])
            p0q.append(tmeans)
            PB_UNITS = []
            for w in ("g", "u"):
                for i in range(NW_T[w]):
                    for q in range(4):
                        PB_UNITS.append((w, i, q))
            for i in range(NW_T["d"]):
                PB_UNITS.append(("d", i, 0))
            for ug, (w, i, q) in enumerate(PB_UNITS):
                def tbu(w=w, i=i, q=q, ug=ug):
                    col = WCOL[w]
                    s = ug % 4
                    gbd = gbufB[:, s * D:(s + 1) * D]
                    g16 = wd_rv[:, ug % 2]
                    ctx.wait(v, None, "ldU", ug)
                    if ug >= 2:
                        ctx.wait(v, None, "stu", ug - 2)
                    vstep(lambda: ts(gbd, gbd, mwc[:, col:col + 1], MAGIC,
                                     Op.mult, Op.add))
                    vstep(lambda: ts(gbd, gbd, MAGIC, -1.0,
                                     Op.subtract, Op.max))
                    labs = [("pBu", ug)]
                    if w == "d" or q == 3:
                        labs.append(("pBq", w, i))
                    vstep(lambda: ts(g16, gbd, 1.0, 0.0, Op.min, Op.add),
                          labels=labs)
                p0q.append(tbu)

            n_p0q = len(p0q)

            def p0_drain(kmax, cap=None):
                k = 0
                while p0q and k < kmax:
                    if cap is not None and n_p0q - len(p0q) >= cap:
                        return
                    p0q.pop(0)()
                    k += 1

            for c in range(SEG):
                cf = c - 1
                if c == 1:
                    p0_drain(2)
                if c == 2:
                    p0_drain(10 ** 9)
                if c < NCH:
                    for j in range(NTJ):
                        for hh in range(2):
                            ctx.wait(v, None, "xl", c, j, hh)
                            vstep(lambda j=j, hh=hh: v.bn_stats(
                                out=bns[:, hh * 6:hh * 6 + 6], in_=xl_v[:, hh]))
                        vstep(lambda j=j: v.bn_aggr(
                            out=lncol[:, j * 8:j * 8 + 2], in_=bns[:]))
                        vstep(lambda j=j: ts(l1(j, 2), l1(j, 0), -1.0, 0.0,
                                             Op.mult, Op.add),
                              "v_ln", [("ln1", c, j)])
                        ctx.wait(v, "a_rstd", "rstd1", c, j)
                        if c >= 2:
                            ctx.wait(v, "s_tq", "tq", c - 2, NTJ - 1)
                        jg = c * NTJ + j
                        for hh in range(2):
                            hg = (c * NTJ + j) * 2 + hh
                            labs = [("xlh", hg)]
                            if hh == 1:
                                labs += [("xln", c, j), ("xlnG", jg)]
                            vstep(lambda j=j, c=c, hh=hh: ts(
                                tok_v[:, j][:, hh * 512:(hh + 1) * 512],
                                xl_v[:, hh], l1(j, 2), l1(j, 1),
                                Op.add, Op.mult),
                                "v_ln" if hh == 0 else "v_xln", labs)
                if 2 <= c <= NCH + 1:
                    cd = c - 2
                    ctx.wait(v, "t_d", "dmm", cd, 1)
                    for sj in range(STP):
                        st = 2 + sj
                        for dh in range(2):
                            def f(sj=sj, st=st, dh=dh, cd=cd):
                                sl = r2_v[:, st][:, dh * 512:(dh + 1) * 512]
                                return v.scalar_tensor_tensor(
                                    out=sl, in0=pd[sj * 2 + dh][:],
                                    scalar=qcol_f(cd, st), in1=sl,
                                    op0=Op.mult, op1=Op.add)
                            labs = [("devst", cd, 1, sj, dh)]
                            if sj == STP - 1 and dh == 1:
                                labs.append(("dev", cd))
                            vstep(f, "v_dev", labs)
                if c < NCH:
                    ctx.wait(v, "t_bc", "bc", c)
                    vstep(lambda: v.tensor_copy(out=bct[:], in_=pr[0][0:32, 0:TT]),
                          "v_bce", [("bce", c)])
                    for dg in range(NDG):
                        ctx.wait(v, "a_sp", "sp", c, dg)
                        vstep(lambda dg=dg: v.tensor_tensor(
                            out=dt_v[:, dg], in0=sp1[:], in1=sp2[:], op=Op.add),
                            "v_dt", [("dt", c, dg)])
                    ctx.wait(v, "s_tl", "tl", c, NTJ - 1)
                    vstep(lambda: v.tensor_tensor(
                        out=u_all[:], in0=dt_all[:], in1=xlnT[:], op=Op.mult),
                        "v_u", [("u", c)])
                if 1 <= c <= NCH:
                    for j in range(NTJ):
                        for oh in range(2):
                            k = cf * NTJ * 2 + j * 2 + oh
                            ctx.wait(v, "t_wout", "wout", cf, j, oh)
                            ctx.wait(v, "s_x", "xr", cf, j, oh)
                            if j == 0 and oh == 0 and cf >= 1:
                                ctx.wait(v, "s_out", "out", cf - 1)
                            def f(j=j, oh=oh):
                                sl = r2_v[:, j][:, oh * 512:(oh + 1) * 512]
                                return v.scalar_tensor_tensor(
                                    out=sl, in0=pd[(j * 2 + oh) % 4][:], scalar=1.0,
                                    in1=xr_v[:, (j * 2 + oh) % 2],
                                    op0=Op.mult, op1=Op.add)
                            labs = [("r2", cf, j, oh), ("r2G", k)]
                            if j == NTJ - 1 and oh == 1:
                                labs.append(("r2done", cf))
                            vstep(f, "v_r2", labs)
                    for j in range(NTJ):
                        vstep(lambda j=j: v.bn_stats(out=bns[:, 0:6],
                                                     in_=r2_v[:, j][:, 0:512]))
                        vstep(lambda j=j: v.bn_stats(out=bns[:, 6:12],
                                                     in_=r2_v[:, j][:, 512:1024]))
                        vstep(lambda j=j: v.bn_aggr(
                            out=lncol[:, 32 + j * 8:32 + j * 8 + 2], in_=bns[:]))
                        vstep(lambda j=j: ts(l2(j, 2), l2(j, 0), -1.0, 0.0,
                                             Op.mult, Op.add),
                              "v_ln", [("ln2", cf, j)])
                        vstep(lambda j=j: v.tensor_reduce(
                            out=l2(j, 3), in_=r2_v[:, j], axis=AX.X, op=Op.max))
                        vstep(lambda j=j: v.tensor_reduce(
                            out=l2(j, 4), in_=r2_v[:, j], axis=AX.X, op=Op.min))
                        vstep(lambda j=j: v.scalar_tensor_tensor(
                            out=l2(j, 3), in0=l2(j, 3), scalar=1.0, in1=l2(j, 2),
                            op0=Op.mult, op1=Op.add))
                        vstep(lambda j=j: v.scalar_tensor_tensor(
                            out=l2(j, 4), in0=l2(j, 4), scalar=-1.0, in1=l2(j, 2),
                            op0=Op.mult, op1=Op.subtract))
                        vstep(lambda j=j: v.tensor_tensor(
                            out=l2(j, 3), in0=l2(j, 3), in1=l2(j, 4), op=Op.max))
                    for j in range(NTJ):
                        ctx.wait(v, "a_rstd", "rstd2", cf, j)
                        vstep(lambda j=j: v.tensor_tensor(
                            out=l2(j, 3), in0=l2(j, 3), in1=l2(j, 1), op=Op.mult))
                        vstep(lambda j=j: ts(l2(j, 3), l2(j, 3), EPS_Q, 0.0,
                                             Op.max, Op.add))
                        vstep(lambda j=j: v.reciprocal(out=l2(j, 5), in_=l2(j, 3)))
                        vstep(lambda j=j: ts(l2(j, 5), l2(j, 5), 127.0, 0.0,
                                             Op.mult, Op.add))
                        vstep(lambda j=j: v.tensor_tensor(
                            out=l2(j, 6), in0=l2(j, 1), in1=l2(j, 5), op=Op.mult))
                        vstep(lambda j=j: ts(l2(j, 3), l2(j, 3), 1.0 / 127.0, 0.0,
                                             Op.mult, Op.add))
                        vstep(lambda j=j: v.tensor_tensor(
                            out=l2(j, 4), in0=l2(j, 3), in1=l2(j, 3), op=Op.mult))
                        vstep(lambda j=j: v.tensor_tensor(
                            out=l2(j, 4), in0=l2(j, 4), in1=l2(j, 3), op=Op.mult))
                        vstep(lambda j=j: v.tensor_tensor(
                            out=l2(j, 7), in0=l2(j, 4), in1=mwc[:, 4:5], op=Op.mult))
                        if c < NCH:
                            ctx.wait(v, "s_tl", "tl", c, NTJ - 1)
                        else:
                            ctx.wait(v, "s_tq", "tq", cf - 1, NTJ - 1)
                        if cf >= 1 and j == 0:
                            ctx.wait(v, "a_hq2", "hq2", cf - 1, NTJ - 1)
                        for ph in range(2):
                            vstep(lambda j=j, ph=ph: ts(
                                t1q[:], r2_v[:, j][:, ph * 512:(ph + 1) * 512],
                                l2(j, 2), l2(j, 6), Op.add, Op.mult))
                            vstep(lambda j=j, ph=ph: ts(
                                tok_v[:, j][:, ph * 512:(ph + 1) * 512],
                                t1q[:], MAGIC, MAGIC, Op.add, Op.subtract),
                                "v_xq" if ph == 1 else None,
                                [("xq", cf, j)] if ph == 1 else ())
                if c < NCH:
                    for n in range(NST):
                        m = c * NST + n
                        ctx.wait(v, "a_bce", "bceB", c, n)
                        def fdB(n=n):
                            bc = bcB_v[:, n % 2].unsqueeze(1).broadcast_to(
                                [128, NDG, TT])
                            return v.tensor_tensor(out=dB_v, in0=u_v, in1=bc,
                                                   op=Op.mult)
                        vstep(fdB, "v_dB", [("dB", c, n), ("dBG", m)])
                        for dg in range(NDG):
                            gidx = m * NDG + dg
                            ctx.wait(v, "s_dA", "dA", c, n, dg)
                            def fsc(n=n, dg=dg, gidx=gidx, c=c):
                                col = n * NDG + dg
                                return v.tensor_tensor_scan(
                                    out=dB_v[:, dg], data0=dA_v[:, gidx % 2],
                                    data1=dB_v[:, dg],
                                    initial=carr_v[:, (c + 1) % 2][:, col:col + 1],
                                    op0=Op.mult, op1=Op.add)
                            vstep(fsc, "v_scan", [("scani", gidx)])
                        def fc(n=n, c=c):
                            src = dB_all[:].rearrange(
                                "p (k t) -> p k t", k=NDG)[:, :, TT - 1]
                            return v.tensor_copy(
                                out=carr_v[:, c % 2][:, n * NDG:(n + 1) * NDG],
                                in_=src)
                        vstep(fc)
                        ctx.wait(v, "a_bce", "bceC", c, n)
                        if m >= 1:
                            ctx.wait(v, "p_ya", "ya", (m - 1) // NST, (m - 1) % NST)
                        def fy(n=n):
                            bc = bcC_v[:, n % 2].unsqueeze(1).broadcast_to(
                                [128, NDG, TT])
                            return v.tensor_tensor(out=yp_v, in0=dB_v, in1=bc,
                                                   op=Op.mult)
                        vstep(fy, "v_ym", [("ym", c, n), ("ymG", m)])
                        if c * NST + n >= 3:
                            p0_drain(8, cap=(sum(NW_T.values()) + 1)
                                     if c == 0 else None)
                        if 1 <= c <= NCH:
                            ps = 99 if c == 1 else 1
                            if n >= ps and 4 * (n - ps) < NPAIR:
                                for q in range(4):
                                    hstep(v, vstep, cf, 4 * (n - ps) + q, c)
                    if c == 1:
                        for idx in range(NPAIR):
                            hstep(v, vstep, cf, idx, c)
                elif 1 <= c <= NCH:
                    for idx in range(NPAIR):
                        hstep(v, vstep, cf, idx, c)
                if 1 <= c <= NCH:
                    ctx.wait(v, "t_d", "dmm", cf, 0)
                    for sj3 in range(STP):
                        st3 = sj3
                        for dh in range(2):
                            def fd(sj3=sj3, st3=st3, dh=dh, cf=cf):
                                sl = r2_v[:, st3][:, dh * 512:(dh + 1) * 512]
                                return v.scalar_tensor_tensor(
                                    out=sl, in0=pd[sj3 * 2 + dh][:],
                                    scalar=qcol_f(cf, st3), in1=sl,
                                    op0=Op.mult, op1=Op.add)
                            vstep(fd, "v_dev", [("devst", cf, 0, sj3, dh)])

        def hstep(v, vstep, cf, idx, c):
            gi, sj = divmod(idx, STP)
            h, ic = divmod(gi, NIC)
            pidx = cf * NPAIR + idx
            ctx.wait(v, "a_rel", "rel", cf, idx)
            ctx.wait(v, "t_u", "u", cf, idx)
            if cf == 0 and idx == 0:
                ctx.wait(v, None, "pA", "d", NW_T["d"] - 1)
            if h == 1:
                ctx.wait_if(v, None, "hq1", cf, sj, ic * ICW // 512)
            else:
                ctx.wait_if(v, None, "hq1", cf - 1, STP + sj, ic * ICW // 512)
            rv = r_rv[:, idx % 3]

            def fh():
                return v.tensor_tensor(
                    out=hf_v[:, sj][:, ic * ICW:(ic + 1) * ICW],
                    in0=rv, in1=pu_of(idx)[:, 0:ICW], op=Op.mult)
            vstep(fh, "v_h", [("h", cf, idx), ("hG", pidx)])

            def fm():
                return v.tensor_reduce(
                    out=maxp_v[:, sj][:, ic:ic + 1],
                    in_=hf_v[:, sj][:, ic * ICW:(ic + 1) * ICW], axis=AX.X,
                    op=Op.max, apply_absolute_value=True)
            vstep(fm)
            if ic == NIC - 1 and sj == STP - 1:
                for sj2 in range(STP):
                    st2 = h * STP + sj2
                    j = st2
                    vstep(lambda sj2=sj2: v.tensor_reduce(
                        out=lncol[:, 6:7], in_=maxp_v[:, sj2], axis=AX.X,
                        op=Op.max))
                    vstep(lambda j=j: v.reciprocal(out=lncol[:, 7:8],
                                                   in_=l2(j, 7)))
                    vstep(lambda: v.tensor_scalar(
                        out=lncol[:, 7:8], in0=lncol[:, 7:8], scalar1=EPS_Q,
                        scalar2=0.0, op0=Op.mult, op1=Op.add))
                    vstep(lambda: v.tensor_tensor(
                        out=lncol[:, 6:7], in0=lncol[:, 6:7],
                        in1=lncol[:, 7:8], op=Op.max))
                    vstep(lambda cf=cf, st2=st2: v.reciprocal(
                        out=qcol_s(cf, st2), in_=lncol[:, 6:7]))
                    vstep(lambda cf=cf, st2=st2: v.tensor_scalar(
                        out=qcol_s(cf, st2), in0=qcol_s(cf, st2), scalar1=127.0,
                        scalar2=0.0, op0=Op.mult, op1=Op.add),
                        labels=[("qcol", cf, st2)])
                    vstep(lambda cf=cf, st2=st2, j=j: v.tensor_tensor(
                        out=qcol_f(cf, st2), in0=l2(j, 7), in1=lncol[:, 6:7],
                        op=Op.mult))
                    vstep(lambda cf=cf, st2=st2: v.tensor_tensor(
                        out=qcol_f(cf, st2), in0=qcol_f(cf, st2), in1=mwc[:, 3:4],
                        op=Op.mult))
                    vstep(lambda cf=cf, st2=st2: v.tensor_scalar(
                        out=qcol_f(cf, st2), in0=qcol_f(cf, st2),
                        scalar1=1.0 / 127.0, scalar2=0.0, op0=Op.mult, op1=Op.add))



        ctx.start(False, sems)
        emit_dve(None)
        emit_act(None)
        emit_pe(None)
        emit_pool(None)
        emit_sync(None)

        ctx.start(True, sems)

        @block.vector
        def _(vector):
            emit_dve(vector)

        @block.scalar
        def _(scalar):
            emit_act(scalar)

        @block.tensor
        def _(tensor):
            emit_pe(tensor)

        @block.gpsimd
        def _(gpsimd):
            emit_pool(gpsimd)

        @block.sync
        def _(sync):
            emit_sync(sync)

    return nc


def _prep_inputs(A_log, W_dt, b_dt, W_xp, b_xp, W_out, b_out, w_gate, w_up,
                 w_down):
    import ml_dtypes
    f32 = np.float32
    d = {}
    d["wgT"] = np.ascontiguousarray(np.asarray(w_gate, f32).T).reshape(NDG, 128, NI)
    d["wuT"] = np.ascontiguousarray(np.asarray(w_up, f32).T).reshape(NDG, 128, NI)
    d["wdT"] = np.ascontiguousarray(np.asarray(w_down, f32).T).reshape(NKI, 128, D)
    d["woT"] = np.ascontiguousarray(np.asarray(W_out, f32).T).reshape(NDG, 128, D)
    d["wxpT"] = np.ascontiguousarray(
        np.asarray(W_xp, f32).T.reshape(NDG, 128, 32).transpose(1, 0, 2)
        .reshape(128, NDG * 32))
    d["alog"] = np.ascontiguousarray(
        np.asarray(A_log, f32).reshape(NDG, 128, NST).transpose(1, 0, 2)
        .reshape(128, NDG * NST))
    d["wdtr"] = np.ascontiguousarray(np.asarray(W_dt, f32)[:, 0][None, :])
    d["bdtr"] = np.ascontiguousarray(np.asarray(b_dt, f32)[None, :])
    d["bxpr"] = np.ascontiguousarray(np.asarray(b_xp, f32)[None, :])
    d["boutr"] = np.ascontiguousarray(np.asarray(b_out, f32)[None, :])
    oh = np.zeros((32, 32, 128), ml_dtypes.bfloat16)
    for n in range(32):
        oh[n, n, :] = 1
    d["oh32"] = np.ascontiguousarray(oh.transpose(1, 0, 2).reshape(32, 32 * 128))
    d["ones1"] = np.ones((1, TT), ml_dtypes.bfloat16)
    d["o128"] = np.ones((128, 128), np.float32)
    return d


_NC_CACHE = {}


def kernel(L0_attn_x, L0_attn_A_log, L0_attn_W_dt, L0_attn_b_dt, L0_attn_W_xp,
           L0_attn_b_xp, L0_attn_W_out, L0_attn_b_out, L0_ffn_w_gate,
           L0_ffn_w_up, L0_ffn_w_down):
    x = np.asarray(L0_attn_x, np.float32)
    B, T = x.shape[0], x.shape[1]
    if T not in _NC_CACHE:
        _NC_CACHE[T] = build_program(T)
    nc = _NC_CACHE[T]
    shared = _prep_inputs(L0_attn_A_log, L0_attn_W_dt, L0_attn_b_dt, L0_attn_W_xp,
                          L0_attn_b_xp, L0_attn_W_out, L0_attn_b_out,
                          L0_ffn_w_gate, L0_ffn_w_up, L0_ffn_w_down)
    in_maps = []
    for b in range(B):
        m = dict(shared)
        m["x"] = np.ascontiguousarray(x[b])
        in_maps.append(m)
    res = run_bass_kernel_spmd(nc, in_maps, core_ids=list(range(B)))
    return np.stack([np.asarray(r["out"]) for r in res.results], axis=0)
